# version 01
# baseline (speedup 1.0000x reference)
import math
import numpy as np
import ml_dtypes

import concourse.bass as bass
import concourse.mybir as mybir
from concourse.bass_utils import run_bass_kernel_spmd

F32 = mybir.dt.float32
BF16 = mybir.dt.bfloat16
I32 = mybir.dt.int32
AF = mybir.ActivationFunctionType
ALU = mybir.AluOpType

L = 2048
D = 1024
M3 = 3 * D
BPC = 4
TT = 8192
NB = ml_dtypes.bfloat16
TWO_PI = float(2.0 * np.pi)



def make_consts():
    c = {}
    t1 = np.arange(64)
    k1 = np.arange(128)
    t2 = np.arange(32)
    k2 = np.arange(32)
    Tq = np.arange(64)

    w128 = np.exp(-2j * np.pi * np.outer(t1, k1) / 128.0)
    tw = np.exp(-2j * np.pi * np.outer(t2, k1) / 4096.0)
    d1 = np.stack([w128[None, :, :] * tw[:, None, :]][0], 0)
    c["d1tw_re"] = d1.real.astype(NB)
    c["d1tw_im"] = d1.imag.astype(NB)
    c["d1tw_imn"] = (-d1.imag).astype(NB)

    W2c = np.exp(-2j * np.pi * np.outer(t2, k2) / 32.0)
    D2BD = np.zeros((128, 128), np.complex128)
    for cs in range(4):
        D2BD[cs * 32:(cs + 1) * 32, cs * 32:(cs + 1) * 32] = W2c
    c["d2_re"] = D2BD.real.astype(NB)
    c["d2_im"] = D2BD.imag.astype(NB)
    c["d2_imn"] = (-D2BD.imag).astype(NB)

    W2i = np.exp(+2j * np.pi * np.outer(k2, t2) / 32.0)
    D2i = np.zeros((128, 128), np.complex128)
    for cs in range(4):
        D2i[cs * 32:(cs + 1) * 32, cs * 32:(cs + 1) * 32] = W2i
    c["d2i_re"] = D2i.real.astype(NB)
    c["d2i_im"] = D2i.imag.astype(NB)
    c["d2i_imn"] = (-D2i.imag).astype(NB)

    TWI = np.exp(+2j * np.pi * np.outer(k1, t2) / 4096.0) / 4096.0
    TWI4 = np.repeat(TWI[:, None, :], 4, axis=1).reshape(128, 128)
    c["twi_re"] = TWI4.real.astype(np.float32)
    c["twi_im"] = TWI4.imag.astype(np.float32)

    D1i = np.exp(+2j * np.pi * np.outer(k1, Tq) / 128.0)
    c["d1i_re"] = D1i.real.astype(NB)
    c["d1i_im"] = D1i.imag.astype(NB)
    c["d1i_imn"] = (-D1i.imag).astype(NB)

    c["ident128"] = np.eye(128, dtype=NB)
    c["ident64"] = np.eye(64, dtype=NB)
    c["ones1x128"] = np.ones((1, 128), dtype=NB)
    c["ones1x128f"] = np.ones((1, 128), dtype=np.float32)
    sa = np.zeros((128, 64), np.float32); sa[:, 0] = 1.0
    sbv = np.zeros((128, 64), np.float32); sbv[:, 32] = 1.0
    c["stat_a"] = sa.astype(NB)
    c["stat_b"] = sbv.astype(NB)
    c["identf128"] = np.eye(128, dtype=np.float32)

    tau = np.arange(L)
    trev = ((L - 1 - tau) / (L - 1.0)).astype(np.float32)
    ang = 2.0 * np.pi * (L - 1 - tau) / L
    phase = 0.0001 * ang
    c["pe_rev"] = np.stack([trev, np.cos(phase).astype(np.float32),
                            (-np.sin(phase)).astype(np.float32)], 0).astype(np.float32)
    dmax, dmin = (math.log(0.01) / 0.3, math.log(0.01) / 1.5)
    c["deltas_neg"] = (-np.abs(np.linspace(dmax, dmin, D))).astype(np.float32)[:, None]
    c["trev_row"] = trev[None, :]
    return c



class Ctx:

    ENGS = ("PE", "ACT", "DVE", "GP", "SP", "BAR")
    DMAS = ("cLD", "cST", "cGL", "cGS")

    def __init__(self, nc, stack):
        self.nc = nc
        self.sem = {}
        self.cnt = {}
        for n in self.ENGS + self.DMAS:
            self.sem[n] = stack.enter_context(nc.semaphore("sem_" + n))
            self.cnt[n] = 0

    def mark(self, name, inst, n=1):
        step = 16 if name in self.DMAS else 1
        inst.then_inc(self.sem[name], n * step)
        self.cnt[name] += n
        return self.cnt[name]

    def dma(self, name, eng, out, in_):
        self.wait(eng, name)
        inst = eng.dma_start(out=out, in_=in_)
        return self.mark(name, inst)

    def wait(self, eng, name, count=None):
        step = 16 if name in self.DMAS else 1
        if count is None:
            count = self.cnt[name]
        if count > 0:
            eng.wait_ge(self.sem[name], count * step)

    def barrier(self, block):
        @block.sync
        def _(sy):
            for d in self.DMAS:
                self.wait(sy, d)
        @block.tensor
        def _(t):
            self.mark("BAR", t.nop())
        @block.scalar
        def _(s):
            self.mark("BAR", s.nop())
        @block.vector
        def _(v):
            self.mark("BAR", v.nop())
        @block.gpsimd
        def _(g):
            self.mark("BAR", g.nop())
        @block.sync
        def _(sy):
            self.mark("BAR", sy.nop())
        tgt = self.cnt["BAR"]
        @block.tensor
        def _(t):
            self.wait(t, "BAR", tgt)
        @block.scalar
        def _(s):
            self.wait(s, "BAR", tgt)
        @block.vector
        def _(v):
            self.wait(v, "BAR", tgt)
        @block.gpsimd
        def _(g):
            self.wait(g, "BAR", tgt)
        @block.sync
        def _(sy):
            self.wait(sy, "BAR", tgt)



def build_program(dbg=(), upto=99):
    import contextlib
    nc = bass.Bass("TRN2", target_bir_lowering=False, debug=False)

    dt = {}

    def din(name, shape, dtype=F32):
        dt[name] = nc.dram_tensor(name, list(shape), dtype, kind="ExternalInput").ap()
        return dt[name]

    def dsc(name, shape, dtype):
        kind = "ExternalOutput" if name in dbg else "Internal"
        dt[name] = nc.dram_tensor(name, list(shape), dtype, kind=kind).ap()
        return dt[name]

    din("xfm", (BPC, D, L))
    din("W_in", (D, M3)); din("bin_pp", (128, 24))
    din("shortw_pp", (128, 24, 3)); din("shortb_pp", (128, 24))
    din("fW1", (3, 64)); din("fb1", (64,)); din("ff1", (64,))
    din("fW2", (64, 64)); din("fb2", (64,)); din("ff2", (64,))
    din("fW3", (64, D)); din("fb3_pp", (128, 8)); din("fbias_pp", (128, 8))
    din("W_out", (D, D)); din("bout_pp", (128, 8))
    din("gate_wT", (D, 16)); din("expert_bias", (16,))
    din("w1T", (16, D, 512)); din("b1e_pp", (128, 16, 4))
    din("w3T", (16, D, 512)); din("b3e_pp", (128, 16, 4))
    din("w2T", (16, 512, D)); din("b2e", (16, D))
    din("sw1T", (D, D)); din("sb1_pp", (128, 8))
    din("sw3T", (D, D)); din("sb3_pp", (128, 8))
    din("sw2T", (D, D)); din("sb2_pp", (128, 8))
    din("dneg_pp", (128, 8))
    for n in ("d1tw_re", "d1tw_im", "d1tw_imn"):
        din(n, (32, 64, 128), BF16)
    for n in ("d2_re", "d2_im", "d2_imn", "d2i_re", "d2i_im", "d2i_imn",
              "ident128"):
        din(n, (128, 128), BF16)
    din("twi_re", (128, 128)); din("twi_im", (128, 128))
    for n in ("d1i_re", "d1i_im", "d1i_imn"):
        din(n, (128, 64), BF16)
    din("ident64", (64, 64), BF16)
    din("ones1x128", (1, 128), BF16)
    din("ones1x128f", (1, 128))
    din("stat_a", (128, 64), BF16); din("stat_b", (128, 64), BF16)
    din("identf128", (128, 128))
    din("pe_rev", (3, L)); din("trev_row", (1, L))

    dsc("gf_sc", (8, 32, 2, 128, 128), BF16)
    dsc("v_sc", (BPC, D, L), BF16)
    dsc("x0_sc", (BPC, D, L), BF16)
    dsc("yx0_sc", (BPC, D, L), BF16)
    dsc("xP_sc", (BPC, D, L), F32)
    dsc("xPb_sc", (BPC, D, L), BF16)
    dsc("gatew_sc", (16, TT), F32)
    dsc("ys_sc", (BPC, D, L), F32)
    dsc("hp_sc", (32, 16, 512, 256), BF16)
    out = nc.dram_tensor("out_fm", [BPC, D, L], F32, kind="ExternalOutput").ap()
    dt["out_fm"] = out

    with contextlib.ExitStack() as stack:
        ctx = Ctx(nc, stack)
        sb = lambda name, shape, dtype: stack.enter_context(
            nc.sbuf_tensor(name, list(shape), dtype))
        ps = lambda name, shape, dtype=F32: stack.enter_context(
            nc.psum_tensor(name, list(shape), dtype))
        block = stack.enter_context(nc.Block())

        P = {}
        for n in ("d2_re", "d2_im", "d2_imn", "d2i_re", "d2i_im", "d2i_imn",
                  "ident128"):
            P[n] = sb("c_" + n, (128, 128), BF16)
        P["twi_re"] = sb("c_twire", (128, 128), F32)
        P["twi_im"] = sb("c_twiim", (128, 128), F32)
        for n in ("d1i_re", "d1i_im", "d1i_imn"):
            P[n] = sb("c_" + n, (128, 64), BF16)
        P["ident64"] = sb("c_i64", (64, 64), BF16)
        P["ones1x128"] = sb("c_ones", (1, 128), BF16)
        P["ones1x128f"] = sb("c_onesf", (1, 128), F32)
        P["stat_a"] = sb("c_stata", (128, 64), BF16)
        P["stat_b"] = sb("c_statb", (128, 64), BF16)
        P["identf128"] = sb("c_if128", (128, 128), F32)

        @block.sync
        def _(sy):
            for n in ("d2_re", "d2_im", "d2_imn", "d2i_re", "d2i_im",
                      "d2i_imn", "ident128", "twi_re", "twi_im", "d1i_re",
                      "d1i_im", "d1i_imn", "ident64", "ones1x128", "ones1x128f",
                      "stat_a", "stat_b", "identf128"):
                ctx.dma("cLD", sy, out=P[n][:, :], in_=dt[n])

        ld_consts = ctx.cnt["cLD"]

        phases = [phase0_w(ld_consts), phase1, phase2, phase3, phase4, phase5]
        for i, ph in enumerate(phases):
            if i > upto:
                break
            ph(nc, ctx, block, dt, P, sb, ps)
            if i < upto and i < len(phases) - 1:
                ctx.barrier(block)

        @block.sync
        def _(sy):
            ctx.wait(sy, "cST")
            ctx.wait(sy, "cGS")

    return nc



def phase0_w(ld_consts):
    def f(nc, ctx, block, dt, P, sb, ps):
        return phase0(nc, ctx, block, dt, P, sb, ps, ld_consts)
    return f


def phase0(nc, ctx, block, dt, P, sb, ps, ld_consts):
    import contextlib
    with contextlib.ExitStack() as st:
        sbl = lambda n, s, d: st.enter_context(nc.sbuf_tensor(n, list(s), d))
        psl = lambda n, s, d=F32: st.enter_context(nc.psum_tensor(n, list(s), d))

        pe_sb = sbl("pe_sb", (3, L), F32)
        fW1 = sbl("fW1s", (3, 64), F32)
        fW2 = sbl("fW2s", (64, 64), F32)
        fW3 = sbl("fW3s", (64, D), F32)
        vecs = sbl("fvecs", (64, 6), F32)
        b3pp = sbl("b3pp", (128, 8), F32)
        dneg = sbl("dneg", (128, 8), F32)
        h_sb = sbl("h_sb", (64, L), F32)
        zi = sbl("zi", (64, L), I32)
        zf = sbl("zf", (64, L), F32)
        trow = sbl("trow", (1, L), F32)
        E_sb = sbl("E_sb", (128, L), F32)
        g_sb = sbl("g_sb", (128, L), BF16)
        kpre = sbl("kpre", (128, L), F32)
        for _n in ("d1tw_re", "d1tw_im", "d1tw_imn"):
            P["_" + _n] = sbl("p0" + _n, (64, 32 * 128), BF16)
        U_re = sbl("p0_ure", (64, 32 * 128), BF16)
        S2_re = sbl("p0_s2re", (128, 32 * 128), BF16)
        S2_im = sbl("p0_s2im", (128, 32 * 128), BF16)
        M2_re = sbl("p0_m2re", (128, 32 * 128), BF16)
        M2_im = sbl("p0_m2im", (128, 32 * 128), BF16)
        G_re = sbl("p0_gre", (128, 32 * 128), BF16)
        G_im = sbl("p0_gim", (128, 32 * 128), BF16)

        pmm = psl("p0_mm", (128, 512))
        pT = psl("p0_T", (128, 512))
        pS = psl("p0_S", (128, 512))
        pS2 = psl("p0_S2", (128, 512))
        pX = psl("p0_X", (128, 512))
        pX2 = psl("p0_X2", (128, 512))

        @block.sync
        def _(sy):
            ctx.dma("cLD", sy, out=pe_sb[:, :], in_=dt["pe_rev"])
            ctx.dma("cLD", sy, out=fW1[:, :], in_=dt["fW1"])
            ctx.dma("cLD", sy, out=fW2[:, :], in_=dt["fW2"])
            ctx.dma("cLD", sy, out=fW3[:, :], in_=dt["fW3"])
            ctx.dma("cLD", sy, out=vecs[:, 0:1], in_=dt["fb1"][:, None])
            ctx.dma("cLD", sy, out=vecs[:, 1:2], in_=dt["ff1"][:, None])
            ctx.dma("cLD", sy, out=vecs[:, 2:3], in_=dt["fb2"][:, None])
            ctx.dma("cLD", sy, out=vecs[:, 3:4], in_=dt["ff2"][:, None])
            ctx.dma("cLD", sy, out=b3pp[:, :], in_=dt["fb3_pp"])
            ctx.dma("cLD", sy, out=dneg[:, :], in_=dt["dneg_pp"])
            ctx.dma("cLD", sy, out=trow[:, :], in_=dt["trev_row"])
            for _n in ("d1tw_re", "d1tw_im", "d1tw_imn"):
                ctx.dma("cLD", sy, out=P["_" + _n].ap().rearrange(
                    "p (t2 k) -> p t2 k", t2=32),
                    in_=dt[_n].rearrange("t2 p k -> p t2 k"))
        ld0 = ctx.cnt["cLD"]

        @block.vector
        def _(v):
            ctx.wait(v, "cLD", ld0)
            v.tensor_tensor(vecs[:, 4:5], vecs[:, 0:1], vecs[:, 1:2], ALU.mult)
            ctx.mark("DVE", v.tensor_tensor(vecs[:, 5:6], vecs[:, 2:3],
                                            vecs[:, 3:4], ALU.mult))
        prod_done = ctx.cnt["DVE"]

        def filt_layer(Wsb, K, scale_col, bias_col, src, dst):
            for tt in range(4):
                sl = slice(tt * 512, (tt + 1) * 512)
                @block.tensor
                def _(t, sl=sl, tt=tt):
                    ctx.wait(t, "DVE", ctx.cnt["DVE"])
                    ctx.wait(t, "ACT", ctx.cnt["ACT"])
                    ctx.mark("PE", t.matmul(pmm[0:64, 0:512], Wsb[0:K, :],
                                            src[0:K, sl], start=True, stop=True))
                @block.vector
                def _(v, sl=sl):
                    ctx.wait(v, "PE", ctx.cnt["PE"])
                    ctx.wait(v, "DVE", prod_done)
                    ctx.mark("DVE", v.tensor_scalar(
                        dst[:, sl], pmm[0:64, 0:512],
                        vecs[:, scale_col:scale_col + 1],
                        vecs[:, bias_col:bias_col + 1], ALU.mult, ALU.add))
            @block.vector
            def _(v):
                ctx.wait(v, "DVE")
                ctx.mark("DVE", v.tensor_scalar_mul(zf[:, :], dst[:, :],
                                                    float(1.0 / TWO_PI)))
                ctx.wait(v, "DVE")
                ctx.mark("DVE", v.tensor_copy(zi[:, :], zf[:, :]))
                ctx.wait(v, "DVE")
                ctx.mark("DVE", v.tensor_copy(zf[:, :], zi[:, :]))
                ctx.wait(v, "DVE")
                ctx.mark("DVE", v.scalar_tensor_tensor(
                    dst[:, :], zf[:, :], -TWO_PI, dst[:, :], ALU.mult, ALU.add))
                ctx.wait(v, "DVE")
                ctx.mark("DVE", v.tensor_single_scalar(zf[:, :], dst[:, :],
                                                       float(np.pi), ALU.is_gt))
                ctx.wait(v, "DVE")
                ctx.mark("DVE", v.scalar_tensor_tensor(
                    dst[:, :], zf[:, :], -TWO_PI, dst[:, :], ALU.mult, ALU.add))
                ctx.wait(v, "DVE")
                ctx.mark("DVE", v.tensor_single_scalar(zf[:, :], dst[:, :],
                                                       float(-np.pi), ALU.is_lt))
                ctx.wait(v, "DVE")
                ctx.mark("DVE", v.scalar_tensor_tensor(
                    dst[:, :], zf[:, :], TWO_PI, dst[:, :], ALU.mult, ALU.add))
            @block.scalar
            def _(s):
                ctx.wait(s, "DVE", ctx.cnt["DVE"])
                ctx.mark("ACT", s.activation(dst[:, :], dst[:, :], AF.Sin))

        h1 = sbl("h1_sb", (64, L), F32)
        filt_layer(fW1, 3, 1, 4, pe_sb, h1)
        filt_layer(fW2, 64, 3, 5, h1, h_sb)

        for dc in range(8):
            csl = slice(dc * 128, (dc + 1) * 128)
            for tt in range(4):
                sl = slice(tt * 512, (tt + 1) * 512)
                @block.tensor
                def _(t, sl=sl, tt=tt, dc=dc):
                    ctx.wait(t, "ACT", ctx.cnt["ACT"])
                    ctx.mark("PE", t.matmul(pT[:, 0:512], P["ones1x128f"][:, :],
                                            trow[:, sl], start=True, stop=True))
                @block.scalar
                def _(s, sl=sl, dc=dc):
                    ctx.wait(s, "PE", ctx.cnt["PE"])
                    ctx.mark("ACT", s.activation(E_sb[:, sl], pT[:, 0:512],
                                                 AF.Exp, scale=dneg[:, dc:dc + 1]))
            for tt in range(4):
                sl = slice(tt * 512, (tt + 1) * 512)
                @block.tensor
                def _(t, sl=sl, dc=dc, tt=tt):
                    ctx.wait(t, "DVE", ctx.cnt["DVE"])
                    ctx.mark("PE", t.matmul(pS[:, 0:512], fW3[:, dc * 128:(dc + 1) * 128],
                                            h_sb[:, sl], start=True, stop=True))
                @block.vector
                def _(v, sl=sl, dc=dc):
                    ctx.wait(v, "PE", ctx.cnt["PE"])
                    ctx.wait(v, "ACT", ctx.cnt["ACT"])
                    ctx.mark("DVE", v.tensor_scalar(kpre[:, sl], pS[:, 0:512],
                                                    b3pp[:, dc:dc + 1], None, ALU.add))
                    ctx.wait(v, "DVE")
                    ctx.mark("DVE", v.tensor_tensor(g_sb[:, sl], kpre[:, sl],
                                                    E_sb[:, sl], ALU.mult))
            fwd_fft(nc, ctx, block, P, g_sb, None, U_re, None, S2_re, S2_im,
                    M2_re, M2_im, G_re, G_im, pT, pS, pS2, pX, pX2)
            @block.sync
            def _(sy, dc=dc):
                ctx.wait(sy, "DVE", ctx.cnt["DVE"])
                ctx.wait(sy, "ACT", ctx.cnt["ACT"])
                ctx.dma("cST", sy, out=dt["gf_sc"][dc, :, 0, :, :].rearrange("j p k -> p j k"),
                    in_=G_re.ap().rearrange("p (j k) -> p j k", j=32))
                ctx.dma("cST", sy, out=dt["gf_sc"][dc, :, 1, :, :].rearrange("j p k -> p j k"),
                    in_=G_im.ap().rearrange("p (j k) -> p j k", j=32))
            @block.vector
            def _(v):
                ctx.wait(v, "cST", ctx.cnt["cST"])
            @block.scalar
            def _(s):
                ctx.wait(s, "cST", ctx.cnt["cST"])


def fwd_fft(nc, ctx, block, P, vre_sb, vim_sb, U_re, U_im, S2_re, S2_im,
            M2_re, M2_im, X_re, X_im, pT, pS, pS2, pX, pX2):
    cplx = vim_sb is not None
    planes = ((vre_sb, U_re), (vim_sb, U_im)) if cplx else ((vre_sb, U_re),)

    per_grp = 4 // len(planes)
    ngrp = 32 // per_grp
    for grp in range(ngrp):
        @block.tensor
        def _(t, grp=grp):
            ctx.wait(t, "DVE", ctx.cnt["DVE"])
            ctx.wait(t, "ACT", ctx.cnt["ACT"])
            for q in range(per_grp):
                t2 = grp * per_grp + q
                for pi, (vsb, _u) in enumerate(planes):
                    src = vsb.ap().rearrange("p (t1 t2) -> p t2 t1", t2=32)[:, t2, 0:64]
                    off = (q * len(planes) + pi) * 128
                    ctx.mark("PE", t.matmul(pT[0:64, off:off + 128], src,
                                            P["ident128"][:, :], start=True, stop=True))
        @block.vector
        def _(v, grp=grp):
            ctx.wait(v, "PE", ctx.cnt["PE"])
            for q in range(per_grp):
                t2 = grp * per_grp + q
                for pi, (_v, usb) in enumerate(planes):
                    off = (q * len(planes) + pi) * 128
                    ctx.mark("DVE", v.tensor_copy(
                        usb[0:64, t2 * 128:(t2 + 1) * 128], pT[0:64, off:off + 128]))

    for grp in range(8):
        @block.tensor
        def _(t, grp=grp):
            ctx.wait(t, "DVE", ctx.cnt["DVE"])
            ctx.wait(t, "ACT", ctx.cnt["ACT"])
            for q in range(4):
                t2 = grp * 4 + q
                d_re = P["_d1tw_re"][:, t2 * 128:(t2 + 1) * 128]
                d_im = P["_d1tw_im"][:, t2 * 128:(t2 + 1) * 128]
                d_imn = P["_d1tw_imn"][:, t2 * 128:(t2 + 1) * 128]
                ure = U_re[0:64, t2 * 128:(t2 + 1) * 128]
                re_ps = pS[:, q * 128:(q + 1) * 128]
                im_ps = pS2[:, q * 128:(q + 1) * 128]
                if cplx:
                    uim = U_im[0:64, t2 * 128:(t2 + 1) * 128]
                    t.matmul(re_ps, d_re, ure, start=True, stop=False)
                    t.matmul(re_ps, d_imn, uim, start=False, stop=True)
                    t.matmul(im_ps, d_im, ure, start=True, stop=False)
                    ctx.mark("PE", t.matmul(im_ps, d_re, uim, start=False, stop=True))
                else:
                    t.matmul(re_ps, d_re, ure, start=True, stop=True)
                    ctx.mark("PE", t.matmul(im_ps, d_im, ure, start=True, stop=True))
        pe_c = ctx.cnt["PE"]
        @block.scalar
        def _(s, grp=grp, pe_c=pe_c):
            ctx.wait(s, "PE", pe_c)
            for q in range(4):
                t2 = grp * 4 + q
                dst = S2_re.ap().rearrange("p (c t2) -> p c t2", t2=32)[:, :, t2]
                ctx.mark("ACT", s.activation(dst, pS[:, q * 128:(q + 1) * 128], AF.Copy))
        @block.vector
        def _(v, grp=grp, pe_c=pe_c):
            ctx.wait(v, "PE", pe_c)
            for q in range(4):
                t2 = grp * 4 + q
                dst = S2_im.ap().rearrange("p (c t2) -> p c t2", t2=32)[:, :, t2]
                ctx.mark("DVE", v.tensor_copy(dst, pS2[:, q * 128:(q + 1) * 128]))

    for grp in range(8):
        @block.tensor
        def _(t, grp=grp):
            ctx.wait(t, "ACT", ctx.cnt["ACT"])
            ctx.wait(t, "DVE", ctx.cnt["DVE"])
            for q in range(2):
                j2 = grp * 4 + q * 2
                for jj in range(2):
                    sl = slice((j2 + jj) * 128, (j2 + jj + 1) * 128)
                    t.matmul(pT[:, (q * 2 + jj) * 128:(q * 2 + jj + 1) * 128],
                             S2_re.ap()[:, sl], P["ident128"][:, :],
                             start=True, stop=True)
                    ctx.mark("PE", t.matmul(
                        pS[:, (q * 2 + jj) * 128:(q * 2 + jj + 1) * 128],
                        S2_im.ap()[:, sl], P["ident128"][:, :],
                        start=True, stop=True))
        pe_c = ctx.cnt["PE"]
        @block.scalar
        def _(s, grp=grp, pe_c=pe_c):
            ctx.wait(s, "PE", pe_c)
            ctx.mark("ACT", s.activation(M2_re[:, grp * 512:(grp + 1) * 512],
                                         pT[:, 0:512], AF.Copy))
        @block.vector
        def _(v, grp=grp, pe_c=pe_c):
            ctx.wait(v, "PE", pe_c)
            ctx.mark("DVE", v.tensor_copy(M2_im[:, grp * 512:(grp + 1) * 512],
                                          pS[:, 0:512]))

    for grp in range(8):
        @block.tensor
        def _(t, grp=grp):
            ctx.wait(t, "ACT", ctx.cnt["ACT"])
            ctx.wait(t, "DVE", ctx.cnt["DVE"])
            sl = slice(grp * 512, (grp + 1) * 512)
            mre = M2_re.ap()[:, sl]
            mim = M2_im.ap()[:, sl]
            t.matmul(pX[:, 0:512], P["d2_re"][:, :], mre, start=True, stop=False)
            t.matmul(pX[:, 0:512], P["d2_imn"][:, :], mim, start=False, stop=True)
            t.matmul(pX2[:, 0:512], P["d2_im"][:, :], mre, start=True, stop=False)
            ctx.mark("PE", t.matmul(pX2[:, 0:512], P["d2_re"][:, :], mim,
                                    start=False, stop=True))
        pe_c = ctx.cnt["PE"]
        @block.scalar
        def _(s, grp=grp, pe_c=pe_c):
            ctx.wait(s, "PE", pe_c)
            ctx.mark("ACT", s.activation(X_re[:, grp * 512:(grp + 1) * 512],
                                         pX[:, 0:512], AF.Copy))
        @block.vector
        def _(v, grp=grp, pe_c=pe_c):
            ctx.wait(v, "PE", pe_c)
            ctx.mark("DVE", v.tensor_copy(X_im[:, grp * 512:(grp + 1) * 512],
                                          pX2[:, 0:512]))


def phase1(nc, ctx, block, dt, P, sb, ps):
    import contextlib
    with contextlib.ExitStack() as st:
        sbl = lambda n, s_, d: st.enter_context(nc.sbuf_tensor(n, list(s_), d))
        psl = lambda n, s_, d=F32: st.enter_context(nc.psum_tensor(n, list(s_), d))

        winb = sbl("winb", (128, 8 * M3), BF16)
        binp = sbl("binp", (128, 24), F32)
        swp = sbl("swp", (128, 24 * 3), F32)
        sbp = sbl("sbp", (128, 24), F32)
        xb = sbl("xb", (128, 8 * L), BF16)
        u_sb = sbl("u_sb", (128, 2 * 2052), F32)
        acc = sbl("acc_sb", (128, 2 * L), F32)
        x1h = sbl("x1h", (128, L), F32)
        stg = sbl("stg", (128, 2 * L), BF16)
        pu = [psl("p1_u0", (128, 512)), psl("p1_u1", (128, 512))]

        @block.sync
        def _(sy):
            ctx.dma("cLD", sy, out=binp[:, :], in_=dt["bin_pp"])
            ctx.dma("cLD", sy, out=swp.ap().rearrange("p (c s) -> p c s", s=3), in_=dt["shortw_pp"])
            ctx.dma("cLD", sy, out=sbp[:, :], in_=dt["shortb_pp"])
        @block.gpsimd
        def _(g):
            ctx.dma("cGL", g, out=winb.ap().rearrange("p (kc m) -> p kc m", kc=8),
                in_=dt["W_in"].rearrange("(kc p) m -> p kc m", p=128))
        ld_w = (ctx.cnt["cLD"], ctx.cnt["cGL"])

        for b in range(BPC):
            @block.gpsimd
            def _(g, b=b):
                ctx.wait(g, "PE", ctx.cnt["PE"])
                ctx.dma("cGL", g, out=xb.ap().rearrange("p (kc t) -> p kc t", kc=8),
                    in_=dt["xfm"][b].rearrange("(kc p) t -> p kc t", p=128))
            xld = ctx.cnt["cGL"]

            order = list(range(8)) + [m for c in range(8) for m in (8 + c, 16 + c)]
            for oi, mc in enumerate(order):
                ub = (oi % 2)
                uof = ub * 2052
                aof = ub * L
                @block.vector
                def _(v, mc=mc, ub=ub, uof=uof, b=b, oi=oi):
                    ctx.wait(v, "DVE", ctx.cnt["DVE"])
                    ctx.mark("DVE", v.memset(u_sb[:, uof:uof + 2], 0.0))
                for tt in range(4):
                    pq = pu[tt % 2]
                    @block.tensor
                    def _(t, mc=mc, tt=tt, pq=pq, xld=xld, oi=oi, b=b):
                        if oi == 0 and tt == 0:
                            ctx.wait(t, "cGL", ld_w[1])
                            ctx.wait(t, "cLD", ld_w[0])
                        ctx.wait(t, "cGL", xld)
                        ctx.wait(t, "ACT", ctx.cnt["ACT"])
                        for kc in range(8):
                            inst = t.matmul(
                                pq[:, :], winb[:, (kc * 24 + mc) * 128:(kc * 24 + mc + 1) * 128],
                                xb[:, kc * L + tt * 512:kc * L + (tt + 1) * 512],
                                start=(kc == 0), stop=(kc == 7))
                        ctx.mark("PE", inst)
                    @block.scalar
                    def _(s_, mc=mc, tt=tt, pq=pq, uof=uof):
                        ctx.wait(s_, "PE", ctx.cnt["PE"])
                        ctx.wait(s_, "DVE", ctx.cnt["DVE"])
                        ctx.wait(s_, "cLD", ld_w[0])
                        ctx.mark("ACT", s_.activation(
                            u_sb[:, uof + 2 + tt * 512:uof + 2 + (tt + 1) * 512],
                            pq[:, :], AF.Copy))
                @block.vector
                def _(v, mc=mc, uof=uof, aof=aof):
                    ctx.wait(v, "ACT", ctx.cnt["ACT"])
                    ctx.wait(v, "DVE", ctx.cnt["DVE"])
                    ctx.mark("DVE", v.tensor_scalar(
                        u_sb[:, uof + 2:uof + 2050], u_sb[:, uof + 2:uof + 2050],
                        binp[:, mc:mc + 1], None, ALU.add))
                    ctx.wait(v, "DVE")
                    ctx.mark("DVE", v.tensor_scalar(
                        acc[:, aof:aof + L], u_sb[:, uof + 2:uof + 2050],
                        swp[:, mc * 3 + 2:mc * 3 + 3], sbp[:, mc:mc + 1],
                        ALU.mult, ALU.add))
                    ctx.wait(v, "DVE")
                    ctx.mark("DVE", v.scalar_tensor_tensor(
                        acc[:, aof:aof + L], u_sb[:, uof + 1:uof + 2049],
                        swp[:, mc * 3 + 1:mc * 3 + 2], acc[:, aof:aof + L],
                        ALU.mult, ALU.add))
                    ctx.wait(v, "DVE")
                    ctx.mark("DVE", v.scalar_tensor_tensor(
                        acc[:, aof:aof + L], u_sb[:, uof:uof + 2048],
                        swp[:, mc * 3 + 0:mc * 3 + 1], acc[:, aof:aof + L],
                        ALU.mult, ALU.add))
                dve_acc = ctx.cnt["DVE"]
                sof = (oi % 2) * L
                if mc < 8:
                    @block.scalar
                    def _(s_, aof=aof, sof=sof):
                        ctx.wait(s_, "DVE", dve_acc)
                        ctx.wait(s_, "cST", ctx.cnt["cST"])
                        ctx.mark("ACT", s_.activation(stg[:, sof:sof + L],
                                                      acc[:, aof:aof + L], AF.Copy))
                    @block.sync
                    def _(sy, b=b, mc=mc, sof=sof):
                        ctx.wait(sy, "ACT", ctx.cnt["ACT"])
                        ctx.dma("cST", sy, out=dt["x0_sc"][b, mc * 128:(mc + 1) * 128, :],
                            in_=stg[:, sof:sof + L])
                elif mc < 16:
                    @block.vector
                    def _(v, aof=aof):
                        ctx.wait(v, "DVE")
                        ctx.mark("DVE", v.tensor_copy(x1h[:, :], acc[:, aof:aof + L]))
                else:
                    @block.vector
                    def _(v, aof=aof, sof=sof):
                        ctx.wait(v, "DVE")
                        ctx.wait(v, "cST", ctx.cnt["cST"])
                        ctx.mark("DVE", v.tensor_tensor(stg[:, sof:sof + L],
                                                        acc[:, aof:aof + L],
                                                        x1h[:, :], ALU.mult))
                    @block.sync
                    def _(sy, b=b, mc=mc, sof=sof):
                        ctx.wait(sy, "DVE", ctx.cnt["DVE"])
                        ctx.dma("cST", sy, out=dt["v_sc"][b, (mc - 16) * 128:(mc - 15) * 128, :],
                            in_=stg[:, sof:sof + L])


def phase2(nc, ctx, block, dt, P, sb, ps):
    import contextlib
    with contextlib.ExitStack() as st:
        sbl = lambda n, s_, d: st.enter_context(nc.sbuf_tensor(n, list(s_), d))
        psl = lambda n, s_, d=F32: st.enter_context(nc.psum_tensor(n, list(s_), d))

        for _n in ("d1tw_re", "d1tw_im", "d1tw_imn"):
            P["_" + _n] = sbl("p2" + _n, (64, 32 * 128), BF16)
        vre = sbl("p2_vre", (128, L), BF16)
        vim = sbl("p2_vim", (128, L), BF16)
        x0re = sbl("p2_x0re", (128, L), BF16)
        x0im = sbl("p2_x0im", (128, L), BF16)
        gre = sbl("p2_gre", (128, 32 * 128), BF16)
        gim = sbl("p2_gim", (128, 32 * 128), BF16)
        U_re = sbl("p2_ure", (64, 32 * 128), BF16)
        U_im = sbl("p2_uim", (64, 32 * 128), BF16)
        S2_re = sbl("p2_s2re", (128, 32 * 128), BF16)
        S2_im = sbl("p2_s2im", (128, 32 * 128), BF16)
        M2_re = sbl("p2_m2re", (128, 32 * 128), BF16)
        M2_im = sbl("p2_m2im", (128, 32 * 128), BF16)
        X_re = sbl("p2_xre", (128, 32 * 128), BF16)
        X_im = sbl("p2_xim", (128, 32 * 128), BF16)
        Y_re = sbl("p2_yre", (128, 32 * 128), BF16)
        Y_im = sbl("p2_yim", (128, 32 * 128), BF16)
        Bt_re = sbl("p2_btre", (128, 256), F32)
        Bt_im = sbl("p2_btim", (128, 256), F32)
        Bm_re = sbl("p2_bmre", (128, 256), F32)
        Bm_im = sbl("p2_bmim", (128, 256), F32)
        Bp_re = sbl("p2_bpre", (128, 32 * 128), BF16)
        Bp_im = sbl("p2_bpim", (128, 32 * 128), BF16)
        Yb_re = sbl("p2_ybre", (64, 32 * 128), BF16)
        Yb_im = sbl("p2_ybim", (64, 32 * 128), BF16)
        yg = sbl("p2_yg", (128, 512), F32)
        stg_re = sbl("p2_stgre", (128, L), BF16)
        stg_im = sbl("p2_stgim", (128, L), BF16)
        fbias = sbl("p2_fbias", (128, 8), F32)
        twi2_re = sbl("p2_twi2re", (128, 256), F32)
        twi2_im = sbl("p2_twi2im", (128, 256), F32)

        pT = psl("p2_T", (128, 512))
        pS = psl("p2_S", (128, 512))
        pS2 = psl("p2_S2", (128, 512))
        pX = psl("p2_X", (128, 512))
        pX2 = psl("p2_X2", (128, 512))

        @block.sync
        def _(sy):
            ctx.dma("cLD", sy, out=fbias[:, :], in_=dt["fbias_pp"])
            for _n in ("d1tw_re", "d1tw_im", "d1tw_imn"):
                ctx.dma("cLD", sy, out=P["_" + _n].ap().rearrange(
                    "p (t2 k) -> p t2 k", t2=32),
                    in_=dt[_n].rearrange("t2 p k -> p t2 k"))
        @block.vector
        def _(v):
            ctx.wait(v, "cLD", ctx.cnt["cLD"])
            for h in range(2):
                v.tensor_copy(twi2_re[:, h * 128:(h + 1) * 128], P["twi_re"][:, :])
                ctx.mark("DVE", v.tensor_copy(twi2_im[:, h * 128:(h + 1) * 128],
                                              P["twi_im"][:, :]))

        for pair in range(2):
            b0, b1 = 2 * pair, 2 * pair + 1
            for dc in range(8):
                csl = slice(dc * 128, (dc + 1) * 128)
                @block.sync
                def _(sy, b0=b0, b1=b1, dc=dc, csl=csl):
                    ctx.wait(sy, "PE", ctx.cnt["PE"])
                    ctx.wait(sy, "DVE", ctx.cnt["DVE"])
                    ctx.dma("cLD", sy, out=vre[:, :], in_=dt["v_sc"][b0, csl, :])
                    ctx.dma("cLD", sy, out=vim[:, :], in_=dt["v_sc"][b1, csl, :])
                    ctx.dma("cLD", sy, out=x0re[:, :], in_=dt["x0_sc"][b0, csl, :])
                    ctx.dma("cLD", sy, out=x0im[:, :], in_=dt["x0_sc"][b1, csl, :])
                    ctx.dma("cLD", sy, out=gre.ap().rearrange("p (j k) -> p j k", j=32),
                            in_=dt["gf_sc"][dc, :, 0, :, :].rearrange("j p k -> p j k"))
                    ctx.dma("cLD", sy, out=gim.ap().rearrange("p (j k) -> p j k", j=32),
                            in_=dt["gf_sc"][dc, :, 1, :, :].rearrange("j p k -> p j k"))
                ld_u = ctx.cnt["cLD"]

                @block.tensor
                def _(t, ld_u=ld_u):
                    ctx.wait(t, "cLD", ld_u)

                fwd_fft(nc, ctx, block, P, vre, vim, U_re, U_im, S2_re, S2_im,
                        M2_re, M2_im, X_re, X_im, pT, pS, pS2, pX, pX2)

                for grp in range(8):
                    sl = slice(grp * 512, (grp + 1) * 512)
                    @block.vector
                    def _(v, sl=sl, ld_u=ld_u):
                        ctx.wait(v, "cLD", ld_u)
                        ctx.wait(v, "ACT", ctx.cnt["ACT"])
                        ctx.wait(v, "DVE", ctx.cnt["DVE"])
                        ctx.mark("DVE", v.tensor_tensor(Y_re[:, sl], X_re[:, sl],
                                                        gre[:, sl], ALU.mult))
                        ctx.mark("DVE", v.tensor_tensor(Y_im[:, sl], X_im[:, sl],
                                                        gim[:, sl], ALU.mult))
                        ctx.wait(v, "DVE")
                        ctx.mark("DVE", v.tensor_tensor(Y_re[:, sl], Y_re[:, sl],
                                                        Y_im[:, sl], ALU.subtract))
                        ctx.wait(v, "DVE")
                        ctx.mark("DVE", v.tensor_tensor(Y_im[:, sl], X_re[:, sl],
                                                        gim[:, sl], ALU.mult))
                        ctx.wait(v, "DVE")
                        ctx.mark("DVE", v.tensor_tensor(yg[:, 0:512], X_im[:, sl],
                                                        gre[:, sl], ALU.mult))
                        ctx.wait(v, "DVE")
                        ctx.mark("DVE", v.tensor_tensor(Y_im[:, sl], Y_im[:, sl],
                                                        yg[:, 0:512], ALU.add))

                for pr in range(16):
                    @block.tensor
                    def _(t, pr=pr):
                        ctx.wait(t, "DVE", ctx.cnt["DVE"])
                        ctx.wait(t, "ACT", ctx.cnt["ACT"])
                        for jj in range(2):
                            j2 = pr * 2 + jj
                            ysl = slice(j2 * 128, (j2 + 1) * 128)
                            po = slice(jj * 128, (jj + 1) * 128)
                            t.matmul(pS[:, po], Y_re[:, ysl], P["d2i_re"][:, :],
                                     start=True, stop=False)
                            t.matmul(pS[:, po], Y_im[:, ysl], P["d2i_imn"][:, :],
                                     start=False, stop=True)
                            t.matmul(pS2[:, po], Y_im[:, ysl], P["d2i_re"][:, :],
                                     start=True, stop=False)
                            ctx.mark("PE", t.matmul(pS2[:, po], Y_re[:, ysl],
                                                    P["d2i_im"][:, :],
                                                    start=False, stop=True))
                    pe_c = ctx.cnt["PE"]
                    @block.scalar
                    def _(s_, pr=pr, pe_c=pe_c):
                        ctx.wait(s_, "PE", pe_c)
                        ctx.wait(s_, "DVE", ctx.cnt["DVE"])
                        ctx.mark("ACT", s_.activation(Bt_re[:, :], pS[:, 0:256], AF.Copy))
                        ctx.mark("ACT", s_.activation(Bt_im[:, :], pS2[:, 0:256], AF.Copy))
                    act_c = ctx.cnt["ACT"]
                    @block.vector
                    def _(v, pr=pr, act_c=act_c):
                        ctx.wait(v, "ACT", act_c)
                        psl_ = slice(pr * 256, (pr + 1) * 256)
                        ctx.mark("DVE", v.tensor_tensor(Bm_re[:, :], Bt_re[:, :],
                                                        twi2_re[:, :], ALU.mult))
                        ctx.mark("DVE", v.tensor_tensor(Bm_im[:, :], Bt_im[:, :],
                                                        twi2_im[:, :], ALU.mult))
                        ctx.wait(v, "DVE")
                        ctx.mark("DVE", v.tensor_tensor(Bp_re[:, psl_], Bm_re[:, :],
                                                        Bm_im[:, :], ALU.subtract))
                        ctx.wait(v, "DVE")
                        ctx.mark("DVE", v.tensor_tensor(Bm_re[:, :], Bt_re[:, :],
                                                        twi2_im[:, :], ALU.mult))
                        ctx.mark("DVE", v.tensor_tensor(Bm_im[:, :], Bt_im[:, :],
                                                        twi2_re[:, :], ALU.mult))
                        ctx.wait(v, "DVE")
                        ctx.mark("DVE", v.tensor_tensor(Bp_im[:, psl_], Bm_re[:, :],
                                                        Bm_im[:, :], ALU.add))

                for grp in range(8):
                    @block.tensor
                    def _(t, grp=grp):
                        ctx.wait(t, "DVE", ctx.cnt["DVE"])
                        ctx.wait(t, "ACT", ctx.cnt["ACT"])
                        sl = slice(grp * 512, (grp + 1) * 512)
                        bre = Bp_re.ap()[:, sl]
                        bim = Bp_im.ap()[:, sl]
                        t.matmul(pX[0:64, 0:512], P["d1i_re"][:, :], bre,
                                 start=True, stop=False)
                        t.matmul(pX[0:64, 0:512], P["d1i_imn"][:, :], bim,
                                 start=False, stop=True)
                        t.matmul(pX2[0:64, 0:512], P["d1i_im"][:, :], bre,
                                 start=True, stop=False)
                        ctx.mark("PE", t.matmul(pX2[0:64, 0:512], P["d1i_re"][:, :],
                                                bim, start=False, stop=True))
                    pe_c = ctx.cnt["PE"]
                    @block.scalar
                    def _(s_, grp=grp, pe_c=pe_c):
                        ctx.wait(s_, "PE", pe_c)
                        ctx.mark("ACT", s_.activation(Yb_re[0:64, grp * 512:(grp + 1) * 512],
                                                      pX[0:64, 0:512], AF.Copy))
                    @block.vector
                    def _(v, grp=grp, pe_c=pe_c):
                        ctx.wait(v, "PE", pe_c)
                        ctx.mark("DVE", v.tensor_copy(Yb_im[0:64, grp * 512:(grp + 1) * 512],
                                                      pX2[0:64, 0:512]))

                for mg in range(4):
                    @block.tensor
                    def _(t, mg=mg):
                        ctx.wait(t, "ACT", ctx.cnt["ACT"])
                        ctx.wait(t, "DVE", ctx.cnt["DVE"])
                        for ml in range(8):
                            m2 = mg * 8 + ml
                            sel_re = Yb_re.ap().rearrange(
                                "p (j c m) -> p m (j c)", c=4, m=32)[:, m2, :]
                            sel_im = Yb_im.ap().rearrange(
                                "p (j c m) -> p m (j c)", c=4, m=32)[:, m2, :]
                            t.matmul(pT[:, ml * 64:(ml + 1) * 64], sel_re,
                                     P["ident64"][:, :], start=True, stop=True)
                            ctx.mark("PE", t.matmul(pS[:, ml * 64:(ml + 1) * 64],
                                                    sel_im, P["ident64"][:, :],
                                                    start=True, stop=True))
                    pe_c = ctx.cnt["PE"]
                    for pi, (pp, x0b, stgb) in enumerate(
                            ((pT, x0re, stg_re), (pS, x0im, stg_im))):
                        @block.vector
                        def _(v, mg=mg, pp=pp, x0b=x0b, stgb=stgb, pe_c=pe_c, dc=dc):
                            ctx.wait(v, "PE", pe_c)
                            ctx.wait(v, "cST", ctx.cnt["cST"])
                            ctx.wait(v, "DVE")
                            ctx.mark("DVE", v.tensor_scalar(
                                yg[:, 0:512], pp[:, 0:512], fbias[:, dc:dc + 1],
                                None, ALU.add))
                            ctx.wait(v, "DVE")
                            xv = x0b.ap().rearrange(
                                "p (T mg ml) -> p mg ml T", T=64, mg=4)[:, mg]
                            sv = stgb.ap().rearrange(
                                "p (T mg ml) -> p mg ml T", T=64, mg=4)[:, mg]
                            ctx.mark("DVE", v.tensor_tensor(sv, yg[:, 0:512], xv,
                                                            ALU.mult))
                @block.sync
                def _(sy, b0=b0, b1=b1, csl=csl):
                    ctx.wait(sy, "DVE", ctx.cnt["DVE"])
                    ctx.dma("cST", sy, out=dt["yx0_sc"][b0, csl, :], in_=stg_re[:, :])
                    ctx.dma("cST", sy, out=dt["yx0_sc"][b1, csl, :], in_=stg_im[:, :])


def phase3(nc, ctx, block, dt, P, sb, ps):
    import contextlib
    with contextlib.ExitStack() as st:
        sbl = lambda n, s_, d: st.enter_context(nc.sbuf_tensor(n, list(s_), d))
        psl = lambda n, s_, d=F32: st.enter_context(nc.psum_tensor(n, list(s_), d))

        woutb = sbl("p3_wout", (128, 8 * D), BF16)
        boutp = sbl("p3_bout", (128, 8), F32)
        gwf = sbl("p3_gwf", (128, 8 * 16), F32)
        gwh = sbl("p3_gwh", (128, 8 * 16), BF16)
        gwhf = sbl("p3_gwhf", (128, 8 * 16), F32)
        gwl = sbl("p3_gwl", (128, 8 * 16), BF16)
        ebias = sbl("p3_ebias", (16, 1), F32)
        yx0b = sbl("p3_yx0", (128, 8 * L), BF16)
        r_sb = sbl("p3_r", (128, 8 * L), BF16)
        xpbb = sbl("p3_xpb", (128, 8 * L), BF16)
        xr = sbl("p3_xr", (128, 2 * 512), F32)
        r2t = sbl("p3_r2t", (128, 2 * 512), BF16)
        rm = sbl("p3_rm", (1, 512), F32)
        rv = sbl("p3_rv", (1, 512), F32)
        rw = sbl("p3_rw", (1, 512), F32)
        m_b = sbl("p3_mb", (128, L), BF16)
        rs_b = sbl("p3_rsb", (128, L), BF16)
        t1t = sbl("p3_t1", (128, 512), BF16)
        xp32 = sbl("p3_xp32", (128, 2 * 512), F32)
        xlot = sbl("p3_xlo", (128, 512), BF16)
        bia = sbl("p3_bia", (16, L), F32)
        sc_sb = sbl("p3_sc", (16, L), F32)
        biaT = sbl("p3_biaT", (128, 256), F32)
        scT = sbl("p3_scT", (128, 256), F32)
        m1c = sbl("p3_m1c", (128, 8), F32)
        eq1 = sbl("p3_eq1", (128, 16), F32)
        neg = sbl("p3_neg", (128, 16), F32)
        eq2 = sbl("p3_eq2", (128, 16), F32)
        msk = sbl("p3_msk", (128, 16), F32)
        num = sbl("p3_num", (128, 16), F32)
        gwt = sbl("p3_gwt", (128, 256), F32)
        gem = bia

        _pw = psl("p3_w0", (128, 512))
        pW = [_pw, _pw]
        pst = psl("p3_st", (64, 512))
        plog = psl("p3_log", (16, 2048))
        ptok = psl("p3_tok", (128, 256))
        pgw = psl("p3_gw", (16, 512))
        pbc = _pw

        @block.sync
        def _(sy):
            ctx.dma("cLD", sy, out=boutp[:, :], in_=dt["bout_pp"])
            ctx.dma("cLD", sy, out=ebias[:, :], in_=dt["expert_bias"][:, None])
            ctx.dma("cLD", sy, out=gwf.ap().rearrange("p (kc e) -> p kc e", kc=8),
                    in_=dt["gate_wT"].rearrange("(kc p) e -> p kc e", p=128))
        @block.gpsimd
        def _(g):
            ctx.dma("cGL", g, out=woutb.ap().rearrange("p (kc m) -> p kc m", kc=8),
                    in_=dt["W_out"].rearrange("(kc p) m -> p kc m", p=128))
        ldw = (ctx.cnt["cLD"], ctx.cnt["cGL"])

        @block.vector
        def _(v):
            ctx.wait(v, "cLD", ldw[0])
            ctx.mark("DVE", v.tensor_copy(gwh[:, :], gwf[:, :]))
            ctx.wait(v, "DVE")
            ctx.mark("DVE", v.tensor_copy(gwhf[:, :], gwh[:, :]))
            ctx.wait(v, "DVE")
            ctx.mark("DVE", v.tensor_tensor(gwl[:, :], gwf[:, :], gwhf[:, :],
                                            ALU.subtract))

        for b in range(BPC):
            @block.sync
            def _(sy, b=b):
                ctx.wait(sy, "PE", ctx.cnt["PE"])
                ctx.dma("cLD", sy, out=yx0b.ap().rearrange("p (kc t) -> p kc t", kc=8),
                        in_=dt["yx0_sc"][b].rearrange("(kc p) t -> p kc t", p=128))
            ldy = ctx.cnt["cLD"]

            for dout in range(8):
                for tt in range(4):
                    par = (dout * 4 + tt) % 2
                    pq = pW[par]
                    @block.sync
                    def _(sy, b=b, dout=dout, tt=tt, par=par):
                        ctx.wait(sy, "DVE", ctx.cnt["DVE"])
                        ctx.dma("cLD", sy, out=xr[:, par * 512:(par + 1) * 512],
                                in_=dt["xfm"][b, dout * 128:(dout + 1) * 128,
                                              tt * 512:(tt + 1) * 512])
                    ldx = ctx.cnt["cLD"]
                    @block.tensor
                    def _(t, dout=dout, tt=tt, pq=pq, ldy=ldy):
                        ctx.wait(t, "cLD", ldy)
                        ctx.wait(t, "cGL", ldw[1])
                        ctx.wait(t, "DVE", ctx.cnt["DVE"])
                        for kc in range(8):
                            inst = t.matmul(
                                pq[:, :],
                                woutb[:, (kc * 8 + dout) * 128:(kc * 8 + dout + 1) * 128],
                                yx0b[:, kc * L + tt * 512:kc * L + (tt + 1) * 512],
                                start=(kc == 0), stop=(kc == 7))
                        ctx.mark("PE", inst)
                    @block.vector
                    def _(v, dout=dout, tt=tt, pq=pq, par=par, ldx=ldx):
                        ctx.wait(v, "PE", ctx.cnt["PE"])
                        ctx.wait(v, "cLD", ldx)
                        ctx.wait(v, "DVE")
                        ctx.mark("DVE", v.tensor_scalar(
                            xp32[:, 0:512], pq[:, :], boutp[:, dout:dout + 1],
                            None, ALU.add))
                        ctx.wait(v, "DVE")
                        ctx.mark("DVE", v.tensor_tensor(
                            r_sb[:, dout * L + tt * 512:dout * L + (tt + 1) * 512],
                            xp32[:, 0:512], xr[:, par * 512:(par + 1) * 512],
                            ALU.add))

            for tt in range(4):
                tsl = slice(tt * 512, (tt + 1) * 512)
                for kc in range(8):
                    par = kc % 2
                    @block.scalar
                    def _(s_, kc=kc, tt=tt, par=par):
                        ctx.wait(s_, "DVE", ctx.cnt["DVE"])
                        ctx.wait(s_, "PE", ctx.cnt["PE"])
                        ctx.mark("ACT", s_.activation(
                            r2t[:, par * 512:(par + 1) * 512],
                            r_sb[:, kc * L + tt * 512:kc * L + (tt + 1) * 512],
                            AF.Square))
                    @block.tensor
                    def _(t, kc=kc, tt=tt, par=par):
                        ctx.wait(t, "ACT", ctx.cnt["ACT"])
                        if kc == 0:
                            ctx.wait(t, "DVE", ctx.cnt["DVE"])
                        t.matmul(pst[:, :], P["stat_a"][:, :],
                                 r_sb[:, kc * L + tt * 512:kc * L + (tt + 1) * 512],
                                 start=(kc == 0), stop=False)
                        ctx.mark("PE", t.matmul(
                            pst[:, :], P["stat_b"][:, :],
                            r2t[:, par * 512:(par + 1) * 512],
                            start=False, stop=(kc == 7)))
                @block.vector
                def _(v, tt=tt):
                    ctx.wait(v, "PE", ctx.cnt["PE"])
                    ctx.wait(v, "ACT", ctx.cnt["ACT"])
                    ctx.mark("DVE", v.tensor_scalar_mul(rm[:, :], pst[0:1, :],
                                                        float(1.0 / D)))
                    ctx.mark("DVE", v.tensor_scalar_mul(rv[:, :], pst[32:33, :],
                                                        float(1.0 / D)))
                    ctx.wait(v, "DVE")
                    ctx.mark("DVE", v.tensor_tensor(rw[:, :], rm[:, :], rm[:, :],
                                                    ALU.mult))
                    ctx.wait(v, "DVE")
                    ctx.mark("DVE", v.tensor_tensor(rv[:, :], rv[:, :], rw[:, :],
                                                    ALU.subtract))
                    ctx.wait(v, "DVE")
                    ctx.mark("DVE", v.tensor_scalar_add(rv[:, :], rv[:, :],
                                                        float(1e-5)))
                @block.scalar
                def _(s_, tt=tt):
                    ctx.wait(s_, "DVE", ctx.cnt["DVE"])
                    ctx.mark("ACT", s_.activation(rv[:, :], rv[:, :], AF.Sqrt))
                @block.vector
                def _(v, tt=tt):
                    ctx.wait(v, "ACT", ctx.cnt["ACT"])
                    ctx.mark("DVE", v.reciprocal(rv[:, :], rv[:, :]))
                @block.tensor
                def _(t, tt=tt, tsl=tsl):
                    ctx.wait(t, "DVE", ctx.cnt["DVE"])
                    ctx.wait(t, "ACT", ctx.cnt["ACT"])
                    ctx.mark("PE", t.matmul(pbc[:, 0:512], P["ones1x128f"][:, :],
                                            rm[:, :], start=True, stop=True))
                @block.scalar
                def _(s_, tsl=tsl):
                    ctx.wait(s_, "PE", ctx.cnt["PE"])
                    ctx.mark("ACT", s_.activation(m_b[:, tsl], pbc[:, 0:512], AF.Copy))
                @block.tensor
                def _(t, tsl=tsl):
                    ctx.wait(t, "ACT", ctx.cnt["ACT"])
                    ctx.mark("PE", t.matmul(pbc[:, 0:512], P["ones1x128f"][:, :],
                                            rv[:, :], start=True, stop=True))
                @block.scalar
                def _(s_, tsl=tsl):
                    ctx.wait(s_, "PE", ctx.cnt["PE"])
                    ctx.mark("ACT", s_.activation(rs_b[:, tsl], pbc[:, 0:512], AF.Copy))
            bc_done = (ctx.cnt["ACT"], ctx.cnt["DVE"])

            for kc in range(8):
                for tt in range(4):
                    par = tt % 2
                    rsl = slice(kc * L + tt * 512, kc * L + (tt + 1) * 512)
                    tsl = slice(tt * 512, (tt + 1) * 512)
                    @block.vector
                    def _(v, kc=kc, tt=tt, par=par, rsl=rsl, tsl=tsl):
                        ctx.wait(v, "ACT", ctx.cnt["ACT"])
                        ctx.wait(v, "DVE")
                        ctx.mark("DVE", v.tensor_tensor(t1t[:, :], r_sb[:, rsl],
                                                        m_b[:, tsl], ALU.subtract))
                        ctx.wait(v, "DVE")
                        ctx.wait(v, "cST", ctx.cnt["cST"])
                        ctx.mark("DVE", v.tensor_tensor(
                            xp32[:, par * 512:(par + 1) * 512], t1t[:, :],
                            rs_b[:, tsl], ALU.mult))
                    @block.scalar
                    def _(s_, kc=kc, tt=tt, par=par):
                        ctx.wait(s_, "DVE", ctx.cnt["DVE"])
                        ctx.mark("ACT", s_.activation(
                            xpbb[:, kc * L + tt * 512:kc * L + (tt + 1) * 512],
                            xp32[:, par * 512:(par + 1) * 512], AF.Copy))
                    @block.vector
                    def _(v, kc=kc, tt=tt, par=par):
                        ctx.wait(v, "ACT", ctx.cnt["ACT"])
                        ctx.wait(v, "PE", ctx.cnt["PE"])
                        ctx.wait(v, "DVE")
                        ctx.mark("DVE", v.tensor_tensor(
                            xlot[:, :], xp32[:, par * 512:(par + 1) * 512],
                            xpbb[:, kc * L + tt * 512:kc * L + (tt + 1) * 512],
                            ALU.subtract))
                    @block.sync
                    def _(sy, b=b, kc=kc, tt=tt, par=par):
                        ctx.wait(sy, "DVE", ctx.cnt["DVE"])
                        ctx.dma("cST", sy,
                                out=dt["xP_sc"][b, kc * 128:(kc + 1) * 128,
                                                tt * 512:(tt + 1) * 512],
                                in_=xp32[:, par * 512:(par + 1) * 512])
                    @block.tensor
                    def _(t, kc=kc, tt=tt):
                        ctx.wait(t, "DVE", ctx.cnt["DVE"])
                        gsl = slice(kc * 16, (kc + 1) * 16)
                        first = (kc == 0)
                        t.matmul(plog[:, tt * 512:(tt + 1) * 512], gwh[:, gsl],
                                 xpbb[:, kc * L + tt * 512:kc * L + (tt + 1) * 512],
                                 start=first, stop=False)
                        t.matmul(plog[:, tt * 512:(tt + 1) * 512], gwl[:, gsl],
                                 xpbb[:, kc * L + tt * 512:kc * L + (tt + 1) * 512],
                                 start=False, stop=False)
                        ctx.mark("PE", t.matmul(
                            plog[:, tt * 512:(tt + 1) * 512], gwh[:, gsl],
                            xlot[:, :], start=False, stop=(kc == 7)))
            @block.sync
            def _(sy, b=b):
                ctx.wait(sy, "ACT", ctx.cnt["ACT"])
                ctx.dma("cST", sy,
                        out=dt["xPb_sc"][b].rearrange("(kc p) t -> p kc t", p=128),
                        in_=xpbb.ap().rearrange("p (kc t) -> p kc t", kc=8))

            @block.vector
            def _(v):
                ctx.wait(v, "PE", ctx.cnt["PE"])
                ctx.wait(v, "DVE")
                ctx.mark("DVE", v.tensor_scalar(bia[:, :], plog[:, :],
                                                ebias[:, 0:1], None, ALU.add))
            @block.scalar
            def _(s_):
                ctx.wait(s_, "PE", ctx.cnt["PE"])
                ctx.mark("ACT", s_.activation(sc_sb[:, :], plog[:, :], AF.Sigmoid))
            for sg in range(4):
                @block.tensor
                def _(t, sg=sg):
                    ctx.wait(t, "DVE", ctx.cnt["DVE"])
                    ctx.wait(t, "ACT", ctx.cnt["ACT"])
                    for q in range(4):
                        sl4 = sg * 4 + q
                        ctx.mark("PE", t.matmul(
                            ptok[:, q * 16:(q + 1) * 16],
                            bia[:, sl4 * 128:(sl4 + 1) * 128],
                            P["identf128"][0:16, 0:16], start=True, stop=True))
                @block.vector
                def _(v, sg=sg):
                    ctx.wait(v, "PE", ctx.cnt["PE"])
                    ctx.mark("DVE", v.tensor_copy(biaT[:, sg * 64:(sg + 1) * 64],
                                                  ptok[:, 0:64]))
                @block.tensor
                def _(t, sg=sg):
                    ctx.wait(t, "DVE", ctx.cnt["DVE"])
                    for q in range(4):
                        sl4 = sg * 4 + q
                        ctx.mark("PE", t.matmul(
                            ptok[:, q * 16:(q + 1) * 16],
                            sc_sb[:, sl4 * 128:(sl4 + 1) * 128],
                            P["identf128"][0:16, 0:16], start=True, stop=True))
                @block.vector
                def _(v, sg=sg):
                    ctx.wait(v, "PE", ctx.cnt["PE"])
                    ctx.mark("DVE", v.tensor_copy(scT[:, sg * 64:(sg + 1) * 64],
                                                  ptok[:, 0:64]))
            for sl4 in range(16):
                esl = slice(sl4 * 16, (sl4 + 1) * 16)
                @block.vector
                def _(v, sl4=sl4, esl=esl):
                    ctx.wait(v, "DVE")
                    ctx.mark("DVE", v.reduce_max(m1c[:, 0:1], biaT[:, esl],
                                                 mybir.AxisListType.X))
                    ctx.wait(v, "DVE")
                    ctx.mark("DVE", v.tensor_scalar(eq1[:, :], biaT[:, esl],
                                                    m1c[:, 0:1], None, ALU.is_equal))
                    ctx.wait(v, "DVE")
                    ctx.mark("DVE", v.scalar_tensor_tensor(
                        neg[:, :], eq1[:, :], -1e30, biaT[:, esl],
                        ALU.mult, ALU.add))
                    ctx.wait(v, "DVE")
                    ctx.mark("DVE", v.reduce_max(m1c[:, 1:2], neg[:, :],
                                                 mybir.AxisListType.X))
                    ctx.wait(v, "DVE")
                    ctx.mark("DVE", v.tensor_scalar(eq2[:, :], neg[:, :],
                                                    m1c[:, 1:2], None, ALU.is_equal))
                    ctx.wait(v, "DVE")
                    ctx.mark("DVE", v.tensor_tensor(msk[:, :], eq1[:, :],
                                                    eq2[:, :], ALU.add))
                    ctx.wait(v, "DVE")
                    ctx.mark("DVE", v.tensor_tensor(num[:, :], scT[:, esl],
                                                    msk[:, :], ALU.mult))
                    ctx.wait(v, "DVE")
                    ctx.mark("DVE", v.reduce_sum(m1c[:, 2:3], num[:, :],
                                                 mybir.AxisListType.X))
                    ctx.wait(v, "DVE")
                    ctx.mark("DVE", v.tensor_scalar_add(m1c[:, 3:4], m1c[:, 2:3],
                                                        float(1e-10)))
                    ctx.wait(v, "DVE")
                    ctx.mark("DVE", v.reciprocal(m1c[:, 4:5], m1c[:, 3:4]))
                    ctx.wait(v, "DVE")
                    ctx.mark("DVE", v.tensor_scalar(gwt[:, esl], num[:, :],
                                                    m1c[:, 4:5], None, ALU.mult))
            for sg in range(4):
                @block.tensor
                def _(t, sg=sg):
                    ctx.wait(t, "DVE", ctx.cnt["DVE"])
                    ctx.wait(t, "ACT", ctx.cnt["ACT"])
                    for q in range(4):
                        sl4 = sg * 4 + q
                        ctx.mark("PE", t.matmul(
                            pgw[:, q * 128:(q + 1) * 128],
                            gwt[:, sl4 * 16:(sl4 + 1) * 16],
                            P["identf128"][:, :], start=True, stop=True))
                @block.scalar
                def _(s_, sg=sg):
                    ctx.wait(s_, "PE", ctx.cnt["PE"])
                    ctx.wait(s_, "DVE", ctx.cnt["DVE"])
                    ctx.mark("ACT", s_.activation(gem[:, sg * 512:(sg + 1) * 512],
                                                  pgw[:, :], AF.Copy))
            @block.sync
            def _(sy, b=b):
                ctx.wait(sy, "ACT", ctx.cnt["ACT"])
                ctx.dma("cST", sy, out=dt["gatew_sc"][:, b * L:(b + 1) * L],
                        in_=gem[:, :])


def phase4(nc, ctx, block, dt, P, sb, ps):
    import contextlib
    with contextlib.ExitStack() as st:
        sbl = lambda n, s_, d: st.enter_context(nc.sbuf_tensor(n, list(s_), d))
        psl = lambda n, s_, d=F32: st.enter_context(nc.psum_tensor(n, list(s_), d))

        xpa = sbl("p4_xpa", (128, 8 * TT), BF16)
        sb1p = sbl("p4_sb1", (128, 8), F32)
        sb3p = sbl("p4_sb3", (128, 8), F32)
        sb2p = sbl("p4_sb2", (128, 8), F32)
        satile = sbl("p4_sa", (128, 512), BF16)
        b1p = sbl("p4_b1p", (128, 16 * 4), F32)
        b3p = sbl("p4_b3p", (128, 16 * 4), F32)
        stsh = contextlib.ExitStack()
        sbl2 = lambda n, s_, d: stsh.enter_context(nc.sbuf_tensor(n, list(s_), d))
        sw1b = sbl2("p4_sw1", (128, 8 * D), BF16)
        sw3b = sbl2("p4_sw3", (128, 8 * D), BF16)
        sw2b = sbl2("p4_sw2", (128, 8 * D), BF16)
        hsb = sbl2("p4_hs", (128, 8 * 256), BF16)
        ysst = sbl2("p4_ysst", (128, 2 * 2048), F32)

        pa = psl("p4_a", (128, 512))
        pb = psl("p4_b", (128, 512))
        pys = psl("p4_ys", (128, 2048))

        @block.sync
        def _(sy):
            ctx.dma("cLD", sy, out=sb1p[:, :], in_=dt["sb1_pp"])
            ctx.dma("cLD", sy, out=sb3p[:, :], in_=dt["sb3_pp"])
            ctx.dma("cLD", sy, out=sb2p[:, :], in_=dt["sb2_pp"])
            ctx.dma("cLD", sy, out=b1p.ap().rearrange("p (e c) -> p e c", e=16),
                    in_=dt["b1e_pp"])
            ctx.dma("cLD", sy, out=b3p.ap().rearrange("p (e c) -> p e c", e=16),
                    in_=dt["b3e_pp"])
        @block.gpsimd
        def _(g):
            for b in range(BPC):
                ctx.dma("cGL", g, out=xpa.ap().rearrange(
                    "p (kc bb t) -> p kc bb t", kc=8, bb=BPC)[:, :, b, :],
                    in_=dt["xPb_sc"][b].rearrange("(kc p) t -> p kc t", p=128))
            ctx.dma("cGL", g, out=sw1b.ap().rearrange("p (kc m) -> p kc m", kc=8),
                    in_=dt["sw1T"].rearrange("(kc p) m -> p kc m", p=128))
            ctx.dma("cGL", g, out=sw3b.ap().rearrange("p (kc m) -> p kc m", kc=8),
                    in_=dt["sw3T"].rearrange("(kc p) m -> p kc m", p=128))
            ctx.dma("cGL", g, out=sw2b.ap().rearrange("p (kc m) -> p kc m", kc=8),
                    in_=dt["sw2T"].rearrange("(kc p) m -> p kc m", p=128))
        ld0 = (ctx.cnt["cLD"], ctx.cnt["cGL"])

        for tau in range(32):
            toff = tau * 256
            for fc in range(8):
                @block.tensor
                def _(t, tau=tau, fc=fc, toff=toff):
                    if tau == 0 and fc == 0:
                        ctx.wait(t, "cLD", ld0[0])
                        ctx.wait(t, "cGL", ld0[1])
                    ctx.wait(t, "ACT", ctx.cnt["ACT"])
                    ctx.wait(t, "DVE", ctx.cnt["DVE"])
                    for kc in range(8):
                        mv = xpa[:, kc * TT + toff:kc * TT + toff + 256]
                        t.matmul(pa[:, 0:256], sw1b[:, (kc * 8 + fc) * 128:(kc * 8 + fc + 1) * 128],
                                 mv, start=(kc == 0), stop=(kc == 7))
                        inst = t.matmul(pb[:, 0:256],
                                        sw3b[:, (kc * 8 + fc) * 128:(kc * 8 + fc + 1) * 128],
                                        mv, start=(kc == 0), stop=(kc == 7))
                    ctx.mark("PE", inst)
                @block.scalar
                def _(s_, fc=fc):
                    ctx.wait(s_, "PE", ctx.cnt["PE"])
                    ctx.wait(s_, "DVE", ctx.cnt["DVE"])
                    ctx.mark("ACT", s_.activation(satile[:, 0:256], pa[:, 0:256],
                                                  AF.Silu, bias=sb1p[:, fc:fc + 1]))
                @block.vector
                def _(v, fc=fc):
                    ctx.wait(v, "ACT", ctx.cnt["ACT"])
                    ctx.wait(v, "PE", ctx.cnt["PE"])
                    ctx.mark("DVE", v.scalar_tensor_tensor(
                        hsb[:, fc * 256:(fc + 1) * 256], pb[:, 0:256],
                        sb3p[:, fc:fc + 1], satile[:, 0:256], ALU.add, ALU.mult))
            @block.tensor
            def _(t, tau=tau):
                ctx.wait(t, "DVE", ctx.cnt["DVE"])
                for dout in range(8):
                    for fc in range(8):
                        inst = t.matmul(
                            pys[:, dout * 256:(dout + 1) * 256],
                            sw2b[:, (fc * 8 + dout) * 128:(fc * 8 + dout + 1) * 128],
                            hsb[:, fc * 256:(fc + 1) * 256],
                            start=(fc == 0 and dout % 2 == 0),
                            stop=(fc == 7 and dout % 2 == 1))
                ctx.mark("PE", inst)
            par = tau % 2
            @block.vector
            def _(v, tau=tau, par=par):
                ctx.wait(v, "PE", ctx.cnt["PE"])
                ctx.wait(v, "cST", ctx.cnt["cST"])
                for dout in range(8):
                    ctx.mark("DVE", v.tensor_scalar(
                        ysst[:, par * 2048 + dout * 256:par * 2048 + (dout + 1) * 256],
                        pys[:, dout * 256:(dout + 1) * 256],
                        sb2p[:, dout:dout + 1], None, ALU.add))
            @block.sync
            def _(sy, tau=tau, par=par, toff=toff):
                ctx.wait(sy, "DVE", ctx.cnt["DVE"])
                b = toff // L
                off = toff % L
                ctx.dma("cST", sy,
                        out=dt["ys_sc"][b].rearrange("(kc p) t -> p kc t", p=128)[:, :, off:off + 256],
                        in_=ysst.ap().rearrange("p (r c t) -> p r c t", r=2, c=8)[:, par])

        stsh.close()
        w1r = sbl("p4_w1r", (128, 2 * 8 * 512), BF16)
        w3r = sbl("p4_w3r", (128, 2 * 8 * 512), BF16)
        ger = sbl("p4_ger", (1, 2 * 512), F32)
        geb = sbl("p4_geb", (128, 512), BF16)
        hpt = sbl("p4_hpt", (128, 2 * 4 * 512), BF16)
        for e in range(16):
            wpar = e % 2
            @block.gpsimd
            def _(g, e=e, wpar=wpar):
                ctx.wait(g, "PE", ctx.cnt["PE"])
                ctx.dma("cGL", g, out=w1r.ap().rearrange(
                    "p (r kc m) -> p r kc m", r=2, kc=8)[:, wpar],
                    in_=dt["w1T"][e].rearrange("(kc p) m -> p kc m", p=128))
                ctx.dma("cGL", g, out=w3r.ap().rearrange(
                    "p (r kc m) -> p r kc m", r=2, kc=8)[:, wpar],
                    in_=dt["w3T"][e].rearrange("(kc p) m -> p kc m", p=128))
            ld_e = (ctx.cnt["cGL"], ctx.cnt["cLD"])

            for ta in range(16):
                toff = ta * 512
                gpar = ta % 2
                @block.sync
                def _(sy, e=e, toff=toff, gpar=gpar):
                    ctx.wait(sy, "PE", ctx.cnt["PE"])
                    ctx.dma("cLD", sy, out=ger[:, gpar * 512:(gpar + 1) * 512],
                            in_=dt["gatew_sc"][e:e + 1, toff:toff + 512])
                ldg = ctx.cnt["cLD"]
                @block.tensor
                def _(t, ta=ta, gpar=gpar, ldg=ldg):
                    ctx.wait(t, "cLD", ldg)
                    ctx.wait(t, "ACT", ctx.cnt["ACT"])
                    ctx.mark("PE", t.matmul(pa[:, 0:512], P["ones1x128f"][:, :],
                                            ger[:, gpar * 512:(gpar + 1) * 512],
                                            start=True, stop=True))
                @block.scalar
                def _(s_):
                    ctx.wait(s_, "PE", ctx.cnt["PE"])
                    ctx.wait(s_, "DVE", ctx.cnt["DVE"])
                    ctx.mark("ACT", s_.activation(geb[:, :], pa[:, 0:512], AF.Copy))
                hpar = ta % 2
                for fc in range(4):
                    @block.tensor
                    def _(t, e=e, ta=ta, fc=fc, toff=toff, wpar=wpar, ld_e=ld_e):
                        if ta == 0 and fc == 0:
                            ctx.wait(t, "cGL", ld_e[0])
                        ctx.wait(t, "ACT", ctx.cnt["ACT"])
                        ctx.wait(t, "DVE", ctx.cnt["DVE"])
                        for kc in range(8):
                            mv = xpa[:, kc * TT + toff:kc * TT + toff + 512]
                            t.matmul(pa[:, :],
                                     w1r[:, (wpar * 8 + kc) * 512 + fc * 128:
                                         (wpar * 8 + kc) * 512 + (fc + 1) * 128],
                                     mv, start=(kc == 0), stop=(kc == 7))
                            inst = t.matmul(
                                pb[:, :],
                                w3r[:, (wpar * 8 + kc) * 512 + fc * 128:
                                    (wpar * 8 + kc) * 512 + (fc + 1) * 128],
                                mv, start=(kc == 0), stop=(kc == 7))
                        ctx.mark("PE", inst)
                    @block.scalar
                    def _(s_, e=e, fc=fc):
                        ctx.wait(s_, "PE", ctx.cnt["PE"])
                        ctx.wait(s_, "DVE", ctx.cnt["DVE"])
                        ctx.mark("ACT", s_.activation(
                            satile[:, :], pa[:, :], AF.Silu,
                            bias=b1p[:, e * 4 + fc:e * 4 + fc + 1]))
                    @block.vector
                    def _(v, e=e, fc=fc, hpar=hpar):
                        ctx.wait(v, "ACT", ctx.cnt["ACT"])
                        ctx.wait(v, "cST", ctx.cnt["cST"])
                        ctx.wait(v, "DVE")
                        ctx.mark("DVE", v.scalar_tensor_tensor(
                            hpt[:, hpar * 2048 + fc * 512:hpar * 2048 + (fc + 1) * 512],
                            pb[:, :], b3p[:, e * 4 + fc:e * 4 + fc + 1],
                            satile[:, :], ALU.add, ALU.mult))
                        ctx.wait(v, "DVE")
                        ctx.mark("DVE", v.tensor_tensor(
                            hpt[:, hpar * 2048 + fc * 512:hpar * 2048 + (fc + 1) * 512],
                            hpt[:, hpar * 2048 + fc * 512:hpar * 2048 + (fc + 1) * 512],
                            geb[:, :], ALU.mult))
                @block.sync
                def _(sy, e=e, ta=ta, hpar=hpar):
                    ctx.wait(sy, "DVE", ctx.cnt["DVE"])
                    for tb in range(2):
                        ctx.dma("cST", sy,
                                out=dt["hp_sc"][2 * ta + tb, e].rearrange(
                                    "(fc p) t -> p fc t", p=128),
                                in_=hpt.ap().rearrange(
                                    "p (r fc tb t) -> p r fc tb t",
                                    r=2, fc=4, tb=2)[:, hpar, :, tb, :])


def phase5(nc, ctx, block, dt, P, sb, ps):
    import contextlib
    with contextlib.ExitStack() as st:
        sbl = lambda n, s_, d: st.enter_context(nc.sbuf_tensor(n, list(s_), d))
        psl = lambda n, s_, d=F32: st.enter_context(nc.psum_tensor(n, list(s_), d))

        w2a = sbl("p5_w2", (128, 16 * 4 * D), BF16)
        b2b = sbl("p5_b2", (16, D), BF16)
        hpr = sbl("p5_hp", (128, 4 * 4 * 4 * 256), BF16)
        gwb = sbl("p5_gwb", (16, 256), BF16)
        gwf = sbl("p5_gwf", (16, 256), F32)
        xr = sbl("p5_xr", (128, 8 * 256), F32)
        ysr = sbl("p5_ysr", (128, 8 * 256), F32)
        r2b = sbl("p5_r2", (128, 8 * 256), BF16)
        r2q = sbl("p5_r2q", (128, 2 * 256), BF16)
        rm = sbl("p5_rm", (1, 256), F32)
        rv = sbl("p5_rv", (1, 256), F32)
        rw = sbl("p5_rw", (1, 256), F32)
        m_b = sbl("p5_mb", (128, 256), BF16)
        rs_b = sbl("p5_rsb", (128, 256), BF16)
        t1t = sbl("p5_t1", (128, 256), BF16)
        outs = sbl("p5_out", (128, 8 * 256), F32)

        py = psl("p5_y", (128, 2048))
        pst = psl("p5_st", (64, 256))
        pbc = psl("p5_bc", (128, 256))

        @block.gpsimd
        def _(g):
            for e in range(16):
                ctx.dma("cGL", g, out=w2a.ap().rearrange(
                    "p (e fc m) -> p e fc m", e=16, fc=4)[:, e],
                    in_=dt["w2T"][e].rearrange("(fc p) m -> p fc m", p=128))
            ctx.dma("cGL", g, out=b2b[:, :], in_=dt["b2e"])
        ldw2 = ctx.cnt["cGL"]

        for tau in range(32):
            toff = tau * 256
            b = toff // L
            off = toff % L
            @block.sync
            def _(sy, tau=tau, b=b, off=off):
                ctx.wait(sy, "DVE", ctx.cnt["DVE"])
                ctx.dma("cLD", sy, out=xr.ap().rearrange("p (c t) -> p c t", c=8),
                        in_=dt["xP_sc"][b].rearrange("(c p) t -> p c t", p=128)[:, :, off:off + 256])
                ctx.dma("cLD", sy, out=ysr.ap().rearrange("p (c t) -> p c t", c=8),
                        in_=dt["ys_sc"][b].rearrange("(c p) t -> p c t", p=128)[:, :, off:off + 256])
                ctx.dma("cLD", sy, out=gwf[:, :], in_=dt["gatew_sc"][:, toff:toff + 256])
            ldx = ctx.cnt["cLD"]
            @block.vector
            def _(v, ldx=ldx):
                ctx.wait(v, "cLD", ldx)
                ctx.wait(v, "PE", ctx.cnt["PE"])
                ctx.mark("DVE", v.tensor_copy(gwb[:, :], gwf[:, :]))
            hld = []
            for gq in range(4):
                @block.sync
                def _(sy, tau=tau, gq=gq):
                    ctx.wait(sy, "PE", ctx.cnt["PE"])
                    ctx.dma("cLD", sy, out=hpr.ap().rearrange(
                        "p (r e fc t) -> p r e fc t", r=4, e=4, fc=4)[:, gq],
                        in_=dt["hp_sc"][tau, 4 * gq:4 * gq + 4].rearrange(
                            "e (fc p) t -> p e fc t", p=128))
                hld.append(ctx.cnt["cLD"])
            for gq in range(4):
                @block.tensor
                def _(t, tau=tau, gq=gq, hld=hld, ldw2=ldw2, ldx=ldx):
                    ctx.wait(t, "cLD", hld[gq])
                    if tau == 0 and gq == 0:
                        ctx.wait(t, "cGL", ldw2)
                    if gq == 0:
                        ctx.wait(t, "DVE", ctx.cnt["DVE"])
                    for dout in range(8):
                        for el in range(4):
                            e = gq * 4 + el
                            for fc in range(4):
                                first = (gq == 0 and el == 0 and fc == 0
                                         and dout % 2 == 0)
                                inst = t.matmul(
                                    py[:, dout * 256:(dout + 1) * 256],
                                    w2a[:, ((e * 4 + fc) * 8 + dout) * 128:
                                        ((e * 4 + fc) * 8 + dout + 1) * 128],
                                    hpr[:, (gq * 16 + el * 4 + fc) * 256:
                                        (gq * 16 + el * 4 + fc + 1) * 256],
                                    start=first, stop=False)
                        if gq == 3:
                            inst = t.matmul(
                                py[:, dout * 256:(dout + 1) * 256],
                                b2b[:, dout * 128:(dout + 1) * 128],
                                gwb[:, :], start=False,
                                stop=(dout % 2 == 1))
                    ctx.mark("PE", inst)
            for dout in range(8):
                @block.vector
                def _(v, dout=dout):
                    ctx.wait(v, "PE", ctx.cnt["PE"])
                    ctx.wait(v, "cST", ctx.cnt["cST"])
                    ctx.wait(v, "DVE")
                    ctx.mark("DVE", v.tensor_tensor(
                        outs[:, dout * 256:(dout + 1) * 256],
                        py[:, dout * 256:(dout + 1) * 256],
                        ysr[:, dout * 256:(dout + 1) * 256], ALU.add))
                    ctx.wait(v, "DVE")
                    ctx.mark("DVE", v.tensor_tensor(
                        r2b[:, dout * 256:(dout + 1) * 256],
                        outs[:, dout * 256:(dout + 1) * 256],
                        xr[:, dout * 256:(dout + 1) * 256], ALU.add))
            for kc in range(8):
                par = kc % 2
                @block.scalar
                def _(s_, kc=kc, par=par):
                    ctx.wait(s_, "DVE", ctx.cnt["DVE"])
                    ctx.wait(s_, "PE", ctx.cnt["PE"])
                    ctx.mark("ACT", s_.activation(r2q[:, par * 256:(par + 1) * 256],
                                                  r2b[:, kc * 256:(kc + 1) * 256],
                                                  AF.Square))
                @block.tensor
                def _(t, kc=kc, par=par):
                    ctx.wait(t, "ACT", ctx.cnt["ACT"])
                    if kc == 0:
                        ctx.wait(t, "DVE", ctx.cnt["DVE"])
                    t.matmul(pst[:, :], P["stat_a"][:, :],
                             r2b[:, kc * 256:(kc + 1) * 256],
                             start=(kc == 0), stop=False)
                    ctx.mark("PE", t.matmul(pst[:, :], P["stat_b"][:, :],
                                            r2q[:, par * 256:(par + 1) * 256],
                                            start=False, stop=(kc == 7)))
            @block.vector
            def _(v):
                ctx.wait(v, "PE", ctx.cnt["PE"])
                ctx.wait(v, "ACT", ctx.cnt["ACT"])
                ctx.mark("DVE", v.tensor_scalar_mul(rm[:, :], pst[0:1, :],
                                                    float(1.0 / D)))
                ctx.mark("DVE", v.tensor_scalar_mul(rv[:, :], pst[32:33, :],
                                                    float(1.0 / D)))
                ctx.wait(v, "DVE")
                ctx.mark("DVE", v.tensor_tensor(rw[:, :], rm[:, :], rm[:, :], ALU.mult))
                ctx.wait(v, "DVE")
                ctx.mark("DVE", v.tensor_tensor(rv[:, :], rv[:, :], rw[:, :], ALU.subtract))
                ctx.wait(v, "DVE")
                ctx.mark("DVE", v.tensor_scalar_add(rv[:, :], rv[:, :], float(1e-5)))
            @block.scalar
            def _(s_):
                ctx.wait(s_, "DVE", ctx.cnt["DVE"])
                ctx.mark("ACT", s_.activation(rv[:, :], rv[:, :], AF.Sqrt))
            @block.vector
            def _(v):
                ctx.wait(v, "ACT", ctx.cnt["ACT"])
                ctx.mark("DVE", v.reciprocal(rv[:, :], rv[:, :]))
            @block.tensor
            def _(t):
                ctx.wait(t, "DVE", ctx.cnt["DVE"])
                ctx.wait(t, "ACT", ctx.cnt["ACT"])
                ctx.mark("PE", t.matmul(pbc[:, :], P["ones1x128f"][:, :], rm[:, :],
                                        start=True, stop=True))
            @block.scalar
            def _(s_):
                ctx.wait(s_, "PE", ctx.cnt["PE"])
                ctx.mark("ACT", s_.activation(m_b[:, :], pbc[:, :], AF.Copy))
            @block.tensor
            def _(t):
                ctx.wait(t, "ACT", ctx.cnt["ACT"])
                ctx.mark("PE", t.matmul(pbc[:, :], P["ones1x128f"][:, :], rv[:, :],
                                        start=True, stop=True))
            @block.scalar
            def _(s_):
                ctx.wait(s_, "PE", ctx.cnt["PE"])
                ctx.mark("ACT", s_.activation(rs_b[:, :], pbc[:, :], AF.Copy))
            for dout in range(8):
                @block.vector
                def _(v, dout=dout):
                    ctx.wait(v, "ACT", ctx.cnt["ACT"])
                    ctx.wait(v, "cST", ctx.cnt["cST"])
                    ctx.wait(v, "DVE")
                    ctx.mark("DVE", v.tensor_tensor(t1t[:, :],
                                                    r2b[:, dout * 256:(dout + 1) * 256],
                                                    m_b[:, :], ALU.subtract))
                    ctx.wait(v, "DVE")
                    ctx.mark("DVE", v.tensor_tensor(outs[:, dout * 256:(dout + 1) * 256],
                                                    t1t[:, :], rs_b[:, :], ALU.mult))
            @block.sync
            def _(sy, b=b, off=off):
                ctx.wait(sy, "DVE", ctx.cnt["DVE"])
                ctx.dma("cST", sy,
                        out=dt["out_fm"][b].rearrange("(c p) t -> p c t", p=128)[:, :, off:off + 256],
                        in_=outs.ap().rearrange("p (c t) -> p c t", c=8))



def ppv(v, groups=None):
    v = np.asarray(v)
    if v.ndim == 1:
        return np.ascontiguousarray(v.reshape(-1, 128).T)
    return np.ascontiguousarray(v.reshape(v.shape[0], -1, 128).transpose(2, 0, 1))


def make_in_maps(inputs):
    c = make_consts()
    x = np.ascontiguousarray(
        np.asarray(inputs["L0_attn_x"]).transpose(0, 2, 1))
    base = {
        "W_in": np.ascontiguousarray(inputs["L0_attn_W_in"]),
        "bin_pp": ppv(inputs["L0_attn_b_in"]),
        "shortw_pp": np.ascontiguousarray(
            np.asarray(inputs["L0_attn_short_w"]).reshape(24, 128, 3).transpose(1, 0, 2)),
        "shortb_pp": ppv(inputs["L0_attn_short_b"]),
        "fW1": np.ascontiguousarray(inputs["L0_attn_W1"]),
        "fb1": np.ascontiguousarray(inputs["L0_attn_b1"]),
        "ff1": np.ascontiguousarray(inputs["L0_attn_f1"]),
        "fW2": np.ascontiguousarray(inputs["L0_attn_W2"]),
        "fb2": np.ascontiguousarray(inputs["L0_attn_b2"]),
        "ff2": np.ascontiguousarray(inputs["L0_attn_f2"]),
        "fW3": np.ascontiguousarray(inputs["L0_attn_W3"]),
        "fb3_pp": ppv(inputs["L0_attn_b3"]),
        "fbias_pp": ppv(inputs["L0_attn_filter_bias"]),
        "W_out": np.ascontiguousarray(inputs["L0_attn_W_out"]),
        "bout_pp": ppv(inputs["L0_attn_b_out"]),
        "gate_wT": np.ascontiguousarray(np.asarray(inputs["L0_ffn_gate_w"]).T),
        "expert_bias": np.ascontiguousarray(inputs["L0_ffn_expert_bias"]),
        "w1T": np.ascontiguousarray(np.asarray(inputs["L0_ffn_w1"]).transpose(0, 2, 1)),
        "b1e_pp": ppv(inputs["L0_ffn_b1"]),
        "w3T": np.ascontiguousarray(np.asarray(inputs["L0_ffn_w3"]).transpose(0, 2, 1)),
        "b3e_pp": ppv(inputs["L0_ffn_b3"]),
        "w2T": np.ascontiguousarray(np.asarray(inputs["L0_ffn_w2"]).transpose(0, 2, 1)),
        "b2e": np.ascontiguousarray(inputs["L0_ffn_b2"]),
        "sw1T": np.ascontiguousarray(np.asarray(inputs["L0_ffn_sw1"]).T),
        "sb1_pp": ppv(inputs["L0_ffn_sb1"]),
        "sw3T": np.ascontiguousarray(np.asarray(inputs["L0_ffn_sw3"]).T),
        "sb3_pp": ppv(inputs["L0_ffn_sb3"]),
        "sw2T": np.ascontiguousarray(np.asarray(inputs["L0_ffn_sw2"]).T),
        "sb2_pp": ppv(inputs["L0_ffn_sb2"]),
        "dneg_pp": ppv(c["deltas_neg"][:, 0]),
    }
    c.pop("deltas_neg", None)
    base.update(c)
    maps = []
    for core in range(8):
        m = dict(base)
        m["xfm"] = np.ascontiguousarray(x[core * BPC:(core + 1) * BPC])
        maps.append(m)
    return maps


def kernel(**inputs):
    nc = build_program()
    maps = make_in_maps(inputs)
    res = run_bass_kernel_spmd(nc, maps, core_ids=list(range(8)))
    outs = [np.ascontiguousarray(r["out_fm"].transpose(0, 2, 1))
            for r in res.results]
    return np.concatenate(outs, 0).astype(np.float32)


# version 02
# speedup vs baseline: 1.2621x; 1.2621x over previous
import math
import numpy as np
import ml_dtypes

import concourse.bass as bass
import concourse.mybir as mybir
from concourse.bass_utils import run_bass_kernel_spmd

F32 = mybir.dt.float32
BF16 = mybir.dt.bfloat16
I32 = mybir.dt.int32
AF = mybir.ActivationFunctionType
ALU = mybir.AluOpType

L = 2048
D = 1024
M3 = 3 * D
BPC = 4
TT = 8192
NB = ml_dtypes.bfloat16
TWO_PI = float(2.0 * np.pi)



def make_consts():
    c = {}
    t1 = np.arange(64)
    k1 = np.arange(128)
    t2 = np.arange(32)
    k2 = np.arange(32)
    Tq = np.arange(64)

    w128 = np.exp(-2j * np.pi * np.outer(t1, k1) / 128.0)
    tw = np.exp(-2j * np.pi * np.outer(t2, k1) / 4096.0)
    d1 = np.stack([w128[None, :, :] * tw[:, None, :]][0], 0)
    c["d1tw_re"] = d1.real.astype(NB)
    c["d1tw_im"] = d1.imag.astype(NB)
    c["d1tw_imn"] = (-d1.imag).astype(NB)

    W2c = np.exp(-2j * np.pi * np.outer(t2, k2) / 32.0)
    D2BD = np.zeros((128, 128), np.complex128)
    for cs in range(4):
        D2BD[cs * 32:(cs + 1) * 32, cs * 32:(cs + 1) * 32] = W2c
    c["d2_re"] = D2BD.real.astype(NB)
    c["d2_im"] = D2BD.imag.astype(NB)
    c["d2_imn"] = (-D2BD.imag).astype(NB)

    W2i = np.exp(+2j * np.pi * np.outer(k2, t2) / 32.0)
    D2i = np.zeros((128, 128), np.complex128)
    for cs in range(4):
        D2i[cs * 32:(cs + 1) * 32, cs * 32:(cs + 1) * 32] = W2i
    c["d2i_re"] = D2i.real.astype(NB)
    c["d2i_im"] = D2i.imag.astype(NB)
    c["d2i_imn"] = (-D2i.imag).astype(NB)

    TWI = np.exp(+2j * np.pi * np.outer(k1, t2) / 4096.0) / 4096.0
    TWI4 = np.repeat(TWI[:, None, :], 4, axis=1).reshape(128, 128)
    c["twi_re"] = TWI4.real.astype(np.float32)
    c["twi_im"] = TWI4.imag.astype(np.float32)

    D1i = np.exp(+2j * np.pi * np.outer(k1, Tq) / 128.0)
    c["d1i_re"] = D1i.real.astype(NB)
    c["d1i_im"] = D1i.imag.astype(NB)
    c["d1i_imn"] = (-D1i.imag).astype(NB)

    c["ident128"] = np.eye(128, dtype=NB)
    c["ident64"] = np.eye(64, dtype=NB)
    c["ones1x128"] = np.ones((1, 128), dtype=NB)
    c["ones1x128f"] = np.ones((1, 128), dtype=np.float32)
    sa = np.zeros((128, 64), np.float32); sa[:, 0] = 1.0
    sbv = np.zeros((128, 64), np.float32); sbv[:, 32] = 1.0
    c["stat_a"] = sa.astype(NB)
    c["stat_b"] = sbv.astype(NB)
    c["identf128"] = np.eye(128, dtype=np.float32)

    tau = np.arange(L)
    trev = ((L - 1 - tau) / (L - 1.0)).astype(np.float32)
    ang = 2.0 * np.pi * (L - 1 - tau) / L
    phase = 0.0001 * ang
    c["pe_rev"] = np.stack([trev, np.cos(phase).astype(np.float32),
                            (-np.sin(phase)).astype(np.float32)], 0).astype(np.float32)
    dmax, dmin = (math.log(0.01) / 0.3, math.log(0.01) / 1.5)
    c["deltas_neg"] = (-np.abs(np.linspace(dmax, dmin, D))).astype(np.float32)[:, None]
    c["trev_row"] = trev[None, :]
    return c



class Ctx:

    ENGS = ("PE", "ACT", "DVE", "GP", "SP", "BAR")
    DMAS = ("cLD", "cST", "cGL", "cGS")

    def __init__(self, nc, stack):
        self.nc = nc
        self.sem = {}
        self.cnt = {}
        for n in self.ENGS + self.DMAS:
            self.sem[n] = stack.enter_context(nc.semaphore("sem_" + n))
            self.cnt[n] = 0

    def mark(self, name, inst, n=1):
        step = 16 if name in self.DMAS else 1
        inst.then_inc(self.sem[name], n * step)
        self.cnt[name] += n
        return self.cnt[name]

    def dma(self, name, eng, out, in_):
        self.wait(eng, name)
        inst = eng.dma_start(out=out, in_=in_)
        return self.mark(name, inst)

    def wait(self, eng, name, count=None):
        step = 16 if name in self.DMAS else 1
        if count is None:
            count = self.cnt[name]
        if count > 0:
            eng.wait_ge(self.sem[name], count * step)

    def barrier(self, block):
        @block.sync
        def _(sy):
            for d in self.DMAS:
                self.wait(sy, d)
        @block.tensor
        def _(t):
            self.mark("BAR", t.nop())
        @block.scalar
        def _(s):
            self.mark("BAR", s.nop())
        @block.vector
        def _(v):
            self.mark("BAR", v.nop())
        @block.gpsimd
        def _(g):
            self.mark("BAR", g.nop())
        @block.sync
        def _(sy):
            self.mark("BAR", sy.nop())
        tgt = self.cnt["BAR"]
        @block.tensor
        def _(t):
            self.wait(t, "BAR", tgt)
        @block.scalar
        def _(s):
            self.wait(s, "BAR", tgt)
        @block.vector
        def _(v):
            self.wait(v, "BAR", tgt)
        @block.gpsimd
        def _(g):
            self.wait(g, "BAR", tgt)
        @block.sync
        def _(sy):
            self.wait(sy, "BAR", tgt)



def build_program(dbg=(), upto=99):
    import contextlib
    nc = bass.Bass("TRN2", target_bir_lowering=False, debug=False)

    dt = {}

    def din(name, shape, dtype=F32):
        dt[name] = nc.dram_tensor(name, list(shape), dtype, kind="ExternalInput").ap()
        return dt[name]

    def dsc(name, shape, dtype):
        kind = "ExternalOutput" if name in dbg else "Internal"
        dt[name] = nc.dram_tensor(name, list(shape), dtype, kind=kind).ap()
        return dt[name]

    din("xfm", (BPC, D, L))
    din("W_in", (D, M3)); din("bin_pp", (128, 24))
    din("shortw_pp", (128, 24, 3)); din("shortb_pp", (128, 24))
    din("fW1", (3, 64)); din("fb1", (64,)); din("ff1", (64,))
    din("fW2", (64, 64)); din("fb2", (64,)); din("ff2", (64,))
    din("fW3", (64, D)); din("fb3_pp", (128, 8)); din("fbias_pp", (128, 8))
    din("W_out", (D, D)); din("bout_pp", (128, 8))
    din("gate_wT", (D, 16)); din("expert_bias", (16,))
    din("w1T", (16, D, 512)); din("b1e_pp", (128, 16, 4))
    din("w3T", (16, D, 512)); din("b3e_pp", (128, 16, 4))
    din("w2T", (16, 512, D)); din("b2e", (16, D))
    din("sw1T", (D, D)); din("sb1_pp", (128, 8))
    din("sw3T", (D, D)); din("sb3_pp", (128, 8))
    din("sw2T", (D, D)); din("sb2_pp", (128, 8))
    din("dneg_pp", (128, 8))
    for n in ("d1tw_re", "d1tw_im", "d1tw_imn"):
        din(n, (32, 64, 128), BF16)
    for n in ("d2_re", "d2_im", "d2_imn", "d2i_re", "d2i_im", "d2i_imn",
              "ident128"):
        din(n, (128, 128), BF16)
    din("twi_re", (128, 128)); din("twi_im", (128, 128))
    for n in ("d1i_re", "d1i_im", "d1i_imn"):
        din(n, (128, 64), BF16)
    din("ident64", (64, 64), BF16)
    din("ones1x128", (1, 128), BF16)
    din("ones1x128f", (1, 128))
    din("stat_a", (128, 64), BF16); din("stat_b", (128, 64), BF16)
    din("identf128", (128, 128))
    din("pe_rev", (3, L)); din("trev_row", (1, L))

    dsc("gf_sc", (8, 32, 2, 128, 128), BF16)
    dsc("v_sc", (BPC, D, L), BF16)
    dsc("x0_sc", (BPC, D, L), BF16)
    dsc("yx0_sc", (BPC, D, L), BF16)
    dsc("xP_sc", (BPC, D, L), F32)
    dsc("xPb_sc", (BPC, D, L), BF16)
    dsc("gatew_sc", (16, TT), F32)
    dsc("ys_sc", (BPC, D, L), BF16)
    dsc("hp_sc", (32, 16, 512, 256), BF16)
    out = nc.dram_tensor("out_fm", [BPC, D, L], F32, kind="ExternalOutput").ap()
    dt["out_fm"] = out

    with contextlib.ExitStack() as stack:
        ctx = Ctx(nc, stack)
        sb = lambda name, shape, dtype: stack.enter_context(
            nc.sbuf_tensor(name, list(shape), dtype))
        ps = lambda name, shape, dtype=F32: stack.enter_context(
            nc.psum_tensor(name, list(shape), dtype))
        block = stack.enter_context(nc.Block())

        P = {}
        for n in ("d2_re", "d2_im", "d2_imn", "d2i_re", "d2i_im", "d2i_imn",
                  "ident128"):
            P[n] = sb("c_" + n, (128, 128), BF16)
        P["twi_re"] = sb("c_twire", (128, 128), F32)
        P["twi_im"] = sb("c_twiim", (128, 128), F32)
        for n in ("d1i_re", "d1i_im", "d1i_imn"):
            P[n] = sb("c_" + n, (128, 64), BF16)
        P["ident64"] = sb("c_i64", (64, 64), BF16)
        P["ones1x128"] = sb("c_ones", (1, 128), BF16)
        P["ones1x128f"] = sb("c_onesf", (1, 128), F32)
        P["stat_a"] = sb("c_stata", (128, 64), BF16)
        P["stat_b"] = sb("c_statb", (128, 64), BF16)
        P["identf128"] = sb("c_if128", (128, 128), F32)

        @block.sync
        def _(sy):
            for n in ("d2_re", "d2_im", "d2_imn", "d2i_re", "d2i_im",
                      "d2i_imn", "ident128", "twi_re", "twi_im", "d1i_re",
                      "d1i_im", "d1i_imn", "ident64", "ones1x128", "ones1x128f",
                      "stat_a", "stat_b", "identf128"):
                ctx.dma("cLD", sy, out=P[n][:, :], in_=dt[n])

        ld_consts = ctx.cnt["cLD"]

        phases = [phase0_w(ld_consts), phase1, phase2, phase3, phase4, phase5]
        for i, ph in enumerate(phases):
            if i > upto:
                break
            ph(nc, ctx, block, dt, P, sb, ps)
            if i < upto and i < len(phases) - 1:
                ctx.barrier(block)

        @block.sync
        def _(sy):
            ctx.wait(sy, "cST")
            ctx.wait(sy, "cGS")

    return nc



def phase0_w(ld_consts):
    def f(nc, ctx, block, dt, P, sb, ps):
        return phase0(nc, ctx, block, dt, P, sb, ps, ld_consts)
    return f


def phase0(nc, ctx, block, dt, P, sb, ps, ld_consts):
    import contextlib
    with contextlib.ExitStack() as st:
        sbl = lambda n, s, d: st.enter_context(nc.sbuf_tensor(n, list(s), d))
        psl = lambda n, s, d=F32: st.enter_context(nc.psum_tensor(n, list(s), d))

        pe_sb = sbl("pe_sb", (3, L), F32)
        fW1 = sbl("fW1s", (3, 64), F32)
        fW2 = sbl("fW2s", (64, 64), F32)
        fW3 = sbl("fW3s", (64, D), F32)
        vecs = sbl("fvecs", (64, 6), F32)
        b3pp = sbl("b3pp", (128, 8), F32)
        dneg = sbl("dneg", (128, 8), F32)
        h_sb = sbl("h_sb", (64, L), F32)
        zi = sbl("zi", (64, L), I32)
        zf = sbl("zf", (64, L), F32)
        trow = sbl("trow", (1, L), F32)
        E_sb = sbl("E_sb", (128, L), F32)
        g_sb = sbl("g_sb", (128, L), BF16)
        kpre = sbl("kpre", (128, L), F32)
        for _n in ("d1tw_re", "d1tw_im", "d1tw_imn"):
            P["_" + _n] = sbl("p0" + _n, (64, 32 * 128), BF16)
        U_re = sbl("p0_ure", (64, 32 * 128), BF16)
        S2_re = sbl("p0_s2re", (128, 32 * 128), BF16)
        S2_im = sbl("p0_s2im", (128, 32 * 128), BF16)
        M2_re = sbl("p0_m2re", (128, 32 * 128), BF16)
        M2_im = sbl("p0_m2im", (128, 32 * 128), BF16)
        G_re = sbl("p0_gre", (128, 32 * 128), BF16)
        G_im = sbl("p0_gim", (128, 32 * 128), BF16)

        pmm = psl("p0_mm", (128, 512))
        pT = psl("p0_T", (128, 512))
        pS = psl("p0_S", (128, 512))
        pS2 = psl("p0_S2", (128, 512))
        pX = psl("p0_X", (128, 512))
        pX2 = psl("p0_X2", (128, 512))

        @block.sync
        def _(sy):
            ctx.dma("cLD", sy, out=pe_sb[:, :], in_=dt["pe_rev"])
            ctx.dma("cLD", sy, out=fW1[:, :], in_=dt["fW1"])
            ctx.dma("cLD", sy, out=fW2[:, :], in_=dt["fW2"])
            ctx.dma("cLD", sy, out=fW3[:, :], in_=dt["fW3"])
            ctx.dma("cLD", sy, out=vecs[:, 0:1], in_=dt["fb1"][:, None])
            ctx.dma("cLD", sy, out=vecs[:, 1:2], in_=dt["ff1"][:, None])
            ctx.dma("cLD", sy, out=vecs[:, 2:3], in_=dt["fb2"][:, None])
            ctx.dma("cLD", sy, out=vecs[:, 3:4], in_=dt["ff2"][:, None])
            ctx.dma("cLD", sy, out=b3pp[:, :], in_=dt["fb3_pp"])
            ctx.dma("cLD", sy, out=dneg[:, :], in_=dt["dneg_pp"])
            ctx.dma("cLD", sy, out=trow[:, :], in_=dt["trev_row"])
            for _n in ("d1tw_re", "d1tw_im", "d1tw_imn"):
                ctx.dma("cLD", sy, out=P["_" + _n].ap().rearrange(
                    "p (t2 k) -> p t2 k", t2=32),
                    in_=dt[_n].rearrange("t2 p k -> p t2 k"))
        ld0 = ctx.cnt["cLD"]

        @block.vector
        def _(v):
            ctx.wait(v, "cLD", ld0)
            v.tensor_tensor(vecs[:, 4:5], vecs[:, 0:1], vecs[:, 1:2], ALU.mult)
            ctx.mark("DVE", v.tensor_tensor(vecs[:, 5:6], vecs[:, 2:3],
                                            vecs[:, 3:4], ALU.mult))
        prod_done = ctx.cnt["DVE"]

        def filt_layer(Wsb, K, scale_col, bias_col, src, dst):
            for tt in range(4):
                sl = slice(tt * 512, (tt + 1) * 512)
                @block.tensor
                def _(t, sl=sl, tt=tt):
                    ctx.wait(t, "DVE", ctx.cnt["DVE"])
                    ctx.wait(t, "ACT", ctx.cnt["ACT"])
                    ctx.mark("PE", t.matmul(pmm[0:64, 0:512], Wsb[0:K, :],
                                            src[0:K, sl], start=True, stop=True))
                @block.vector
                def _(v, sl=sl):
                    ctx.wait(v, "PE", ctx.cnt["PE"])
                    ctx.wait(v, "DVE", prod_done)
                    ctx.mark("DVE", v.tensor_scalar(
                        dst[:, sl], pmm[0:64, 0:512],
                        vecs[:, scale_col:scale_col + 1],
                        vecs[:, bias_col:bias_col + 1], ALU.mult, ALU.add))
            @block.vector
            def _(v):
                ctx.wait(v, "DVE")
                ctx.mark("DVE", v.tensor_scalar_mul(zf[:, :], dst[:, :],
                                                    float(1.0 / TWO_PI)))
                ctx.wait(v, "DVE")
                ctx.mark("DVE", v.tensor_copy(zi[:, :], zf[:, :]))
                ctx.wait(v, "DVE")
                ctx.mark("DVE", v.tensor_copy(zf[:, :], zi[:, :]))
                ctx.wait(v, "DVE")
                ctx.mark("DVE", v.scalar_tensor_tensor(
                    dst[:, :], zf[:, :], -TWO_PI, dst[:, :], ALU.mult, ALU.add))
                ctx.wait(v, "DVE")
                ctx.mark("DVE", v.tensor_single_scalar(zf[:, :], dst[:, :],
                                                       float(np.pi), ALU.is_gt))
                ctx.wait(v, "DVE")
                ctx.mark("DVE", v.scalar_tensor_tensor(
                    dst[:, :], zf[:, :], -TWO_PI, dst[:, :], ALU.mult, ALU.add))
                ctx.wait(v, "DVE")
                ctx.mark("DVE", v.tensor_single_scalar(zf[:, :], dst[:, :],
                                                       float(-np.pi), ALU.is_lt))
                ctx.wait(v, "DVE")
                ctx.mark("DVE", v.scalar_tensor_tensor(
                    dst[:, :], zf[:, :], TWO_PI, dst[:, :], ALU.mult, ALU.add))
            @block.scalar
            def _(s):
                ctx.wait(s, "DVE", ctx.cnt["DVE"])
                ctx.mark("ACT", s.activation(dst[:, :], dst[:, :], AF.Sin))

        h1 = sbl("h1_sb", (64, L), F32)
        filt_layer(fW1, 3, 1, 4, pe_sb, h1)
        filt_layer(fW2, 64, 3, 5, h1, h_sb)

        for dc in range(8):
            csl = slice(dc * 128, (dc + 1) * 128)
            for tt in range(4):
                sl = slice(tt * 512, (tt + 1) * 512)
                @block.tensor
                def _(t, sl=sl, tt=tt, dc=dc):
                    ctx.wait(t, "ACT", ctx.cnt["ACT"])
                    ctx.mark("PE", t.matmul(pT[:, 0:512], P["ones1x128f"][:, :],
                                            trow[:, sl], start=True, stop=True))
                @block.scalar
                def _(s, sl=sl, dc=dc):
                    ctx.wait(s, "PE", ctx.cnt["PE"])
                    ctx.mark("ACT", s.activation(E_sb[:, sl], pT[:, 0:512],
                                                 AF.Exp, scale=dneg[:, dc:dc + 1]))
            for tt in range(4):
                sl = slice(tt * 512, (tt + 1) * 512)
                @block.tensor
                def _(t, sl=sl, dc=dc, tt=tt):
                    ctx.wait(t, "DVE", ctx.cnt["DVE"])
                    ctx.mark("PE", t.matmul(pS[:, 0:512], fW3[:, dc * 128:(dc + 1) * 128],
                                            h_sb[:, sl], start=True, stop=True))
                @block.vector
                def _(v, sl=sl, dc=dc):
                    ctx.wait(v, "PE", ctx.cnt["PE"])
                    ctx.wait(v, "ACT", ctx.cnt["ACT"])
                    ctx.mark("DVE", v.tensor_scalar(kpre[:, sl], pS[:, 0:512],
                                                    b3pp[:, dc:dc + 1], None, ALU.add))
                    ctx.wait(v, "DVE")
                    ctx.mark("DVE", v.tensor_tensor(g_sb[:, sl], kpre[:, sl],
                                                    E_sb[:, sl], ALU.mult))
            fwd_fft(nc, ctx, block, P, g_sb, None, U_re, None, S2_re, S2_im,
                    M2_re, M2_im, G_re, G_im, pT, pS, pS2, pX, pX2)
            @block.sync
            def _(sy, dc=dc):
                ctx.wait(sy, "DVE", ctx.cnt["DVE"])
                ctx.wait(sy, "ACT", ctx.cnt["ACT"])
                ctx.dma("cST", sy, out=dt["gf_sc"][dc, :, 0, :, :].rearrange("j p k -> p j k"),
                    in_=G_re.ap().rearrange("p (j k) -> p j k", j=32))
                ctx.dma("cST", sy, out=dt["gf_sc"][dc, :, 1, :, :].rearrange("j p k -> p j k"),
                    in_=G_im.ap().rearrange("p (j k) -> p j k", j=32))
            @block.vector
            def _(v):
                ctx.wait(v, "cST", ctx.cnt["cST"])
            @block.scalar
            def _(s):
                ctx.wait(s, "cST", ctx.cnt["cST"])


def fwd_fft(nc, ctx, block, P, vre_sb, vim_sb, U_re, U_im, S2_re, S2_im,
            M2_re, M2_im, X_re, X_im, pT, pS, pS2, pX, pX2):
    cplx = vim_sb is not None
    planes = ((vre_sb, U_re), (vim_sb, U_im)) if cplx else ((vre_sb, U_re),)

    per_grp = 4 // len(planes)
    ngrp = 32 // per_grp
    for grp in range(ngrp):
        @block.tensor
        def _(t, grp=grp):
            ctx.wait(t, "DVE", ctx.cnt["DVE"])
            ctx.wait(t, "ACT", ctx.cnt["ACT"])
            for q in range(per_grp):
                t2 = grp * per_grp + q
                for pi, (vsb, _u) in enumerate(planes):
                    src = vsb.ap().rearrange("p (t1 t2) -> p t2 t1", t2=32)[:, t2, 0:64]
                    off = (q * len(planes) + pi) * 128
                    ctx.mark("PE", t.matmul(pT[0:64, off:off + 128], src,
                                            P["ident128"][:, :], start=True, stop=True))
        @block.vector
        def _(v, grp=grp):
            ctx.wait(v, "PE", ctx.cnt["PE"])
            for q in range(per_grp):
                t2 = grp * per_grp + q
                for pi, (_v, usb) in enumerate(planes):
                    off = (q * len(planes) + pi) * 128
                    ctx.mark("DVE", v.tensor_copy(
                        usb[0:64, t2 * 128:(t2 + 1) * 128], pT[0:64, off:off + 128]))

    for grp in range(8):
        @block.tensor
        def _(t, grp=grp):
            ctx.wait(t, "DVE", ctx.cnt["DVE"])
            ctx.wait(t, "ACT", ctx.cnt["ACT"])
            for q in range(4):
                t2 = grp * 4 + q
                d_re = P["_d1tw_re"][:, t2 * 128:(t2 + 1) * 128]
                d_im = P["_d1tw_im"][:, t2 * 128:(t2 + 1) * 128]
                d_imn = P["_d1tw_imn"][:, t2 * 128:(t2 + 1) * 128]
                ure = U_re[0:64, t2 * 128:(t2 + 1) * 128]
                re_ps = pS[:, q * 128:(q + 1) * 128]
                im_ps = pS2[:, q * 128:(q + 1) * 128]
                if cplx:
                    uim = U_im[0:64, t2 * 128:(t2 + 1) * 128]
                    t.matmul(re_ps, d_re, ure, start=True, stop=False)
                    t.matmul(re_ps, d_imn, uim, start=False, stop=True)
                    t.matmul(im_ps, d_im, ure, start=True, stop=False)
                    ctx.mark("PE", t.matmul(im_ps, d_re, uim, start=False, stop=True))
                else:
                    t.matmul(re_ps, d_re, ure, start=True, stop=True)
                    ctx.mark("PE", t.matmul(im_ps, d_im, ure, start=True, stop=True))
        pe_c = ctx.cnt["PE"]
        @block.scalar
        def _(s, grp=grp, pe_c=pe_c):
            ctx.wait(s, "PE", pe_c)
            for q in range(4):
                t2 = grp * 4 + q
                dst = S2_re.ap().rearrange("p (c t2) -> p c t2", t2=32)[:, :, t2]
                ctx.mark("ACT", s.activation(dst, pS[:, q * 128:(q + 1) * 128], AF.Copy))
        @block.vector
        def _(v, grp=grp, pe_c=pe_c):
            ctx.wait(v, "PE", pe_c)
            for q in range(4):
                t2 = grp * 4 + q
                dst = S2_im.ap().rearrange("p (c t2) -> p c t2", t2=32)[:, :, t2]
                ctx.mark("DVE", v.tensor_copy(dst, pS2[:, q * 128:(q + 1) * 128]))

    for grp in range(8):
        @block.tensor
        def _(t, grp=grp):
            ctx.wait(t, "ACT", ctx.cnt["ACT"])
            ctx.wait(t, "DVE", ctx.cnt["DVE"])
            for q in range(2):
                j2 = grp * 4 + q * 2
                for jj in range(2):
                    sl = slice((j2 + jj) * 128, (j2 + jj + 1) * 128)
                    t.matmul(pT[:, (q * 2 + jj) * 128:(q * 2 + jj + 1) * 128],
                             S2_re.ap()[:, sl], P["ident128"][:, :],
                             start=True, stop=True)
                    ctx.mark("PE", t.matmul(
                        pS[:, (q * 2 + jj) * 128:(q * 2 + jj + 1) * 128],
                        S2_im.ap()[:, sl], P["ident128"][:, :],
                        start=True, stop=True))
        pe_c = ctx.cnt["PE"]
        @block.scalar
        def _(s, grp=grp, pe_c=pe_c):
            ctx.wait(s, "PE", pe_c)
            ctx.mark("ACT", s.activation(M2_re[:, grp * 512:(grp + 1) * 512],
                                         pT[:, 0:512], AF.Copy))
        @block.vector
        def _(v, grp=grp, pe_c=pe_c):
            ctx.wait(v, "PE", pe_c)
            ctx.mark("DVE", v.tensor_copy(M2_im[:, grp * 512:(grp + 1) * 512],
                                          pS[:, 0:512]))

    for grp in range(8):
        @block.tensor
        def _(t, grp=grp):
            ctx.wait(t, "ACT", ctx.cnt["ACT"])
            ctx.wait(t, "DVE", ctx.cnt["DVE"])
            sl = slice(grp * 512, (grp + 1) * 512)
            mre = M2_re.ap()[:, sl]
            mim = M2_im.ap()[:, sl]
            t.matmul(pX[:, 0:512], P["d2_re"][:, :], mre, start=True, stop=False)
            t.matmul(pX[:, 0:512], P["d2_imn"][:, :], mim, start=False, stop=True)
            t.matmul(pX2[:, 0:512], P["d2_im"][:, :], mre, start=True, stop=False)
            ctx.mark("PE", t.matmul(pX2[:, 0:512], P["d2_re"][:, :], mim,
                                    start=False, stop=True))
        pe_c = ctx.cnt["PE"]
        @block.scalar
        def _(s, grp=grp, pe_c=pe_c):
            ctx.wait(s, "PE", pe_c)
            ctx.mark("ACT", s.activation(X_re[:, grp * 512:(grp + 1) * 512],
                                         pX[:, 0:512], AF.Copy))
        @block.vector
        def _(v, grp=grp, pe_c=pe_c):
            ctx.wait(v, "PE", pe_c)
            ctx.mark("DVE", v.tensor_copy(X_im[:, grp * 512:(grp + 1) * 512],
                                          pX2[:, 0:512]))


def phase1(nc, ctx, block, dt, P, sb, ps):
    import contextlib
    with contextlib.ExitStack() as st:
        sbl = lambda n, s_, d: st.enter_context(nc.sbuf_tensor(n, list(s_), d))
        psl = lambda n, s_, d=F32: st.enter_context(nc.psum_tensor(n, list(s_), d))

        winb = sbl("winb", (128, 8 * M3), BF16)
        binp = sbl("binp", (128, 24), F32)
        swp = sbl("swp", (128, 24 * 3), F32)
        sbp = sbl("sbp", (128, 24), F32)
        xb = sbl("xb", (128, 8 * L), BF16)
        u_sb = sbl("u_sb", (128, 2 * 2052), F32)
        acc = sbl("acc_sb", (128, 2 * L), F32)
        x1h = sbl("x1h", (128, L), F32)
        stg = sbl("stg", (128, 2 * L), BF16)
        pu = [psl("p1_u0", (128, 512)), psl("p1_u1", (128, 512))]

        stg1 = st.enter_context(nc.sbuf_tensor("p1_cstg", [128, 2 * 2048], F32))
        @block.sync
        def _(sy):
            ctx.dma("cLD", sy, out=binp[:, :], in_=dt["bin_pp"])
            ctx.dma("cLD", sy, out=swp.ap().rearrange("p (c s) -> p c s", s=3), in_=dt["shortw_pp"])
            ctx.dma("cLD", sy, out=sbp[:, :], in_=dt["shortb_pp"])
        win_src = dt["W_in"].rearrange("(kc p) m -> p kc m", p=128)
        cl_marks = []
        for i in range(16):
            kc, h = i // 2, i % 2
            par = i % 2
            @block.sync
            def _(sy, i=i, kc=kc, h=h, par=par):
                if i >= 2:
                    ctx.wait(sy, "GP", cl_marks[i - 2])
                ctx.dma("cLD", sy, out=stg1[:, par * 2048:par * 2048 + 1536],
                        in_=win_src[:, kc, h * 1536:(h + 1) * 1536])
            ldi = ctx.cnt["cLD"]
            @block.gpsimd
            def _(g, i=i, kc=kc, h=h, par=par, ldi=ldi):
                ctx.wait(g, "cLD", ldi)
                cl_marks.append(ctx.mark("GP", g.tensor_copy(
                    winb[:, kc * M3 + h * 1536:kc * M3 + (h + 1) * 1536],
                    stg1[:, par * 2048:par * 2048 + 1536])))
        ld_w = (ctx.cnt["cLD"], ctx.cnt["GP"])

        for b in range(BPC):
            x_src = dt["xfm"][b].rearrange("(kc p) t -> p kc t", p=128)
            pe_before = ctx.cnt["PE"]
            xm = []
            for i in range(8):
                par = i % 2
                @block.sync
                def _(sy, i=i, par=par, b=b, pe_before=pe_before):
                    if i >= 2:
                        ctx.wait(sy, "GP", xm[i - 2])
                    else:
                        ctx.wait(sy, "GP", ctx.cnt["GP"])
                    if i == 0:
                        ctx.wait(sy, "PE", pe_before)
                    ctx.dma("cLD", sy, out=stg1[:, par * 2048:(par + 1) * 2048],
                            in_=x_src[:, i, :])
                ldi = ctx.cnt["cLD"]
                @block.gpsimd
                def _(g, i=i, par=par, ldi=ldi, pe_before=pe_before):
                    ctx.wait(g, "cLD", ldi)
                    if i == 0:
                        ctx.wait(g, "PE", pe_before)
                    xm.append(ctx.mark("GP", g.tensor_copy(
                        xb[:, i * L:(i + 1) * L],
                        stg1[:, par * 2048:(par + 1) * 2048])))
            xld = ctx.cnt["GP"]

            order = list(range(8)) + [m for c in range(8) for m in (8 + c, 16 + c)]
            for oi, mc in enumerate(order):
                ub = (oi % 2)
                uof = ub * 2052
                aof = ub * L
                @block.vector
                def _(v, mc=mc, ub=ub, uof=uof, b=b, oi=oi):
                    ctx.wait(v, "DVE", ctx.cnt["DVE"])
                    ctx.mark("DVE", v.memset(u_sb[:, uof:uof + 2], 0.0))
                for tt in range(4):
                    pq = pu[tt % 2]
                    @block.tensor
                    def _(t, mc=mc, tt=tt, pq=pq, xld=xld, oi=oi, b=b):
                        if oi == 0 and tt == 0:
                            ctx.wait(t, "GP", ld_w[1])
                            ctx.wait(t, "cLD", ld_w[0])
                            ctx.wait(t, "GP", xld)
                        ctx.wait(t, "ACT", ctx.cnt["ACT"])
                        for kc in range(8):
                            inst = t.matmul(
                                pq[:, :], winb[:, (kc * 24 + mc) * 128:(kc * 24 + mc + 1) * 128],
                                xb[:, kc * L + tt * 512:kc * L + (tt + 1) * 512],
                                start=(kc == 0), stop=(kc == 7))
                        ctx.mark("PE", inst)
                    @block.scalar
                    def _(s_, mc=mc, tt=tt, pq=pq, uof=uof):
                        ctx.wait(s_, "PE", ctx.cnt["PE"])
                        ctx.wait(s_, "DVE", ctx.cnt["DVE"])
                        ctx.wait(s_, "cLD", ld_w[0])
                        ctx.mark("ACT", s_.activation(
                            u_sb[:, uof + 2 + tt * 512:uof + 2 + (tt + 1) * 512],
                            pq[:, :], AF.Copy))
                @block.vector
                def _(v, mc=mc, uof=uof, aof=aof):
                    ctx.wait(v, "ACT", ctx.cnt["ACT"])
                    ctx.wait(v, "DVE", ctx.cnt["DVE"])
                    ctx.mark("DVE", v.tensor_scalar(
                        u_sb[:, uof + 2:uof + 2050], u_sb[:, uof + 2:uof + 2050],
                        binp[:, mc:mc + 1], None, ALU.add))
                    ctx.wait(v, "DVE")
                    ctx.mark("DVE", v.tensor_scalar(
                        acc[:, aof:aof + L], u_sb[:, uof + 2:uof + 2050],
                        swp[:, mc * 3 + 2:mc * 3 + 3], sbp[:, mc:mc + 1],
                        ALU.mult, ALU.add))
                    ctx.wait(v, "DVE")
                    ctx.mark("DVE", v.scalar_tensor_tensor(
                        acc[:, aof:aof + L], u_sb[:, uof + 1:uof + 2049],
                        swp[:, mc * 3 + 1:mc * 3 + 2], acc[:, aof:aof + L],
                        ALU.mult, ALU.add))
                    ctx.wait(v, "DVE")
                    ctx.mark("DVE", v.scalar_tensor_tensor(
                        acc[:, aof:aof + L], u_sb[:, uof:uof + 2048],
                        swp[:, mc * 3 + 0:mc * 3 + 1], acc[:, aof:aof + L],
                        ALU.mult, ALU.add))
                dve_acc = ctx.cnt["DVE"]
                sof = (oi % 2) * L
                if mc < 8:
                    @block.scalar
                    def _(s_, aof=aof, sof=sof):
                        ctx.wait(s_, "DVE", dve_acc)
                        ctx.wait(s_, "cST", ctx.cnt["cST"])
                        ctx.mark("ACT", s_.activation(stg[:, sof:sof + L],
                                                      acc[:, aof:aof + L], AF.Copy))
                    @block.sync
                    def _(sy, b=b, mc=mc, sof=sof):
                        ctx.wait(sy, "ACT", ctx.cnt["ACT"])
                        ctx.dma("cST", sy, out=dt["x0_sc"][b, mc * 128:(mc + 1) * 128, :],
                            in_=stg[:, sof:sof + L])
                elif mc < 16:
                    @block.vector
                    def _(v, aof=aof):
                        ctx.wait(v, "DVE")
                        ctx.mark("DVE", v.tensor_copy(x1h[:, :], acc[:, aof:aof + L]))
                else:
                    @block.vector
                    def _(v, aof=aof, sof=sof):
                        ctx.wait(v, "DVE")
                        ctx.wait(v, "cST", ctx.cnt["cST"])
                        ctx.mark("DVE", v.tensor_tensor(stg[:, sof:sof + L],
                                                        acc[:, aof:aof + L],
                                                        x1h[:, :], ALU.mult))
                    @block.sync
                    def _(sy, b=b, mc=mc, sof=sof):
                        ctx.wait(sy, "DVE", ctx.cnt["DVE"])
                        ctx.dma("cST", sy, out=dt["v_sc"][b, (mc - 16) * 128:(mc - 15) * 128, :],
                            in_=stg[:, sof:sof + L])


def phase2(nc, ctx, block, dt, P, sb, ps):
    import contextlib
    with contextlib.ExitStack() as st:
        sbl = lambda n, s_, d: st.enter_context(nc.sbuf_tensor(n, list(s_), d))
        psl = lambda n, s_, d=F32: st.enter_context(nc.psum_tensor(n, list(s_), d))

        for _n in ("d1tw_re", "d1tw_im", "d1tw_imn"):
            P["_" + _n] = sbl("p2" + _n, (64, 32 * 128), BF16)
        vre = sbl("p2_vre", (128, L), BF16)
        vim = sbl("p2_vim", (128, L), BF16)
        x0re = sbl("p2_x0re", (128, L), BF16)
        x0im = sbl("p2_x0im", (128, L), BF16)
        gre = sbl("p2_gre", (128, 32 * 128), BF16)
        gim = sbl("p2_gim", (128, 32 * 128), BF16)
        U_re = sbl("p2_ure", (64, 32 * 128), BF16)
        U_im = sbl("p2_uim", (64, 32 * 128), BF16)
        S2_re = sbl("p2_s2re", (128, 32 * 128), BF16)
        S2_im = sbl("p2_s2im", (128, 32 * 128), BF16)
        M2_re = sbl("p2_m2re", (128, 32 * 128), BF16)
        M2_im = sbl("p2_m2im", (128, 32 * 128), BF16)
        X_re = sbl("p2_xre", (128, 32 * 128), BF16)
        X_im = sbl("p2_xim", (128, 32 * 128), BF16)
        Y_re = sbl("p2_yre", (128, 32 * 128), BF16)
        Y_im = sbl("p2_yim", (128, 32 * 128), BF16)
        Bt_re = sbl("p2_btre", (128, 256), F32)
        Bt_im = sbl("p2_btim", (128, 256), F32)
        Bm_re = sbl("p2_bmre", (128, 256), F32)
        Bm_im = sbl("p2_bmim", (128, 256), F32)
        Bp_re = sbl("p2_bpre", (128, 32 * 128), BF16)
        Bp_im = sbl("p2_bpim", (128, 32 * 128), BF16)
        Yb_re = sbl("p2_ybre", (64, 32 * 128), BF16)
        Yb_im = sbl("p2_ybim", (64, 32 * 128), BF16)
        yg = sbl("p2_yg", (128, 512), F32)
        stg_re = sbl("p2_stgre", (128, L), BF16)
        stg_im = sbl("p2_stgim", (128, L), BF16)
        fbias = sbl("p2_fbias", (128, 8), F32)
        twi2_re = sbl("p2_twi2re", (128, 256), F32)
        twi2_im = sbl("p2_twi2im", (128, 256), F32)

        pT = psl("p2_T", (128, 512))
        pS = psl("p2_S", (128, 512))
        pS2 = psl("p2_S2", (128, 512))
        pX = psl("p2_X", (128, 512))
        pX2 = psl("p2_X2", (128, 512))

        @block.sync
        def _(sy):
            ctx.dma("cLD", sy, out=fbias[:, :], in_=dt["fbias_pp"])
            for _n in ("d1tw_re", "d1tw_im", "d1tw_imn"):
                ctx.dma("cLD", sy, out=P["_" + _n].ap().rearrange(
                    "p (t2 k) -> p t2 k", t2=32),
                    in_=dt[_n].rearrange("t2 p k -> p t2 k"))
        @block.vector
        def _(v):
            ctx.wait(v, "cLD", ctx.cnt["cLD"])
            for h in range(2):
                v.tensor_copy(twi2_re[:, h * 128:(h + 1) * 128], P["twi_re"][:, :])
                ctx.mark("DVE", v.tensor_copy(twi2_im[:, h * 128:(h + 1) * 128],
                                              P["twi_im"][:, :]))

        for pair in range(2):
            b0, b1 = 2 * pair, 2 * pair + 1
            for dc in range(8):
                csl = slice(dc * 128, (dc + 1) * 128)
                @block.sync
                def _(sy, b0=b0, b1=b1, dc=dc, csl=csl):
                    ctx.wait(sy, "PE", ctx.cnt["PE"])
                    ctx.wait(sy, "DVE", ctx.cnt["DVE"])
                    ctx.dma("cLD", sy, out=vre[:, :], in_=dt["v_sc"][b0, csl, :])
                    ctx.dma("cLD", sy, out=vim[:, :], in_=dt["v_sc"][b1, csl, :])
                    ctx.dma("cLD", sy, out=x0re[:, :], in_=dt["x0_sc"][b0, csl, :])
                    ctx.dma("cLD", sy, out=x0im[:, :], in_=dt["x0_sc"][b1, csl, :])
                    ctx.dma("cLD", sy, out=gre.ap().rearrange("p (j k) -> p j k", j=32),
                            in_=dt["gf_sc"][dc, :, 0, :, :].rearrange("j p k -> p j k"))
                    ctx.dma("cLD", sy, out=gim.ap().rearrange("p (j k) -> p j k", j=32),
                            in_=dt["gf_sc"][dc, :, 1, :, :].rearrange("j p k -> p j k"))
                ld_u = ctx.cnt["cLD"]

                @block.tensor
                def _(t, ld_u=ld_u):
                    ctx.wait(t, "cLD", ld_u)

                fwd_fft(nc, ctx, block, P, vre, vim, U_re, U_im, S2_re, S2_im,
                        M2_re, M2_im, X_re, X_im, pT, pS, pS2, pX, pX2)

                for grp in range(8):
                    sl = slice(grp * 512, (grp + 1) * 512)
                    @block.vector
                    def _(v, sl=sl, ld_u=ld_u):
                        ctx.wait(v, "cLD", ld_u)
                        ctx.wait(v, "ACT", ctx.cnt["ACT"])
                        ctx.wait(v, "DVE", ctx.cnt["DVE"])
                        ctx.mark("DVE", v.tensor_tensor(Y_re[:, sl], X_re[:, sl],
                                                        gre[:, sl], ALU.mult))
                        ctx.mark("DVE", v.tensor_tensor(Y_im[:, sl], X_im[:, sl],
                                                        gim[:, sl], ALU.mult))
                        ctx.wait(v, "DVE")
                        ctx.mark("DVE", v.tensor_tensor(Y_re[:, sl], Y_re[:, sl],
                                                        Y_im[:, sl], ALU.subtract))
                        ctx.wait(v, "DVE")
                        ctx.mark("DVE", v.tensor_tensor(Y_im[:, sl], X_re[:, sl],
                                                        gim[:, sl], ALU.mult))
                        ctx.wait(v, "DVE")
                        ctx.mark("DVE", v.tensor_tensor(yg[:, 0:512], X_im[:, sl],
                                                        gre[:, sl], ALU.mult))
                        ctx.wait(v, "DVE")
                        ctx.mark("DVE", v.tensor_tensor(Y_im[:, sl], Y_im[:, sl],
                                                        yg[:, 0:512], ALU.add))

                for pr in range(16):
                    @block.tensor
                    def _(t, pr=pr):
                        ctx.wait(t, "DVE", ctx.cnt["DVE"])
                        ctx.wait(t, "ACT", ctx.cnt["ACT"])
                        for jj in range(2):
                            j2 = pr * 2 + jj
                            ysl = slice(j2 * 128, (j2 + 1) * 128)
                            po = slice(jj * 128, (jj + 1) * 128)
                            t.matmul(pS[:, po], Y_re[:, ysl], P["d2i_re"][:, :],
                                     start=True, stop=False)
                            t.matmul(pS[:, po], Y_im[:, ysl], P["d2i_imn"][:, :],
                                     start=False, stop=True)
                            t.matmul(pS2[:, po], Y_im[:, ysl], P["d2i_re"][:, :],
                                     start=True, stop=False)
                            ctx.mark("PE", t.matmul(pS2[:, po], Y_re[:, ysl],
                                                    P["d2i_im"][:, :],
                                                    start=False, stop=True))
                    pe_c = ctx.cnt["PE"]
                    @block.scalar
                    def _(s_, pr=pr, pe_c=pe_c):
                        ctx.wait(s_, "PE", pe_c)
                        ctx.wait(s_, "DVE", ctx.cnt["DVE"])
                        ctx.mark("ACT", s_.activation(Bt_re[:, :], pS[:, 0:256], AF.Copy))
                        ctx.mark("ACT", s_.activation(Bt_im[:, :], pS2[:, 0:256], AF.Copy))
                    act_c = ctx.cnt["ACT"]
                    @block.vector
                    def _(v, pr=pr, act_c=act_c):
                        ctx.wait(v, "ACT", act_c)
                        psl_ = slice(pr * 256, (pr + 1) * 256)
                        ctx.mark("DVE", v.tensor_tensor(Bm_re[:, :], Bt_re[:, :],
                                                        twi2_re[:, :], ALU.mult))
                        ctx.mark("DVE", v.tensor_tensor(Bm_im[:, :], Bt_im[:, :],
                                                        twi2_im[:, :], ALU.mult))
                        ctx.wait(v, "DVE")
                        ctx.mark("DVE", v.tensor_tensor(Bp_re[:, psl_], Bm_re[:, :],
                                                        Bm_im[:, :], ALU.subtract))
                        ctx.wait(v, "DVE")
                        ctx.mark("DVE", v.tensor_tensor(Bm_re[:, :], Bt_re[:, :],
                                                        twi2_im[:, :], ALU.mult))
                        ctx.mark("DVE", v.tensor_tensor(Bm_im[:, :], Bt_im[:, :],
                                                        twi2_re[:, :], ALU.mult))
                        ctx.wait(v, "DVE")
                        ctx.mark("DVE", v.tensor_tensor(Bp_im[:, psl_], Bm_re[:, :],
                                                        Bm_im[:, :], ALU.add))

                for grp in range(8):
                    @block.tensor
                    def _(t, grp=grp):
                        ctx.wait(t, "DVE", ctx.cnt["DVE"])
                        ctx.wait(t, "ACT", ctx.cnt["ACT"])
                        sl = slice(grp * 512, (grp + 1) * 512)
                        bre = Bp_re.ap()[:, sl]
                        bim = Bp_im.ap()[:, sl]
                        t.matmul(pX[0:64, 0:512], P["d1i_re"][:, :], bre,
                                 start=True, stop=False)
                        t.matmul(pX[0:64, 0:512], P["d1i_imn"][:, :], bim,
                                 start=False, stop=True)
                        t.matmul(pX2[0:64, 0:512], P["d1i_im"][:, :], bre,
                                 start=True, stop=False)
                        ctx.mark("PE", t.matmul(pX2[0:64, 0:512], P["d1i_re"][:, :],
                                                bim, start=False, stop=True))
                    pe_c = ctx.cnt["PE"]
                    @block.scalar
                    def _(s_, grp=grp, pe_c=pe_c):
                        ctx.wait(s_, "PE", pe_c)
                        ctx.mark("ACT", s_.activation(Yb_re[0:64, grp * 512:(grp + 1) * 512],
                                                      pX[0:64, 0:512], AF.Copy))
                    @block.vector
                    def _(v, grp=grp, pe_c=pe_c):
                        ctx.wait(v, "PE", pe_c)
                        ctx.mark("DVE", v.tensor_copy(Yb_im[0:64, grp * 512:(grp + 1) * 512],
                                                      pX2[0:64, 0:512]))

                for mg in range(4):
                    @block.tensor
                    def _(t, mg=mg):
                        ctx.wait(t, "ACT", ctx.cnt["ACT"])
                        ctx.wait(t, "DVE", ctx.cnt["DVE"])
                        for ml in range(8):
                            m2 = mg * 8 + ml
                            sel_re = Yb_re.ap().rearrange(
                                "p (j c m) -> p m (j c)", c=4, m=32)[:, m2, :]
                            sel_im = Yb_im.ap().rearrange(
                                "p (j c m) -> p m (j c)", c=4, m=32)[:, m2, :]
                            t.matmul(pT[:, ml * 64:(ml + 1) * 64], sel_re,
                                     P["ident64"][:, :], start=True, stop=True)
                            ctx.mark("PE", t.matmul(pS[:, ml * 64:(ml + 1) * 64],
                                                    sel_im, P["ident64"][:, :],
                                                    start=True, stop=True))
                    pe_c = ctx.cnt["PE"]
                    for pi, (pp, x0b, stgb) in enumerate(
                            ((pT, x0re, stg_re), (pS, x0im, stg_im))):
                        @block.vector
                        def _(v, mg=mg, pp=pp, x0b=x0b, stgb=stgb, pe_c=pe_c, dc=dc):
                            ctx.wait(v, "PE", pe_c)
                            ctx.wait(v, "cST", ctx.cnt["cST"])
                            ctx.wait(v, "DVE")
                            ctx.mark("DVE", v.tensor_scalar(
                                yg[:, 0:512], pp[:, 0:512], fbias[:, dc:dc + 1],
                                None, ALU.add))
                            ctx.wait(v, "DVE")
                            xv = x0b.ap().rearrange(
                                "p (T mg ml) -> p mg ml T", T=64, mg=4)[:, mg]
                            sv = stgb.ap().rearrange(
                                "p (T mg ml) -> p mg ml T", T=64, mg=4)[:, mg]
                            ctx.mark("DVE", v.tensor_tensor(sv, yg[:, 0:512], xv,
                                                            ALU.mult))
                @block.sync
                def _(sy, b0=b0, b1=b1, csl=csl):
                    ctx.wait(sy, "DVE", ctx.cnt["DVE"])
                    ctx.dma("cST", sy, out=dt["yx0_sc"][b0, csl, :], in_=stg_re[:, :])
                    ctx.dma("cST", sy, out=dt["yx0_sc"][b1, csl, :], in_=stg_im[:, :])


def phase3(nc, ctx, block, dt, P, sb, ps):
    import contextlib
    with contextlib.ExitStack() as st:
        sbl = lambda n, s_, d: st.enter_context(nc.sbuf_tensor(n, list(s_), d))
        psl = lambda n, s_, d=F32: st.enter_context(nc.psum_tensor(n, list(s_), d))

        woutb = sbl("p3_wout", (128, 8 * D), BF16)
        boutp = sbl("p3_bout", (128, 8), F32)
        gwf = sbl("p3_gwf", (128, 8 * 16), F32)
        gwh = sbl("p3_gwh", (128, 8 * 16), BF16)
        gwhf = sbl("p3_gwhf", (128, 8 * 16), F32)
        gwl = sbl("p3_gwl", (128, 8 * 16), BF16)
        ebias = sbl("p3_ebias", (16, 1), F32)
        yx0b = sbl("p3_yx0", (128, 8 * L), BF16)
        r_sb = sbl("p3_r", (128, 8 * L), BF16)
        xpbb = sbl("p3_xpb", (128, 8 * L), BF16)
        xr = sbl("p3_xr", (128, 2 * 512), F32)
        r2t = sbl("p3_r2t", (128, 2 * 512), BF16)
        rm = sbl("p3_rm", (1, 512), F32)
        rv = sbl("p3_rv", (1, 512), F32)
        rw = sbl("p3_rw", (1, 512), F32)
        m_b = sbl("p3_mb", (128, L), BF16)
        rs_b = sbl("p3_rsb", (128, L), BF16)
        t1t = sbl("p3_t1", (128, 512), BF16)
        xp32 = sbl("p3_xp32", (128, 2 * 512), F32)
        xlot = sbl("p3_xlo", (128, 512), BF16)
        bia = sbl("p3_bia", (16, L), F32)
        sc_sb = sbl("p3_sc", (16, L), F32)
        biaT = sbl("p3_biaT", (128, 256), F32)
        scT = sbl("p3_scT", (128, 256), F32)
        m1c = sbl("p3_m1c", (128, 8), F32)
        eq1 = sbl("p3_eq1", (128, 16), F32)
        neg = sbl("p3_neg", (128, 16), F32)
        eq2 = sbl("p3_eq2", (128, 16), F32)
        msk = sbl("p3_msk", (128, 16), F32)
        num = sbl("p3_num", (128, 16), F32)
        gwt = sbl("p3_gwt", (128, 256), F32)
        gem = bia

        _pw = psl("p3_w0", (128, 512))
        pW = [_pw, _pw]
        pst = psl("p3_st", (64, 512))
        plog = psl("p3_log", (16, 2048))
        ptok = psl("p3_tok", (128, 256))
        pgw = psl("p3_gw", (16, 512))
        pbc = _pw

        @block.sync
        def _(sy):
            ctx.dma("cLD", sy, out=boutp[:, :], in_=dt["bout_pp"])
            ctx.dma("cLD", sy, out=ebias[:, :], in_=dt["expert_bias"][:, None])
            ctx.dma("cLD", sy, out=gwf.ap().rearrange("p (kc e) -> p kc e", kc=8),
                    in_=dt["gate_wT"].rearrange("(kc p) e -> p kc e", p=128))
        stg3 = st.enter_context(nc.sbuf_tensor("p3_cstg", [128, 2 * 1024], F32))
        wout_src = dt["W_out"].rearrange("(kc p) m -> p kc m", p=128)
        wm3 = []
        for i in range(8):
            par = i % 2
            @block.sync
            def _(sy, i=i, par=par):
                if i >= 2:
                    ctx.wait(sy, "GP", wm3[i - 2])
                ctx.dma("cLD", sy, out=stg3[:, par * 1024:(par + 1) * 1024],
                        in_=wout_src[:, i, :])
            ldi = ctx.cnt["cLD"]
            @block.gpsimd
            def _(g, i=i, par=par, ldi=ldi):
                ctx.wait(g, "cLD", ldi)
                wm3.append(ctx.mark("GP", g.tensor_copy(
                    woutb[:, i * D:(i + 1) * D],
                    stg3[:, par * 1024:(par + 1) * 1024])))
        ldw = (ctx.cnt["cLD"], ctx.cnt["GP"])

        @block.vector
        def _(v):
            ctx.wait(v, "cLD", ldw[0])
            ctx.mark("DVE", v.tensor_copy(gwh[:, :], gwf[:, :]))
            ctx.wait(v, "DVE")
            ctx.mark("DVE", v.tensor_copy(gwhf[:, :], gwh[:, :]))
            ctx.wait(v, "DVE")
            ctx.mark("DVE", v.tensor_tensor(gwl[:, :], gwf[:, :], gwhf[:, :],
                                            ALU.subtract))

        for b in range(BPC):
            @block.sync
            def _(sy, b=b):
                ctx.wait(sy, "PE", ctx.cnt["PE"])
                ctx.dma("cLD", sy, out=yx0b.ap().rearrange("p (kc t) -> p kc t", kc=8),
                        in_=dt["yx0_sc"][b].rearrange("(kc p) t -> p kc t", p=128))
            ldy = ctx.cnt["cLD"]

            for dout in range(8):
                for tt in range(4):
                    par = (dout * 4 + tt) % 2
                    pq = pW[par]
                    @block.sync
                    def _(sy, b=b, dout=dout, tt=tt, par=par):
                        ctx.wait(sy, "DVE", ctx.cnt["DVE"])
                        ctx.dma("cLD", sy, out=xr[:, par * 512:(par + 1) * 512],
                                in_=dt["xfm"][b, dout * 128:(dout + 1) * 128,
                                              tt * 512:(tt + 1) * 512])
                    ldx = ctx.cnt["cLD"]
                    @block.tensor
                    def _(t, dout=dout, tt=tt, pq=pq, ldy=ldy):
                        ctx.wait(t, "cLD", ldy)
                        ctx.wait(t, "GP", ldw[1])
                        ctx.wait(t, "DVE", ctx.cnt["DVE"])
                        for kc in range(8):
                            inst = t.matmul(
                                pq[:, :],
                                woutb[:, (kc * 8 + dout) * 128:(kc * 8 + dout + 1) * 128],
                                yx0b[:, kc * L + tt * 512:kc * L + (tt + 1) * 512],
                                start=(kc == 0), stop=(kc == 7))
                        ctx.mark("PE", inst)
                    @block.vector
                    def _(v, dout=dout, tt=tt, pq=pq, par=par, ldx=ldx):
                        ctx.wait(v, "PE", ctx.cnt["PE"])
                        ctx.wait(v, "cLD", ldx)
                        ctx.wait(v, "DVE")
                        ctx.mark("DVE", v.tensor_scalar(
                            xp32[:, 0:512], pq[:, :], boutp[:, dout:dout + 1],
                            None, ALU.add))
                        ctx.wait(v, "DVE")
                        ctx.mark("DVE", v.tensor_tensor(
                            r_sb[:, dout * L + tt * 512:dout * L + (tt + 1) * 512],
                            xp32[:, 0:512], xr[:, par * 512:(par + 1) * 512],
                            ALU.add))

            for tt in range(4):
                tsl = slice(tt * 512, (tt + 1) * 512)
                for kc in range(8):
                    par = kc % 2
                    @block.vector
                    def _(v, kc=kc, tt=tt, par=par):
                        ctx.wait(v, "DVE")
                        ctx.wait(v, "PE", ctx.cnt["PE"])
                        ctx.mark("DVE", v.tensor_tensor(
                            r2t[:, par * 512:(par + 1) * 512],
                            r_sb[:, kc * L + tt * 512:kc * L + (tt + 1) * 512],
                            r_sb[:, kc * L + tt * 512:kc * L + (tt + 1) * 512],
                            ALU.mult))
                    @block.tensor
                    def _(t, kc=kc, tt=tt, par=par):
                        ctx.wait(t, "DVE", ctx.cnt["DVE"])
                        t.matmul(pst[:, :], P["stat_a"][:, :],
                                 r_sb[:, kc * L + tt * 512:kc * L + (tt + 1) * 512],
                                 start=(kc == 0), stop=False)
                        ctx.mark("PE", t.matmul(
                            pst[:, :], P["stat_b"][:, :],
                            r2t[:, par * 512:(par + 1) * 512],
                            start=False, stop=(kc == 7)))
                @block.vector
                def _(v, tt=tt):
                    ctx.wait(v, "PE", ctx.cnt["PE"])
                    ctx.wait(v, "ACT", ctx.cnt["ACT"])
                    ctx.mark("DVE", v.tensor_scalar_mul(rm[:, :], pst[0:1, :],
                                                        float(1.0 / D)))
                    ctx.mark("DVE", v.tensor_scalar_mul(rv[:, :], pst[32:33, :],
                                                        float(1.0 / D)))
                    ctx.wait(v, "DVE")
                    ctx.mark("DVE", v.tensor_tensor(rw[:, :], rm[:, :], rm[:, :],
                                                    ALU.mult))
                    ctx.wait(v, "DVE")
                    ctx.mark("DVE", v.tensor_tensor(rv[:, :], rv[:, :], rw[:, :],
                                                    ALU.subtract))
                    ctx.wait(v, "DVE")
                    ctx.mark("DVE", v.tensor_scalar_add(rv[:, :], rv[:, :],
                                                        float(1e-5)))
                @block.scalar
                def _(s_, tt=tt):
                    ctx.wait(s_, "DVE", ctx.cnt["DVE"])
                    ctx.mark("ACT", s_.activation(rv[:, :], rv[:, :], AF.Sqrt))
                @block.vector
                def _(v, tt=tt):
                    ctx.wait(v, "ACT", ctx.cnt["ACT"])
                    ctx.mark("DVE", v.reciprocal(rv[:, :], rv[:, :]))
                @block.tensor
                def _(t, tt=tt, tsl=tsl):
                    ctx.wait(t, "DVE", ctx.cnt["DVE"])
                    ctx.wait(t, "ACT", ctx.cnt["ACT"])
                    ctx.mark("PE", t.matmul(pbc[:, 0:512], P["ones1x128f"][:, :],
                                            rm[:, :], start=True, stop=True))
                @block.scalar
                def _(s_, tsl=tsl):
                    ctx.wait(s_, "PE", ctx.cnt["PE"])
                    ctx.mark("ACT", s_.activation(m_b[:, tsl], pbc[:, 0:512], AF.Copy))
                @block.tensor
                def _(t, tsl=tsl):
                    ctx.wait(t, "ACT", ctx.cnt["ACT"])
                    ctx.mark("PE", t.matmul(pbc[:, 0:512], P["ones1x128f"][:, :],
                                            rv[:, :], start=True, stop=True))
                @block.scalar
                def _(s_, tsl=tsl):
                    ctx.wait(s_, "PE", ctx.cnt["PE"])
                    ctx.mark("ACT", s_.activation(rs_b[:, tsl], pbc[:, 0:512], AF.Copy))
            bc_done = (ctx.cnt["ACT"], ctx.cnt["DVE"])

            for kc in range(8):
                for tt in range(4):
                    par = tt % 2
                    rsl = slice(kc * L + tt * 512, kc * L + (tt + 1) * 512)
                    tsl = slice(tt * 512, (tt + 1) * 512)
                    @block.vector
                    def _(v, kc=kc, tt=tt, par=par, rsl=rsl, tsl=tsl):
                        ctx.wait(v, "ACT", ctx.cnt["ACT"])
                        ctx.wait(v, "DVE")
                        ctx.mark("DVE", v.tensor_tensor(t1t[:, :], r_sb[:, rsl],
                                                        m_b[:, tsl], ALU.subtract))
                        ctx.wait(v, "DVE")
                        ctx.wait(v, "cST", ctx.cnt["cST"])
                        ctx.mark("DVE", v.tensor_tensor(
                            xp32[:, par * 512:(par + 1) * 512], t1t[:, :],
                            rs_b[:, tsl], ALU.mult))
                    @block.scalar
                    def _(s_, kc=kc, tt=tt, par=par):
                        ctx.wait(s_, "DVE", ctx.cnt["DVE"])
                        ctx.mark("ACT", s_.activation(
                            xpbb[:, kc * L + tt * 512:kc * L + (tt + 1) * 512],
                            xp32[:, par * 512:(par + 1) * 512], AF.Copy))
                    @block.vector
                    def _(v, kc=kc, tt=tt, par=par):
                        ctx.wait(v, "ACT", ctx.cnt["ACT"])
                        ctx.wait(v, "PE", ctx.cnt["PE"])
                        ctx.wait(v, "DVE")
                        ctx.mark("DVE", v.tensor_tensor(
                            xlot[:, :], xp32[:, par * 512:(par + 1) * 512],
                            xpbb[:, kc * L + tt * 512:kc * L + (tt + 1) * 512],
                            ALU.subtract))
                    @block.sync
                    def _(sy, b=b, kc=kc, tt=tt, par=par):
                        ctx.wait(sy, "DVE", ctx.cnt["DVE"])
                        ctx.dma("cST", sy,
                                out=dt["xP_sc"][b, kc * 128:(kc + 1) * 128,
                                                tt * 512:(tt + 1) * 512],
                                in_=xp32[:, par * 512:(par + 1) * 512])
                    @block.tensor
                    def _(t, kc=kc, tt=tt):
                        ctx.wait(t, "DVE", ctx.cnt["DVE"])
                        gsl = slice(kc * 16, (kc + 1) * 16)
                        first = (kc == 0)
                        t.matmul(plog[:, tt * 512:(tt + 1) * 512], gwh[:, gsl],
                                 xpbb[:, kc * L + tt * 512:kc * L + (tt + 1) * 512],
                                 start=first, stop=False)
                        t.matmul(plog[:, tt * 512:(tt + 1) * 512], gwl[:, gsl],
                                 xpbb[:, kc * L + tt * 512:kc * L + (tt + 1) * 512],
                                 start=False, stop=False)
                        ctx.mark("PE", t.matmul(
                            plog[:, tt * 512:(tt + 1) * 512], gwh[:, gsl],
                            xlot[:, :], start=False, stop=(kc == 7)))
            @block.sync
            def _(sy, b=b):
                ctx.wait(sy, "ACT", ctx.cnt["ACT"])
                ctx.dma("cST", sy,
                        out=dt["xPb_sc"][b].rearrange("(kc p) t -> p kc t", p=128),
                        in_=xpbb.ap().rearrange("p (kc t) -> p kc t", kc=8))

            @block.vector
            def _(v):
                ctx.wait(v, "PE", ctx.cnt["PE"])
                ctx.wait(v, "DVE")
                ctx.mark("DVE", v.tensor_scalar(bia[:, :], plog[:, :],
                                                ebias[:, 0:1], None, ALU.add))
            @block.scalar
            def _(s_):
                ctx.wait(s_, "PE", ctx.cnt["PE"])
                ctx.mark("ACT", s_.activation(sc_sb[:, :], plog[:, :], AF.Sigmoid))
            for sg in range(4):
                @block.tensor
                def _(t, sg=sg):
                    ctx.wait(t, "DVE", ctx.cnt["DVE"])
                    ctx.wait(t, "ACT", ctx.cnt["ACT"])
                    for q in range(4):
                        sl4 = sg * 4 + q
                        ctx.mark("PE", t.matmul(
                            ptok[:, q * 16:(q + 1) * 16],
                            bia[:, sl4 * 128:(sl4 + 1) * 128],
                            P["identf128"][0:16, 0:16], start=True, stop=True))
                @block.vector
                def _(v, sg=sg):
                    ctx.wait(v, "PE", ctx.cnt["PE"])
                    ctx.mark("DVE", v.tensor_copy(biaT[:, sg * 64:(sg + 1) * 64],
                                                  ptok[:, 0:64]))
                @block.tensor
                def _(t, sg=sg):
                    ctx.wait(t, "DVE", ctx.cnt["DVE"])
                    for q in range(4):
                        sl4 = sg * 4 + q
                        ctx.mark("PE", t.matmul(
                            ptok[:, q * 16:(q + 1) * 16],
                            sc_sb[:, sl4 * 128:(sl4 + 1) * 128],
                            P["identf128"][0:16, 0:16], start=True, stop=True))
                @block.vector
                def _(v, sg=sg):
                    ctx.wait(v, "PE", ctx.cnt["PE"])
                    ctx.mark("DVE", v.tensor_copy(scT[:, sg * 64:(sg + 1) * 64],
                                                  ptok[:, 0:64]))
            for sl4 in range(16):
                esl = slice(sl4 * 16, (sl4 + 1) * 16)
                @block.vector
                def _(v, sl4=sl4, esl=esl):
                    ctx.wait(v, "DVE")
                    ctx.mark("DVE", v.reduce_max(m1c[:, 0:1], biaT[:, esl],
                                                 mybir.AxisListType.X))
                    ctx.wait(v, "DVE")
                    ctx.mark("DVE", v.tensor_scalar(eq1[:, :], biaT[:, esl],
                                                    m1c[:, 0:1], None, ALU.is_equal))
                    ctx.wait(v, "DVE")
                    ctx.mark("DVE", v.scalar_tensor_tensor(
                        neg[:, :], eq1[:, :], -1e30, biaT[:, esl],
                        ALU.mult, ALU.add))
                    ctx.wait(v, "DVE")
                    ctx.mark("DVE", v.reduce_max(m1c[:, 1:2], neg[:, :],
                                                 mybir.AxisListType.X))
                    ctx.wait(v, "DVE")
                    ctx.mark("DVE", v.tensor_scalar(eq2[:, :], neg[:, :],
                                                    m1c[:, 1:2], None, ALU.is_equal))
                    ctx.wait(v, "DVE")
                    ctx.mark("DVE", v.tensor_tensor(msk[:, :], eq1[:, :],
                                                    eq2[:, :], ALU.add))
                    ctx.wait(v, "DVE")
                    ctx.mark("DVE", v.tensor_tensor(num[:, :], scT[:, esl],
                                                    msk[:, :], ALU.mult))
                    ctx.wait(v, "DVE")
                    ctx.mark("DVE", v.reduce_sum(m1c[:, 2:3], num[:, :],
                                                 mybir.AxisListType.X))
                    ctx.wait(v, "DVE")
                    ctx.mark("DVE", v.tensor_scalar_add(m1c[:, 3:4], m1c[:, 2:3],
                                                        float(1e-10)))
                    ctx.wait(v, "DVE")
                    ctx.mark("DVE", v.reciprocal(m1c[:, 4:5], m1c[:, 3:4]))
                    ctx.wait(v, "DVE")
                    ctx.mark("DVE", v.tensor_scalar(gwt[:, esl], num[:, :],
                                                    m1c[:, 4:5], None, ALU.mult))
            for sg in range(4):
                @block.tensor
                def _(t, sg=sg):
                    ctx.wait(t, "DVE", ctx.cnt["DVE"])
                    ctx.wait(t, "ACT", ctx.cnt["ACT"])
                    for q in range(4):
                        sl4 = sg * 4 + q
                        ctx.mark("PE", t.matmul(
                            pgw[:, q * 128:(q + 1) * 128],
                            gwt[:, sl4 * 16:(sl4 + 1) * 16],
                            P["identf128"][:, :], start=True, stop=True))
                @block.scalar
                def _(s_, sg=sg):
                    ctx.wait(s_, "PE", ctx.cnt["PE"])
                    ctx.wait(s_, "DVE", ctx.cnt["DVE"])
                    ctx.mark("ACT", s_.activation(gem[:, sg * 512:(sg + 1) * 512],
                                                  pgw[:, :], AF.Copy))
            @block.sync
            def _(sy, b=b):
                ctx.wait(sy, "ACT", ctx.cnt["ACT"])
                ctx.dma("cST", sy, out=dt["gatew_sc"][:, b * L:(b + 1) * L],
                        in_=gem[:, :])


def phase4(nc, ctx, block, dt, P, sb, ps):
    import contextlib
    with contextlib.ExitStack() as st:
        sbl = lambda n, s_, d: st.enter_context(nc.sbuf_tensor(n, list(s_), d))
        psl = lambda n, s_, d=F32: st.enter_context(nc.psum_tensor(n, list(s_), d))

        xpa = sbl("p4_xpa", (128, 8 * TT), BF16)
        sb1p = sbl("p4_sb1", (128, 8), F32)
        sb3p = sbl("p4_sb3", (128, 8), F32)
        sb2p = sbl("p4_sb2", (128, 8), F32)
        satile = sbl("p4_sa", (128, 2 * 512), BF16)
        b1p = sbl("p4_b1p", (128, 16 * 4), F32)
        b3p = sbl("p4_b3p", (128, 16 * 4), F32)
        stg4 = st.enter_context(nc.sbuf_tensor("p4_cstg", [128, 2 * 1024], F32))
        stsh = contextlib.ExitStack()
        sbl2 = lambda n, s_, d: stsh.enter_context(nc.sbuf_tensor(n, list(s_), d))
        sw1b = sbl2("p4_sw1", (128, 8 * D), BF16)
        sw3b = sbl2("p4_sw3", (128, 8 * D), BF16)
        sw2b = sbl2("p4_sw2", (128, 8 * D), BF16)
        hsb = sbl2("p4_hs", (128, 8 * 256), BF16)
        ysst = sbl2("p4_ysst", (128, 2048), BF16)

        pa = psl("p4_a", (128, 1024))
        pb = psl("p4_b", (128, 1024))
        pys = psl("p4_ys", (128, 2048))

        @block.sync
        def _(sy):
            ctx.dma("cLD", sy, out=sb1p[:, :], in_=dt["sb1_pp"])
            ctx.dma("cLD", sy, out=sb3p[:, :], in_=dt["sb3_pp"])
            ctx.dma("cLD", sy, out=sb2p[:, :], in_=dt["sb2_pp"])
            ctx.dma("cLD", sy, out=b1p.ap().rearrange("p (e c) -> p e c", e=16),
                    in_=dt["b1e_pp"])
            ctx.dma("cLD", sy, out=b3p.ap().rearrange("p (e c) -> p e c", e=16),
                    in_=dt["b3e_pp"])
        @block.sync
        def _(sy):
            for b in range(BPC):
                ctx.dma("cLD", sy, out=xpa.ap().rearrange(
                    "p (kc bb t) -> p kc bb t", kc=8, bb=BPC)[:, :, b, :],
                    in_=dt["xPb_sc"][b].rearrange("(kc p) t -> p kc t", p=128))
        swm = []
        for wi, (nm, dstb) in enumerate((("sw1T", sw1b), ("sw3T", sw3b),
                                         ("sw2T", sw2b))):
            src = dt[nm].rearrange("(kc p) m -> p kc m", p=128)
            for i in range(8):
                j = wi * 8 + i
                par = j % 2
                @block.sync
                def _(sy, i=i, par=par, src=src, j=j):
                    if j >= 2:
                        ctx.wait(sy, "GP", swm[j - 2])
                    ctx.dma("cLD", sy, out=stg4[:, par * 1024:(par + 1) * 1024],
                            in_=src[:, i, :])
                ldi = ctx.cnt["cLD"]
                @block.gpsimd
                def _(g, i=i, par=par, dstb=dstb, ldi=ldi):
                    ctx.wait(g, "cLD", ldi)
                    swm.append(ctx.mark("GP", g.tensor_copy(
                        dstb[:, i * D:(i + 1) * D],
                        stg4[:, par * 1024:(par + 1) * 1024])))
        ld0 = (ctx.cnt["cLD"], ctx.cnt["GP"])

        sh_act = {0: 0, 1: 0}
        sh_dve = {0: 0, 1: 0}
        sh_pe = {0: 0, 1: 0}
        for tau in range(32):
            toff = tau * 256
            for fc in range(8):
                par = fc % 2
                po = par * 512
                @block.tensor
                def _(t, tau=tau, fc=fc, toff=toff, par=par, po=po):
                    if tau == 0 and fc == 0:
                        ctx.wait(t, "cLD", ld0[0])
                        ctx.wait(t, "GP", ld0[1])
                    ctx.wait(t, "ACT", sh_act[par])
                    ctx.wait(t, "DVE", sh_dve[par])
                    for kc in range(8):
                        mv = xpa[:, kc * TT + toff:kc * TT + toff + 256]
                        t.matmul(pa[:, po:po + 256], sw1b[:, (kc * 8 + fc) * 128:(kc * 8 + fc + 1) * 128],
                                 mv, start=(kc == 0), stop=(kc == 7))
                        inst = t.matmul(pb[:, po:po + 256],
                                        sw3b[:, (kc * 8 + fc) * 128:(kc * 8 + fc + 1) * 128],
                                        mv, start=(kc == 0), stop=(kc == 7))
                    sh_pe[par] = ctx.mark("PE", inst)
                @block.scalar
                def _(s_, fc=fc, par=par, po=po):
                    ctx.wait(s_, "PE", sh_pe[par])
                    ctx.wait(s_, "DVE", sh_dve[par])
                    sh_act[par] = ctx.mark("ACT", s_.activation(
                        satile[:, po:po + 256], pa[:, po:po + 256],
                        AF.Silu, bias=sb1p[:, fc:fc + 1]))
                @block.vector
                def _(v, fc=fc, par=par, po=po):
                    ctx.wait(v, "ACT", sh_act[par])
                    ctx.wait(v, "PE", ctx.cnt["PE"])
                    sh_dve[par] = ctx.mark("DVE", v.scalar_tensor_tensor(
                        hsb[:, fc * 256:(fc + 1) * 256], pb[:, po:po + 256],
                        sb3p[:, fc:fc + 1], satile[:, po:po + 256],
                        ALU.add, ALU.mult))
            @block.tensor
            def _(t, tau=tau):
                ctx.wait(t, "DVE", ctx.cnt["DVE"])
                for dout in range(8):
                    for fc in range(8):
                        inst = t.matmul(
                            pys[:, dout * 256:(dout + 1) * 256],
                            sw2b[:, (fc * 8 + dout) * 128:(fc * 8 + dout + 1) * 128],
                            hsb[:, fc * 256:(fc + 1) * 256],
                            start=(fc == 0 and dout % 2 == 0),
                            stop=(fc == 7 and dout % 2 == 1))
                ctx.mark("PE", inst)
            @block.vector
            def _(v, tau=tau):
                ctx.wait(v, "PE", ctx.cnt["PE"])
                ctx.wait(v, "cST", ctx.cnt["cST"])
                for dout in range(8):
                    ctx.mark("DVE", v.tensor_scalar(
                        ysst[:, dout * 256:(dout + 1) * 256],
                        pys[:, dout * 256:(dout + 1) * 256],
                        sb2p[:, dout:dout + 1], None, ALU.add))
            @block.sync
            def _(sy, tau=tau, toff=toff):
                ctx.wait(sy, "DVE", ctx.cnt["DVE"])
                b = toff // L
                off = toff % L
                ctx.dma("cST", sy,
                        out=dt["ys_sc"][b].rearrange("(kc p) t -> p kc t", p=128)[:, :, off:off + 256],
                        in_=ysst.ap().rearrange("p (c t) -> p c t", c=8))

        stsh.close()
        w1r = sbl("p4_w1r", (128, 2 * 8 * 512), BF16)
        w3r = sbl("p4_w3r", (128, 2 * 8 * 512), BF16)
        ger = sbl("p4_ger", (1, 2 * 512), F32)
        geb = sbl("p4_geb", (128, 512), BF16)
        hpt = sbl("p4_hpt", (128, 2 * 4 * 512), BF16)
        pe_after = {}
        for e in range(16):
            wpar = e % 2
            war_pe = pe_after.get(e - 2, 0)
            for wi, (wt, dstb) in enumerate((("w1T", w1r), ("w3T", w3r))):
                src = dt[wt][e].rearrange("(kc p) m -> p kc m", p=128)
                for i in range(4):
                    j = wi * 4 + i
                    spar = j % 2
                    @block.sync
                    def _(sy, i=i, spar=spar, src=src, j=j):
                        ctx.wait(sy, "GP", max(0, ctx.cnt["GP"] - 1))
                        ctx.dma("cLD", sy, out=stg4.ap().rearrange(
                            "p (r k m) -> p r k m", r=2, k=2)[:, spar],
                            in_=src[:, 2 * i:2 * i + 2, :])
                    ldi = ctx.cnt["cLD"]
                    @block.gpsimd
                    def _(g, i=i, spar=spar, dstb=dstb, ldi=ldi,
                          wpar=wpar, war_pe=war_pe):
                        ctx.wait(g, "cLD", ldi)
                        ctx.wait(g, "PE", war_pe)
                        ctx.mark("GP", g.tensor_copy(
                            dstb[:, (wpar * 8 + 2 * i) * 512:(wpar * 8 + 2 * (i + 1)) * 512],
                            stg4[:, spar * 1024:(spar + 1) * 1024]))
            ld_e = (ctx.cnt["GP"], ctx.cnt["cLD"])

            if e == 0:
                ex_act = {0: 0, 1: 0}
                ex_stt = {0: 0, 1: 0}
                ex_gm = {0: 0, 1: 0}
                ex_pe = {0: 0, 1: 0}
                ex_st = {0: 0, 1: 0}
                ex_geb = [0]
                ex_bc = [0]
            for ta in range(16):
                toff = ta * 512
                gpar = ta % 2
                @block.sync
                def _(sy, e=e, toff=toff, gpar=gpar):
                    ctx.wait(sy, "PE", ex_bc[0])
                    ctx.dma("cLD", sy, out=ger[:, gpar * 512:(gpar + 1) * 512],
                            in_=dt["gatew_sc"][e:e + 1, toff:toff + 512])
                ldg = ctx.cnt["cLD"]
                @block.tensor
                def _(t, ta=ta, gpar=gpar, ldg=ldg):
                    ctx.wait(t, "cLD", ldg)
                    ctx.wait(t, "DVE", ex_geb[0])
                    ex_bc[0] = ctx.mark("PE", t.matmul(
                        pys[:, 0:512], P["ones1x128f"][:, :],
                        ger[:, gpar * 512:(gpar + 1) * 512],
                        start=True, stop=True))
                @block.vector
                def _(v):
                    ctx.wait(v, "PE", ex_bc[0])
                    ctx.wait(v, "DVE", ex_gm[0])
                    ctx.wait(v, "DVE", ex_gm[1])
                    ex_geb[0] = ctx.mark("DVE", v.tensor_copy(geb[:, :], pys[:, 0:512]))
                hpar = ta % 2
                for fc in range(4):
                    par = fc % 2
                    po = par * 512
                    @block.tensor
                    def _(t, e=e, ta=ta, fc=fc, toff=toff, wpar=wpar,
                          ld_e=ld_e, par=par, po=po):
                        if ta == 0 and fc == 0:
                            ctx.wait(t, "GP", ld_e[0])
                        ctx.wait(t, "ACT", ex_act[par])
                        ctx.wait(t, "DVE", ex_stt[par])
                        for kc in range(8):
                            mv = xpa[:, kc * TT + toff:kc * TT + toff + 512]
                            t.matmul(pa[:, po:po + 512],
                                     w1r[:, (wpar * 8 + kc) * 512 + fc * 128:
                                         (wpar * 8 + kc) * 512 + (fc + 1) * 128],
                                     mv, start=(kc == 0), stop=(kc == 7))
                            inst = t.matmul(
                                pb[:, po:po + 512],
                                w3r[:, (wpar * 8 + kc) * 512 + fc * 128:
                                    (wpar * 8 + kc) * 512 + (fc + 1) * 128],
                                mv, start=(kc == 0), stop=(kc == 7))
                        ex_pe[par] = ctx.mark("PE", inst)
                    @block.scalar
                    def _(s_, e=e, fc=fc, par=par, po=po):
                        ctx.wait(s_, "PE", ex_pe[par])
                        ctx.wait(s_, "DVE", ex_stt[par])
                        ex_act[par] = ctx.mark("ACT", s_.activation(
                            satile[:, po:po + 512], pa[:, po:po + 512], AF.Silu,
                            bias=b1p[:, e * 4 + fc:e * 4 + fc + 1]))
                    @block.vector
                    def _(v, e=e, fc=fc, hpar=hpar, par=par, po=po):
                        ctx.wait(v, "ACT", ex_act[par])
                        ctx.wait(v, "cST", ex_st[hpar])
                        ex_stt[par] = ctx.mark("DVE", v.scalar_tensor_tensor(
                            hpt[:, hpar * 2048 + fc * 512:hpar * 2048 + (fc + 1) * 512],
                            pb[:, po:po + 512], b3p[:, e * 4 + fc:e * 4 + fc + 1],
                            satile[:, po:po + 512], ALU.add, ALU.mult))
                        ctx.wait(v, "DVE")
                        ex_gm[par] = ctx.mark("DVE", v.tensor_tensor(
                            hpt[:, hpar * 2048 + fc * 512:hpar * 2048 + (fc + 1) * 512],
                            hpt[:, hpar * 2048 + fc * 512:hpar * 2048 + (fc + 1) * 512],
                            geb[:, :], ALU.mult))
                @block.sync
                def _(sy, e=e, ta=ta, hpar=hpar):
                    ctx.wait(sy, "DVE", max(ex_gm[0], ex_gm[1]))
                    for tb in range(2):
                        inst_m = ctx.dma("cST", sy,
                                out=dt["hp_sc"][2 * ta + tb, e].rearrange(
                                    "(fc p) t -> p fc t", p=128),
                                in_=hpt.ap().rearrange(
                                    "p (r fc tb t) -> p r fc tb t",
                                    r=2, fc=4, tb=2)[:, hpar, :, tb, :])
                    ex_st[hpar] = inst_m
                pe_after[e] = ctx.cnt["PE"]


def phase5(nc, ctx, block, dt, P, sb, ps):
    import contextlib
    with contextlib.ExitStack() as st:
        sbl = lambda n, s_, d: st.enter_context(nc.sbuf_tensor(n, list(s_), d))
        psl = lambda n, s_, d=F32: st.enter_context(nc.psum_tensor(n, list(s_), d))

        w2a = sbl("p5_w2", (128, 16 * 4 * D), BF16)
        b2b = sbl("p5_b2", (16, D), BF16)
        st5 = contextlib.ExitStack()
        stg5 = st5.enter_context(nc.sbuf_tensor("p5_cstg", [128, 2048], F32))

        py = psl("p5_y", (128, 2048))
        pst = psl("p5_st", (64, 256))
        pbc = psl("p5_bc", (128, 256))

        w2m = []
        for j in range(32):
            e, h = j // 2, j % 2
            src = dt["w2T"][e].rearrange("(fc p) m -> p fc m", p=128)
            @block.sync
            def _(sy, j=j, src=src, h=h):
                if j >= 1:
                    ctx.wait(sy, "GP", w2m[j - 1])
                ctx.dma("cLD", sy, out=stg5.ap().rearrange(
                    "p (k m) -> p k m", k=2), in_=src[:, 2 * h:2 * h + 2, :])
            ldi = ctx.cnt["cLD"]
            @block.gpsimd
            def _(g, j=j, e=e, h=h, ldi=ldi):
                ctx.wait(g, "cLD", ldi)
                w2m.append(ctx.mark("GP", g.tensor_copy(
                    w2a[:, (e * 4 + 2 * h) * D:(e * 4 + 2 * h + 2) * D],
                    stg5[:, :])))
        @block.sync
        def _(sy):
            ctx.wait(sy, "GP", w2m[-1])
            ctx.dma("cLD", sy, out=stg5[0:16, 0:D], in_=dt["b2e"])
        ldb2 = ctx.cnt["cLD"]
        @block.gpsimd
        def _(g, ldb2=ldb2):
            ctx.wait(g, "cLD", ldb2)
            ctx.mark("GP", g.tensor_copy(b2b[:, :], stg5[0:16, 0:D]))
        gp_b2 = ctx.cnt["GP"]
        st5.close()
        hpr = sbl("p5_hp", (128, 3 * 4 * 4 * 256), BF16)
        gwb = sbl("p5_gwb", (16, 2 * 256), BF16)
        gwf = sbl("p5_gwf", (16, 2 * 256), F32)
        xr = sbl("p5_xr", (128, 2 * 8 * 256), BF16)
        ysr = sbl("p5_ysr", (128, 2 * 8 * 256), BF16)
        r2b = sbl("p5_r2", (128, 2 * 8 * 256), BF16)
        r2q = sbl("p5_r2q", (128, 2 * 256), BF16)
        rm = sbl("p5_rm", (1, 256), F32)
        rv = sbl("p5_rv", (1, 256), F32)
        rw = sbl("p5_rw", (1, 256), F32)
        m_b = sbl("p5_mb", (128, 256), BF16)
        rs_b = sbl("p5_rsb", (128, 256), BF16)
        t1t = sbl("p5_t1", (128, 256), BF16)
        outs = sbl("p5_out", (128, 2 * 8 * 256), F32)

        evac_marks = {0: 0, 1: 0}
        hp_marks = {}
        ld_marks = {}
        st_marks = {0: 0, 1: 0}
        ln_state = {}

        def emit_loads(tau):
            if tau >= 32:
                return
            toff = tau * 256
            b = toff // L
            off = toff % L
            rpar = tau % 2
            @block.sync
            def _(sy, tau=tau, b=b, off=off, rpar=rpar):
                ctx.wait(sy, "DVE", evac_marks[rpar])
                ctx.dma("cLD", sy, out=xr.ap().rearrange(
                    "p (r c t) -> p r c t", r=2, c=8)[:, rpar],
                    in_=dt["xPb_sc"][b].rearrange("(c p) t -> p c t", p=128)[:, :, off:off + 256])
                ctx.dma("cLD", sy, out=ysr.ap().rearrange(
                    "p (r c t) -> p r c t", r=2, c=8)[:, rpar],
                    in_=dt["ys_sc"][b].rearrange("(c p) t -> p c t", p=128)[:, :, off:off + 256])
                ctx.dma("cLD", sy, out=gwf[:, rpar * 256:(rpar + 1) * 256],
                        in_=dt["gatew_sc"][:, toff:toff + 256])
            ldx = ctx.cnt["cLD"]
            @block.vector
            def _(v, ldx=ldx, rpar=rpar):
                ctx.wait(v, "cLD", ldx)
                ctx.wait(v, "PE", ctx.cnt["PE"])
                ctx.mark("DVE", v.tensor_copy(gwb[:, rpar * 256:(rpar + 1) * 256],
                                              gwf[:, rpar * 256:(rpar + 1) * 256]))
            ld_marks[tau] = (ldx, ctx.cnt["DVE"])

        def emit_hp_load(tau, gq):
            if tau >= 32:
                return
            rq = (tau * 4 + gq) % 3
            @block.sync
            def _(sy, tau=tau, gq=gq, rq=rq):
                if tau == 0 and gq == 0:
                    ctx.wait(sy, "GP", gp_b2)
                prev = hp_marks.get(("pe", tau * 4 + gq - 3), 0)
                ctx.wait(sy, "PE", prev)
                ctx.dma("cLD", sy, out=hpr.ap().rearrange(
                    "p (r e fc t) -> p r e fc t", r=3, e=4, fc=4)[:, rq],
                    in_=dt["hp_sc"][tau, 4 * gq:4 * gq + 4].rearrange(
                        "e (fc p) t -> p e fc t", p=128))
            hp_marks[(tau, gq)] = ctx.cnt["cLD"]

        def emit_mms(tau):
            rpar = tau % 2
            for gq in range(4):
                rq = (tau * 4 + gq) % 3
                @block.tensor
                def _(t, tau=tau, gq=gq, rq=rq, rpar=rpar):
                    ctx.wait(t, "cLD", hp_marks[(tau, gq)])
                    if tau == 0 and gq == 0:
                        ctx.wait(t, "GP", gp_b2)
                    if gq == 0:
                        ctx.wait(t, "DVE", max(evac_marks[0], evac_marks[1],
                                               ld_marks[tau][1]))
                    elif tau == 0:
                        pass
                    if tau == 0:
                        ctx.wait(t, "GP", w2m[(gq + 1) * 8 - 1])
                    for dout in range(8):
                        for el in range(4):
                            e = gq * 4 + el
                            for fc in range(4):
                                first = (gq == 0 and el == 0 and fc == 0
                                         and dout % 2 == 0)
                                inst = t.matmul(
                                    py[:, dout * 256:(dout + 1) * 256],
                                    w2a[:, ((e * 4 + fc) * 8 + dout) * 128:
                                        ((e * 4 + fc) * 8 + dout + 1) * 128],
                                    hpr[:, (rq * 16 + el * 4 + fc) * 256:
                                        (rq * 16 + el * 4 + fc + 1) * 256],
                                    start=first, stop=False)
                        if gq == 3:
                            inst = t.matmul(
                                py[:, dout * 256:(dout + 1) * 256],
                                b2b[:, dout * 128:(dout + 1) * 128],
                                gwb[:, rpar * 256:(rpar + 1) * 256],
                                start=False, stop=(dout % 2 == 1))
                    m = ctx.mark("PE", inst)
                    hp_marks[("pe", tau * 4 + gq)] = m
                g3 = tau * 4 + gq + 3
                emit_hp_load(g3 // 4, g3 % 4)

        def emit_evac(tau):
            rpar = tau % 2
            ro = rpar * 2048
            for dout in range(8):
                @block.vector
                def _(v, tau=tau, dout=dout, rpar=rpar, ro=ro):
                    if dout == 0:
                        ctx.wait(v, "PE", hp_marks[("pe", tau * 4 + 3)])
                        ctx.wait(v, "cST", st_marks[rpar])
                    ctx.wait(v, "DVE")
                    ctx.mark("DVE", v.tensor_tensor(
                        outs[:, ro + dout * 256:ro + (dout + 1) * 256],
                        py[:, dout * 256:(dout + 1) * 256],
                        ysr[:, ro + dout * 256:ro + (dout + 1) * 256], ALU.add))
                    ctx.wait(v, "DVE")
                    m1 = ctx.mark("DVE", v.tensor_tensor(
                        r2b[:, ro + dout * 256:ro + (dout + 1) * 256],
                        outs[:, ro + dout * 256:ro + (dout + 1) * 256],
                        xr[:, ro + dout * 256:ro + (dout + 1) * 256], ALU.add))
                    evac_marks[rpar] = m1

        def emit_ln(tau):
            rpar = tau % 2
            ro = rpar * 2048
            b = (tau * 256) // L
            off = (tau * 256) % L
            dve_r2 = evac_marks[rpar]
            for kc in range(8):
                qp = kc % 2
                @block.vector
                def _(v, kc=kc, qp=qp, ro=ro, dve_r2=dve_r2):
                    ctx.wait(v, "DVE")
                    ctx.wait(v, "PE", ctx.cnt["PE"])
                    ctx.mark("DVE", v.tensor_tensor(
                        r2q[:, qp * 256:(qp + 1) * 256],
                        r2b[:, ro + kc * 256:(kc + 1) * 256 + ro],
                        r2b[:, ro + kc * 256:(kc + 1) * 256 + ro], ALU.mult))
                @block.tensor
                def _(t, kc=kc, qp=qp, ro=ro):
                    ctx.wait(t, "DVE", ctx.cnt["DVE"])
                    t.matmul(pst[:, :], P["stat_a"][:, :],
                             r2b[:, ro + kc * 256:(kc + 1) * 256 + ro],
                             start=(kc == 0), stop=False)
                    ctx.mark("PE", t.matmul(pst[:, :], P["stat_b"][:, :],
                                            r2q[:, qp * 256:(qp + 1) * 256],
                                            start=False, stop=(kc == 7)))
            @block.vector
            def _(v):
                ctx.wait(v, "PE", ctx.cnt["PE"])
                ctx.wait(v, "ACT", ctx.cnt["ACT"])
                ctx.mark("DVE", v.tensor_scalar_mul(rm[:, :], pst[0:1, :],
                                                    float(1.0 / D)))
                ctx.mark("DVE", v.tensor_scalar_mul(rv[:, :], pst[32:33, :],
                                                    float(1.0 / D)))
                ctx.wait(v, "DVE")
                ctx.mark("DVE", v.tensor_tensor(rw[:, :], rm[:, :], rm[:, :], ALU.mult))
                ctx.wait(v, "DVE")
                ctx.mark("DVE", v.tensor_tensor(rv[:, :], rv[:, :], rw[:, :], ALU.subtract))
                ctx.wait(v, "DVE")
                ctx.mark("DVE", v.tensor_scalar_add(rv[:, :], rv[:, :], float(1e-5)))
            @block.scalar
            def _(s_):
                ctx.wait(s_, "DVE", ctx.cnt["DVE"])
                ctx.mark("ACT", s_.activation(rv[:, :], rv[:, :], AF.Sqrt))
            @block.vector
            def _(v):
                ctx.wait(v, "ACT", ctx.cnt["ACT"])
                ctx.mark("DVE", v.reciprocal(rv[:, :], rv[:, :]))
            @block.tensor
            def _(t):
                ctx.wait(t, "DVE", ctx.cnt["DVE"])
                ctx.wait(t, "ACT", ctx.cnt["ACT"])
                ctx.mark("PE", t.matmul(pbc[:, :], P["ones1x128f"][:, :], rm[:, :],
                                        start=True, stop=True))
            @block.scalar
            def _(s_):
                ctx.wait(s_, "PE", ctx.cnt["PE"])
                ctx.mark("ACT", s_.activation(m_b[:, :], pbc[:, :], AF.Copy))
            @block.tensor
            def _(t):
                ctx.wait(t, "ACT", ctx.cnt["ACT"])
                ctx.mark("PE", t.matmul(pbc[:, :], P["ones1x128f"][:, :], rv[:, :],
                                        start=True, stop=True))
            @block.scalar
            def _(s_):
                ctx.wait(s_, "PE", ctx.cnt["PE"])
                ctx.mark("ACT", s_.activation(rs_b[:, :], pbc[:, :], AF.Copy))
            for dout in range(8):
                @block.vector
                def _(v, dout=dout, ro=ro):
                    ctx.wait(v, "ACT", ctx.cnt["ACT"])
                    ctx.wait(v, "DVE")
                    ctx.mark("DVE", v.tensor_tensor(
                        t1t[:, :], r2b[:, ro + dout * 256:ro + (dout + 1) * 256],
                        m_b[:, :], ALU.subtract))
                    ctx.wait(v, "DVE")
                    ctx.mark("DVE", v.tensor_tensor(
                        outs[:, ro + dout * 256:ro + (dout + 1) * 256],
                        t1t[:, :], rs_b[:, :], ALU.mult))
            @block.sync
            def _(sy, b=b, off=off, ro=ro, rpar=rpar):
                ctx.wait(sy, "DVE", ctx.cnt["DVE"])
                st_marks[rpar] = ctx.dma("cST", sy,
                        out=dt["out_fm"][b].rearrange("(c p) t -> p c t", p=128)[:, :, off:off + 256],
                        in_=outs.ap().rearrange("p (r c t) -> p r c t", r=2, c=8)[:, rpar])

        emit_loads(0)
        for gq in range(3):
            emit_hp_load(0, gq)
        for tau in range(32):
            emit_mms(tau)
            emit_loads(tau + 1)
            emit_evac(tau)
            if tau >= 1:
                emit_ln(tau - 1)
        emit_ln(31)



def ppv(v, groups=None):
    v = np.asarray(v)
    if v.ndim == 1:
        return np.ascontiguousarray(v.reshape(-1, 128).T)
    return np.ascontiguousarray(v.reshape(v.shape[0], -1, 128).transpose(2, 0, 1))


def make_in_maps(inputs):
    c = make_consts()
    x = np.ascontiguousarray(
        np.asarray(inputs["L0_attn_x"]).transpose(0, 2, 1))
    base = {
        "W_in": np.ascontiguousarray(inputs["L0_attn_W_in"]),
        "bin_pp": ppv(inputs["L0_attn_b_in"]),
        "shortw_pp": np.ascontiguousarray(
            np.asarray(inputs["L0_attn_short_w"]).reshape(24, 128, 3).transpose(1, 0, 2)),
        "shortb_pp": ppv(inputs["L0_attn_short_b"]),
        "fW1": np.ascontiguousarray(inputs["L0_attn_W1"]),
        "fb1": np.ascontiguousarray(inputs["L0_attn_b1"]),
        "ff1": np.ascontiguousarray(inputs["L0_attn_f1"]),
        "fW2": np.ascontiguousarray(inputs["L0_attn_W2"]),
        "fb2": np.ascontiguousarray(inputs["L0_attn_b2"]),
        "ff2": np.ascontiguousarray(inputs["L0_attn_f2"]),
        "fW3": np.ascontiguousarray(inputs["L0_attn_W3"]),
        "fb3_pp": ppv(inputs["L0_attn_b3"]),
        "fbias_pp": ppv(inputs["L0_attn_filter_bias"]),
        "W_out": np.ascontiguousarray(inputs["L0_attn_W_out"]),
        "bout_pp": ppv(inputs["L0_attn_b_out"]),
        "gate_wT": np.ascontiguousarray(np.asarray(inputs["L0_ffn_gate_w"]).T),
        "expert_bias": np.ascontiguousarray(inputs["L0_ffn_expert_bias"]),
        "w1T": np.ascontiguousarray(np.asarray(inputs["L0_ffn_w1"]).transpose(0, 2, 1)),
        "b1e_pp": ppv(inputs["L0_ffn_b1"]),
        "w3T": np.ascontiguousarray(np.asarray(inputs["L0_ffn_w3"]).transpose(0, 2, 1)),
        "b3e_pp": ppv(inputs["L0_ffn_b3"]),
        "w2T": np.ascontiguousarray(np.asarray(inputs["L0_ffn_w2"]).transpose(0, 2, 1)),
        "b2e": np.ascontiguousarray(inputs["L0_ffn_b2"]),
        "sw1T": np.ascontiguousarray(np.asarray(inputs["L0_ffn_sw1"]).T),
        "sb1_pp": ppv(inputs["L0_ffn_sb1"]),
        "sw3T": np.ascontiguousarray(np.asarray(inputs["L0_ffn_sw3"]).T),
        "sb3_pp": ppv(inputs["L0_ffn_sb3"]),
        "sw2T": np.ascontiguousarray(np.asarray(inputs["L0_ffn_sw2"]).T),
        "sb2_pp": ppv(inputs["L0_ffn_sb2"]),
        "dneg_pp": ppv(c["deltas_neg"][:, 0]),
    }
    c.pop("deltas_neg", None)
    base.update(c)
    maps = []
    for core in range(8):
        m = dict(base)
        m["xfm"] = np.ascontiguousarray(x[core * BPC:(core + 1) * BPC])
        maps.append(m)
    return maps


def kernel(**inputs):
    nc = build_program()
    maps = make_in_maps(inputs)
    res = run_bass_kernel_spmd(nc, maps, core_ids=list(range(8)))
    outs = [np.ascontiguousarray(r["out_fm"].transpose(0, 2, 1))
            for r in res.results]
    return np.concatenate(outs, 0).astype(np.float32)


# version 03
# speedup vs baseline: 1.2739x; 1.0093x over previous
import math
import numpy as np
import ml_dtypes

import concourse.bass as bass
import concourse.mybir as mybir
from concourse.bass_utils import run_bass_kernel_spmd

F32 = mybir.dt.float32
BF16 = mybir.dt.bfloat16
I32 = mybir.dt.int32
AF = mybir.ActivationFunctionType
ALU = mybir.AluOpType

L = 2048
D = 1024
M3 = 3 * D
BPC = 4
TT = 8192
NB = ml_dtypes.bfloat16
TWO_PI = float(2.0 * np.pi)



def make_consts():
    c = {}
    t1 = np.arange(64)
    k1 = np.arange(128)
    t2 = np.arange(32)
    k2 = np.arange(32)
    Tq = np.arange(64)

    w128 = np.exp(-2j * np.pi * np.outer(t1, k1) / 128.0)
    tw = np.exp(-2j * np.pi * np.outer(t2, k1) / 4096.0)
    d1 = np.stack([w128[None, :, :] * tw[:, None, :]][0], 0)
    c["d1tw_re"] = d1.real.astype(NB)
    c["d1tw_im"] = d1.imag.astype(NB)
    c["d1tw_imn"] = (-d1.imag).astype(NB)

    W2c = np.exp(-2j * np.pi * np.outer(t2, k2) / 32.0)
    D2BD = np.zeros((128, 128), np.complex128)
    for cs in range(4):
        D2BD[cs * 32:(cs + 1) * 32, cs * 32:(cs + 1) * 32] = W2c
    c["d2_re"] = D2BD.real.astype(NB)
    c["d2_im"] = D2BD.imag.astype(NB)
    c["d2_imn"] = (-D2BD.imag).astype(NB)

    W2i = np.exp(+2j * np.pi * np.outer(k2, t2) / 32.0)
    D2i = np.zeros((128, 128), np.complex128)
    for cs in range(4):
        D2i[cs * 32:(cs + 1) * 32, cs * 32:(cs + 1) * 32] = W2i
    c["d2i_re"] = D2i.real.astype(NB)
    c["d2i_im"] = D2i.imag.astype(NB)
    c["d2i_imn"] = (-D2i.imag).astype(NB)

    TWI = np.exp(+2j * np.pi * np.outer(k1, t2) / 4096.0) / 4096.0
    TWI4 = np.repeat(TWI[:, None, :], 4, axis=1).reshape(128, 128)
    c["twi_re"] = TWI4.real.astype(np.float32)
    c["twi_im"] = TWI4.imag.astype(np.float32)

    D1i = np.exp(+2j * np.pi * np.outer(k1, Tq) / 128.0)
    c["d1i_re"] = D1i.real.astype(NB)
    c["d1i_im"] = D1i.imag.astype(NB)
    c["d1i_imn"] = (-D1i.imag).astype(NB)

    c["ident128"] = np.eye(128, dtype=NB)
    c["ident64"] = np.eye(64, dtype=NB)
    c["ones1x128"] = np.ones((1, 128), dtype=NB)
    c["ones1x128f"] = np.ones((1, 128), dtype=np.float32)
    sa = np.zeros((128, 64), np.float32); sa[:, 0] = 1.0
    sbv = np.zeros((128, 64), np.float32); sbv[:, 32] = 1.0
    c["stat_a"] = sa.astype(NB)
    c["stat_b"] = sbv.astype(NB)
    c["identf128"] = np.eye(128, dtype=np.float32)

    tau = np.arange(L)
    trev = ((L - 1 - tau) / (L - 1.0)).astype(np.float32)
    ang = 2.0 * np.pi * (L - 1 - tau) / L
    phase = 0.0001 * ang
    c["pe_rev"] = np.stack([trev, np.cos(phase).astype(np.float32),
                            (-np.sin(phase)).astype(np.float32)], 0).astype(np.float32)
    dmax, dmin = (math.log(0.01) / 0.3, math.log(0.01) / 1.5)
    c["deltas_neg"] = (-np.abs(np.linspace(dmax, dmin, D))).astype(np.float32)[:, None]
    c["trev_row"] = trev[None, :]
    return c



class Ctx:

    ENGS = ("PE", "ACT", "DVE", "GP", "SP", "BAR")
    DMAS = ("cLD", "cST", "cGL", "cGS")

    def __init__(self, nc, stack):
        self.nc = nc
        self.sem = {}
        self.cnt = {}
        for n in self.ENGS + self.DMAS:
            self.sem[n] = stack.enter_context(nc.semaphore("sem_" + n))
            self.cnt[n] = 0

    def mark(self, name, inst, n=1):
        step = 16 if name in self.DMAS else 1
        inst.then_inc(self.sem[name], n * step)
        self.cnt[name] += n
        return self.cnt[name]

    def dma(self, name, eng, out, in_):
        self.wait(eng, name)
        inst = eng.dma_start(out=out, in_=in_)
        return self.mark(name, inst)

    def wait(self, eng, name, count=None):
        step = 16 if name in self.DMAS else 1
        if count is None:
            count = self.cnt[name]
        if count > 0:
            eng.wait_ge(self.sem[name], count * step)

    def barrier(self, block):
        @block.sync
        def _(sy):
            for d in self.DMAS:
                self.wait(sy, d)
        @block.tensor
        def _(t):
            self.mark("BAR", t.nop())
        @block.scalar
        def _(s):
            self.mark("BAR", s.nop())
        @block.vector
        def _(v):
            self.mark("BAR", v.nop())
        @block.gpsimd
        def _(g):
            self.mark("BAR", g.nop())
        @block.sync
        def _(sy):
            self.mark("BAR", sy.nop())
        tgt = self.cnt["BAR"]
        @block.tensor
        def _(t):
            self.wait(t, "BAR", tgt)
        @block.scalar
        def _(s):
            self.wait(s, "BAR", tgt)
        @block.vector
        def _(v):
            self.wait(v, "BAR", tgt)
        @block.gpsimd
        def _(g):
            self.wait(g, "BAR", tgt)
        @block.sync
        def _(sy):
            self.wait(sy, "BAR", tgt)



def build_program(dbg=(), upto=99):
    import contextlib
    nc = bass.Bass("TRN2", target_bir_lowering=False, debug=False)

    dt = {}

    def din(name, shape, dtype=F32):
        dt[name] = nc.dram_tensor(name, list(shape), dtype, kind="ExternalInput").ap()
        return dt[name]

    def dsc(name, shape, dtype):
        kind = "ExternalOutput" if name in dbg else "Internal"
        dt[name] = nc.dram_tensor(name, list(shape), dtype, kind=kind).ap()
        return dt[name]

    din("xfm", (BPC, D, L))
    din("W_in", (D, M3)); din("bin_pp", (128, 24))
    din("shortw_pp", (128, 24, 3)); din("shortb_pp", (128, 24))
    din("fW1", (3, 64)); din("fb1", (64,)); din("ff1", (64,))
    din("fW2", (64, 64)); din("fb2", (64,)); din("ff2", (64,))
    din("fW3", (64, D)); din("fb3_pp", (128, 8)); din("fbias_pp", (128, 8))
    din("W_out", (D, D)); din("bout_pp", (128, 8))
    din("gate_wT", (D, 16)); din("expert_bias", (16,))
    din("w1T", (16, D, 512)); din("b1e_pp", (128, 16, 4))
    din("w3T", (16, D, 512)); din("b3e_pp", (128, 16, 4))
    din("w2T", (16, 512, D)); din("b2e", (16, D))
    din("sw1T", (D, D)); din("sb1_pp", (128, 8))
    din("sw3T", (D, D)); din("sb3_pp", (128, 8))
    din("sw2T", (D, D)); din("sb2_pp", (128, 8))
    din("dneg_pp", (128, 8))
    for n in ("d1tw_re", "d1tw_im", "d1tw_imn"):
        din(n, (32, 64, 128), BF16)
    for n in ("d2_re", "d2_im", "d2_imn", "d2i_re", "d2i_im", "d2i_imn",
              "ident128"):
        din(n, (128, 128), BF16)
    din("twi_re", (128, 128)); din("twi_im", (128, 128))
    for n in ("d1i_re", "d1i_im", "d1i_imn"):
        din(n, (128, 64), BF16)
    din("ident64", (64, 64), BF16)
    din("ones1x128", (1, 128), BF16)
    din("ones1x128f", (1, 128))
    din("stat_a", (128, 64), BF16); din("stat_b", (128, 64), BF16)
    din("identf128", (128, 128))
    din("pe_rev", (3, L)); din("trev_row", (1, L))

    dsc("gf_sc", (8, 32, 2, 128, 128), BF16)
    dsc("v_sc", (BPC, D, L), BF16)
    dsc("x0_sc", (BPC, D, L), BF16)
    dsc("yx0_sc", (BPC, D, L), BF16)
    dsc("xP_sc", (BPC, D, L), F32)
    dsc("xPb_sc", (BPC, D, L), BF16)
    dsc("gatew_sc", (16, TT), F32)
    dsc("ys_sc", (BPC, D, L), BF16)
    dsc("hp_sc", (32, 16, 512, 256), BF16)
    out = nc.dram_tensor("out_fm", [BPC, D, L], F32, kind="ExternalOutput").ap()
    dt["out_fm"] = out

    with contextlib.ExitStack() as stack:
        ctx = Ctx(nc, stack)
        sb = lambda name, shape, dtype: stack.enter_context(
            nc.sbuf_tensor(name, list(shape), dtype))
        ps = lambda name, shape, dtype=F32: stack.enter_context(
            nc.psum_tensor(name, list(shape), dtype))
        block = stack.enter_context(nc.Block())

        P = {}
        for n in ("d2_re", "d2_im", "d2_imn", "d2i_re", "d2i_im", "d2i_imn",
                  "ident128"):
            P[n] = sb("c_" + n, (128, 128), BF16)
        P["twi_re"] = sb("c_twire", (128, 128), F32)
        P["twi_im"] = sb("c_twiim", (128, 128), F32)
        for n in ("d1i_re", "d1i_im", "d1i_imn"):
            P[n] = sb("c_" + n, (128, 64), BF16)
        P["ident64"] = sb("c_i64", (64, 64), BF16)
        P["ones1x128"] = sb("c_ones", (1, 128), BF16)
        P["ones1x128f"] = sb("c_onesf", (1, 128), F32)
        P["stat_a"] = sb("c_stata", (128, 64), BF16)
        P["stat_b"] = sb("c_statb", (128, 64), BF16)
        P["identf128"] = sb("c_if128", (128, 128), F32)

        @block.sync
        def _(sy):
            for n in ("d2_re", "d2_im", "d2_imn", "d2i_re", "d2i_im",
                      "d2i_imn", "ident128", "twi_re", "twi_im", "d1i_re",
                      "d1i_im", "d1i_imn", "ident64", "ones1x128", "ones1x128f",
                      "stat_a", "stat_b", "identf128"):
                ctx.dma("cLD", sy, out=P[n][:, :], in_=dt[n])

        ld_consts = ctx.cnt["cLD"]

        phases = [phase0_w(ld_consts), phase1, phase2, phase3, phase4, phase5]
        for i, ph in enumerate(phases):
            if i > upto:
                break
            ph(nc, ctx, block, dt, P, sb, ps)
            if i < upto and i < len(phases) - 1:
                ctx.barrier(block)

        @block.sync
        def _(sy):
            ctx.wait(sy, "cST")
            ctx.wait(sy, "cGS")

    return nc



def phase0_w(ld_consts):
    def f(nc, ctx, block, dt, P, sb, ps):
        return phase0(nc, ctx, block, dt, P, sb, ps, ld_consts)
    return f


def phase0(nc, ctx, block, dt, P, sb, ps, ld_consts):
    import contextlib
    with contextlib.ExitStack() as st:
        sbl = lambda n, s, d: st.enter_context(nc.sbuf_tensor(n, list(s), d))
        psl = lambda n, s, d=F32: st.enter_context(nc.psum_tensor(n, list(s), d))

        pe_sb = sbl("pe_sb", (3, L), F32)
        fW1 = sbl("fW1s", (3, 64), F32)
        fW2 = sbl("fW2s", (64, 64), F32)
        fW3 = sbl("fW3s", (64, D), F32)
        vecs = sbl("fvecs", (64, 6), F32)
        b3pp = sbl("b3pp", (128, 8), F32)
        dneg = sbl("dneg", (128, 8), F32)
        h_sb = sbl("h_sb", (64, L), F32)
        zi = sbl("zi", (64, L), I32)
        zf = sbl("zf", (64, L), F32)
        trow = sbl("trow", (1, L), F32)
        E_sb = sbl("E_sb", (128, L), F32)
        g_sb = sbl("g_sb", (128, L), BF16)
        kpre = sbl("kpre", (128, L), F32)
        for _n in ("d1tw_re", "d1tw_im", "d1tw_imn"):
            P["_" + _n] = sbl("p0" + _n, (64, 32 * 128), BF16)
        U_re = sbl("p0_ure", (64, 32 * 128), BF16)
        S2_re = sbl("p0_s2re", (128, 32 * 128), BF16)
        S2_im = sbl("p0_s2im", (128, 32 * 128), BF16)
        M2_re = sbl("p0_m2re", (128, 32 * 128), BF16)
        M2_im = sbl("p0_m2im", (128, 32 * 128), BF16)
        G_re = sbl("p0_gre", (128, 32 * 128), BF16)
        G_im = sbl("p0_gim", (128, 32 * 128), BF16)

        pmm = psl("p0_mm", (128, 512))
        pT = psl("p0_T", (128, 512))
        pS = psl("p0_S", (128, 512))
        pS2 = psl("p0_S2", (128, 512))
        pX = psl("p0_X", (128, 512))
        pX2 = psl("p0_X2", (128, 512))

        @block.sync
        def _(sy):
            ctx.dma("cLD", sy, out=pe_sb[:, :], in_=dt["pe_rev"])
            ctx.dma("cLD", sy, out=fW1[:, :], in_=dt["fW1"])
            ctx.dma("cLD", sy, out=fW2[:, :], in_=dt["fW2"])
            ctx.dma("cLD", sy, out=fW3[:, :], in_=dt["fW3"])
            ctx.dma("cLD", sy, out=vecs[:, 0:1], in_=dt["fb1"][:, None])
            ctx.dma("cLD", sy, out=vecs[:, 1:2], in_=dt["ff1"][:, None])
            ctx.dma("cLD", sy, out=vecs[:, 2:3], in_=dt["fb2"][:, None])
            ctx.dma("cLD", sy, out=vecs[:, 3:4], in_=dt["ff2"][:, None])
            ctx.dma("cLD", sy, out=b3pp[:, :], in_=dt["fb3_pp"])
            ctx.dma("cLD", sy, out=dneg[:, :], in_=dt["dneg_pp"])
            ctx.dma("cLD", sy, out=trow[:, :], in_=dt["trev_row"])
            for _n in ("d1tw_re", "d1tw_im", "d1tw_imn"):
                ctx.dma("cLD", sy, out=P["_" + _n].ap().rearrange(
                    "p (t2 k) -> p t2 k", t2=32),
                    in_=dt[_n].rearrange("t2 p k -> p t2 k"))
        ld0 = ctx.cnt["cLD"]

        @block.vector
        def _(v):
            ctx.wait(v, "cLD", ld0)
            v.tensor_tensor(vecs[:, 4:5], vecs[:, 0:1], vecs[:, 1:2], ALU.mult)
            ctx.mark("DVE", v.tensor_tensor(vecs[:, 5:6], vecs[:, 2:3],
                                            vecs[:, 3:4], ALU.mult))
        prod_done = ctx.cnt["DVE"]

        def filt_layer(Wsb, K, scale_col, bias_col, src, dst):
            for tt in range(4):
                sl = slice(tt * 512, (tt + 1) * 512)
                @block.tensor
                def _(t, sl=sl, tt=tt):
                    ctx.wait(t, "DVE", ctx.cnt["DVE"])
                    ctx.wait(t, "ACT", ctx.cnt["ACT"])
                    ctx.mark("PE", t.matmul(pmm[0:64, 0:512], Wsb[0:K, :],
                                            src[0:K, sl], start=True, stop=True))
                @block.vector
                def _(v, sl=sl):
                    ctx.wait(v, "PE", ctx.cnt["PE"])
                    ctx.wait(v, "DVE", prod_done)
                    ctx.mark("DVE", v.tensor_scalar(
                        dst[:, sl], pmm[0:64, 0:512],
                        vecs[:, scale_col:scale_col + 1],
                        vecs[:, bias_col:bias_col + 1], ALU.mult, ALU.add))
            @block.vector
            def _(v):
                ctx.wait(v, "DVE")
                ctx.mark("DVE", v.tensor_scalar_mul(zf[:, :], dst[:, :],
                                                    float(1.0 / TWO_PI)))
                ctx.wait(v, "DVE")
                ctx.mark("DVE", v.tensor_copy(zi[:, :], zf[:, :]))
                ctx.wait(v, "DVE")
                ctx.mark("DVE", v.tensor_copy(zf[:, :], zi[:, :]))
                ctx.wait(v, "DVE")
                ctx.mark("DVE", v.scalar_tensor_tensor(
                    dst[:, :], zf[:, :], -TWO_PI, dst[:, :], ALU.mult, ALU.add))
                ctx.wait(v, "DVE")
                ctx.mark("DVE", v.tensor_single_scalar(zf[:, :], dst[:, :],
                                                       float(np.pi), ALU.is_gt))
                ctx.wait(v, "DVE")
                ctx.mark("DVE", v.scalar_tensor_tensor(
                    dst[:, :], zf[:, :], -TWO_PI, dst[:, :], ALU.mult, ALU.add))
                ctx.wait(v, "DVE")
                ctx.mark("DVE", v.tensor_single_scalar(zf[:, :], dst[:, :],
                                                       float(-np.pi), ALU.is_lt))
                ctx.wait(v, "DVE")
                ctx.mark("DVE", v.scalar_tensor_tensor(
                    dst[:, :], zf[:, :], TWO_PI, dst[:, :], ALU.mult, ALU.add))
            @block.scalar
            def _(s):
                ctx.wait(s, "DVE", ctx.cnt["DVE"])
                ctx.mark("ACT", s.activation(dst[:, :], dst[:, :], AF.Sin))

        h1 = sbl("h1_sb", (64, L), F32)
        filt_layer(fW1, 3, 1, 4, pe_sb, h1)
        filt_layer(fW2, 64, 3, 5, h1, h_sb)

        for dc in range(8):
            csl = slice(dc * 128, (dc + 1) * 128)
            for tt in range(4):
                sl = slice(tt * 512, (tt + 1) * 512)
                @block.tensor
                def _(t, sl=sl, tt=tt, dc=dc):
                    ctx.wait(t, "ACT", ctx.cnt["ACT"])
                    ctx.mark("PE", t.matmul(pT[:, 0:512], P["ones1x128f"][:, :],
                                            trow[:, sl], start=True, stop=True))
                @block.scalar
                def _(s, sl=sl, dc=dc):
                    ctx.wait(s, "PE", ctx.cnt["PE"])
                    ctx.mark("ACT", s.activation(E_sb[:, sl], pT[:, 0:512],
                                                 AF.Exp, scale=dneg[:, dc:dc + 1]))
            for tt in range(4):
                sl = slice(tt * 512, (tt + 1) * 512)
                @block.tensor
                def _(t, sl=sl, dc=dc, tt=tt):
                    ctx.wait(t, "DVE", ctx.cnt["DVE"])
                    ctx.mark("PE", t.matmul(pS[:, 0:512], fW3[:, dc * 128:(dc + 1) * 128],
                                            h_sb[:, sl], start=True, stop=True))
                @block.vector
                def _(v, sl=sl, dc=dc):
                    ctx.wait(v, "PE", ctx.cnt["PE"])
                    ctx.wait(v, "ACT", ctx.cnt["ACT"])
                    ctx.mark("DVE", v.tensor_scalar(kpre[:, sl], pS[:, 0:512],
                                                    b3pp[:, dc:dc + 1], None, ALU.add))
                    ctx.wait(v, "DVE")
                    ctx.mark("DVE", v.tensor_tensor(g_sb[:, sl], kpre[:, sl],
                                                    E_sb[:, sl], ALU.mult))
            fwd_fft(nc, ctx, block, P, g_sb, None, U_re, None, S2_re, S2_im,
                    M2_re, M2_im, G_re, G_im, pT, pS, pS2, pX, pX2)
            @block.sync
            def _(sy, dc=dc):
                ctx.wait(sy, "DVE", ctx.cnt["DVE"])
                ctx.wait(sy, "ACT", ctx.cnt["ACT"])
                ctx.dma("cST", sy, out=dt["gf_sc"][dc, :, 0, :, :].rearrange("j p k -> p j k"),
                    in_=G_re.ap().rearrange("p (j k) -> p j k", j=32))
                ctx.dma("cST", sy, out=dt["gf_sc"][dc, :, 1, :, :].rearrange("j p k -> p j k"),
                    in_=G_im.ap().rearrange("p (j k) -> p j k", j=32))
            @block.vector
            def _(v):
                ctx.wait(v, "cST", ctx.cnt["cST"])
            @block.scalar
            def _(s):
                ctx.wait(s, "cST", ctx.cnt["cST"])


def fwd_fft(nc, ctx, block, P, vre_sb, vim_sb, U_re, U_im, S2_re, S2_im,
            M2_re, M2_im, X_re, X_im, pT, pS, pS2, pX, pX2):
    cplx = vim_sb is not None
    planes = ((vre_sb, U_re), (vim_sb, U_im)) if cplx else ((vre_sb, U_re),)

    per_grp = 4 // len(planes)
    ngrp = 32 // per_grp
    for grp in range(ngrp):
        @block.tensor
        def _(t, grp=grp):
            ctx.wait(t, "DVE", ctx.cnt["DVE"])
            ctx.wait(t, "ACT", ctx.cnt["ACT"])
            for q in range(per_grp):
                t2 = grp * per_grp + q
                for pi, (vsb, _u) in enumerate(planes):
                    src = vsb.ap().rearrange("p (t1 t2) -> p t2 t1", t2=32)[:, t2, 0:64]
                    off = (q * len(planes) + pi) * 128
                    ctx.mark("PE", t.matmul(pT[0:64, off:off + 128], src,
                                            P["ident128"][:, :], start=True, stop=True))
        @block.vector
        def _(v, grp=grp):
            ctx.wait(v, "PE", ctx.cnt["PE"])
            for q in range(per_grp):
                t2 = grp * per_grp + q
                for pi, (_v, usb) in enumerate(planes):
                    off = (q * len(planes) + pi) * 128
                    ctx.mark("DVE", v.tensor_copy(
                        usb[0:64, t2 * 128:(t2 + 1) * 128], pT[0:64, off:off + 128]))

    for grp in range(8):
        @block.tensor
        def _(t, grp=grp):
            ctx.wait(t, "DVE", ctx.cnt["DVE"])
            ctx.wait(t, "ACT", ctx.cnt["ACT"])
            for q in range(4):
                t2 = grp * 4 + q
                d_re = P["_d1tw_re"][:, t2 * 128:(t2 + 1) * 128]
                d_im = P["_d1tw_im"][:, t2 * 128:(t2 + 1) * 128]
                d_imn = P["_d1tw_imn"][:, t2 * 128:(t2 + 1) * 128]
                ure = U_re[0:64, t2 * 128:(t2 + 1) * 128]
                re_ps = pS[:, q * 128:(q + 1) * 128]
                im_ps = pS2[:, q * 128:(q + 1) * 128]
                if cplx:
                    uim = U_im[0:64, t2 * 128:(t2 + 1) * 128]
                    t.matmul(re_ps, d_re, ure, start=True, stop=False)
                    t.matmul(re_ps, d_imn, uim, start=False, stop=True)
                    t.matmul(im_ps, d_im, ure, start=True, stop=False)
                    ctx.mark("PE", t.matmul(im_ps, d_re, uim, start=False, stop=True))
                else:
                    t.matmul(re_ps, d_re, ure, start=True, stop=True)
                    ctx.mark("PE", t.matmul(im_ps, d_im, ure, start=True, stop=True))
        pe_c = ctx.cnt["PE"]
        @block.scalar
        def _(s, grp=grp, pe_c=pe_c):
            ctx.wait(s, "PE", pe_c)
            for q in range(4):
                t2 = grp * 4 + q
                dst = S2_re.ap().rearrange("p (c t2) -> p c t2", t2=32)[:, :, t2]
                ctx.mark("ACT", s.activation(dst, pS[:, q * 128:(q + 1) * 128], AF.Copy))
        @block.vector
        def _(v, grp=grp, pe_c=pe_c):
            ctx.wait(v, "PE", pe_c)
            for q in range(4):
                t2 = grp * 4 + q
                dst = S2_im.ap().rearrange("p (c t2) -> p c t2", t2=32)[:, :, t2]
                ctx.mark("DVE", v.tensor_copy(dst, pS2[:, q * 128:(q + 1) * 128]))

    for grp in range(8):
        @block.tensor
        def _(t, grp=grp):
            ctx.wait(t, "ACT", ctx.cnt["ACT"])
            ctx.wait(t, "DVE", ctx.cnt["DVE"])
            for q in range(2):
                j2 = grp * 4 + q * 2
                for jj in range(2):
                    sl = slice((j2 + jj) * 128, (j2 + jj + 1) * 128)
                    t.matmul(pT[:, (q * 2 + jj) * 128:(q * 2 + jj + 1) * 128],
                             S2_re.ap()[:, sl], P["ident128"][:, :],
                             start=True, stop=True)
                    ctx.mark("PE", t.matmul(
                        pS[:, (q * 2 + jj) * 128:(q * 2 + jj + 1) * 128],
                        S2_im.ap()[:, sl], P["ident128"][:, :],
                        start=True, stop=True))
        pe_c = ctx.cnt["PE"]
        @block.scalar
        def _(s, grp=grp, pe_c=pe_c):
            ctx.wait(s, "PE", pe_c)
            ctx.mark("ACT", s.activation(M2_re[:, grp * 512:(grp + 1) * 512],
                                         pT[:, 0:512], AF.Copy))
        @block.vector
        def _(v, grp=grp, pe_c=pe_c):
            ctx.wait(v, "PE", pe_c)
            ctx.mark("DVE", v.tensor_copy(M2_im[:, grp * 512:(grp + 1) * 512],
                                          pS[:, 0:512]))

    for grp in range(8):
        @block.tensor
        def _(t, grp=grp):
            ctx.wait(t, "ACT", ctx.cnt["ACT"])
            ctx.wait(t, "DVE", ctx.cnt["DVE"])
            sl = slice(grp * 512, (grp + 1) * 512)
            mre = M2_re.ap()[:, sl]
            mim = M2_im.ap()[:, sl]
            t.matmul(pX[:, 0:512], P["d2_re"][:, :], mre, start=True, stop=False)
            t.matmul(pX[:, 0:512], P["d2_imn"][:, :], mim, start=False, stop=True)
            t.matmul(pX2[:, 0:512], P["d2_im"][:, :], mre, start=True, stop=False)
            ctx.mark("PE", t.matmul(pX2[:, 0:512], P["d2_re"][:, :], mim,
                                    start=False, stop=True))
        pe_c = ctx.cnt["PE"]
        @block.scalar
        def _(s, grp=grp, pe_c=pe_c):
            ctx.wait(s, "PE", pe_c)
            ctx.mark("ACT", s.activation(X_re[:, grp * 512:(grp + 1) * 512],
                                         pX[:, 0:512], AF.Copy))
        @block.vector
        def _(v, grp=grp, pe_c=pe_c):
            ctx.wait(v, "PE", pe_c)
            ctx.mark("DVE", v.tensor_copy(X_im[:, grp * 512:(grp + 1) * 512],
                                          pX2[:, 0:512]))


def phase1(nc, ctx, block, dt, P, sb, ps):
    import contextlib
    with contextlib.ExitStack() as st:
        sbl = lambda n, s_, d: st.enter_context(nc.sbuf_tensor(n, list(s_), d))
        psl = lambda n, s_, d=F32: st.enter_context(nc.psum_tensor(n, list(s_), d))

        winb = sbl("winb", (128, 8 * M3), BF16)
        binp = sbl("binp", (128, 24), F32)
        swp = sbl("swp", (128, 24 * 3), F32)
        sbp = sbl("sbp", (128, 24), F32)
        xb = sbl("xb", (128, 8 * L), BF16)
        u_sb = sbl("u_sb", (128, 2 * 2052), BF16)
        acc = sbl("acc_sb", (128, 2 * L), F32)
        x1h = sbl("x1h", (128, L), F32)
        stg = sbl("stg", (128, 2 * L), BF16)
        pu = [psl("p1_u0", (128, 512)), psl("p1_u1", (128, 512))]

        stg1 = st.enter_context(nc.sbuf_tensor("p1_cstg", [128, 2 * 2048], F32))
        @block.sync
        def _(sy):
            ctx.dma("cLD", sy, out=binp[:, :], in_=dt["bin_pp"])
            ctx.dma("cLD", sy, out=swp.ap().rearrange("p (c s) -> p c s", s=3), in_=dt["shortw_pp"])
            ctx.dma("cLD", sy, out=sbp[:, :], in_=dt["shortb_pp"])
        win_src = dt["W_in"].rearrange("(kc p) m -> p kc m", p=128)
        cl_marks = []
        for i in range(16):
            kc, h = i // 2, i % 2
            par = i % 2
            @block.sync
            def _(sy, i=i, kc=kc, h=h, par=par):
                if i >= 2:
                    ctx.wait(sy, "ACT", cl_marks[i - 2])
                ctx.dma("cLD", sy, out=stg1[:, par * 2048:par * 2048 + 1536],
                        in_=win_src[:, kc, h * 1536:(h + 1) * 1536])
            ldi = ctx.cnt["cLD"]
            @block.scalar
            def _(s_, i=i, kc=kc, h=h, par=par, ldi=ldi):
                ctx.wait(s_, "cLD", ldi)
                cl_marks.append(ctx.mark("ACT", s_.activation(
                    winb[:, kc * M3 + h * 1536:kc * M3 + (h + 1) * 1536],
                    stg1[:, par * 2048:par * 2048 + 1536], AF.Copy)))
        ld_w = (ctx.cnt["cLD"], ctx.cnt["ACT"])

        for b in range(BPC):
            x_src = dt["xfm"][b].rearrange("(kc p) t -> p kc t", p=128)
            pe_before = ctx.cnt["PE"]
            xm = []
            for i in range(8):
                par = i % 2
                @block.sync
                def _(sy, i=i, par=par, b=b, pe_before=pe_before):
                    if i >= 2:
                        ctx.wait(sy, "ACT", xm[i - 2])
                    else:
                        ctx.wait(sy, "ACT", ctx.cnt["ACT"])
                    if i == 0:
                        ctx.wait(sy, "PE", pe_before)
                    ctx.dma("cLD", sy, out=stg1[:, par * 2048:(par + 1) * 2048],
                            in_=x_src[:, i, :])
                ldi = ctx.cnt["cLD"]
                @block.scalar
                def _(s_, i=i, par=par, ldi=ldi, pe_before=pe_before):
                    ctx.wait(s_, "cLD", ldi)
                    if i == 0:
                        ctx.wait(s_, "PE", pe_before)
                    xm.append(ctx.mark("ACT", s_.activation(
                        xb[:, i * L:(i + 1) * L],
                        stg1[:, par * 2048:(par + 1) * 2048], AF.Copy)))
            xld = ctx.cnt["ACT"]

            order = list(range(8)) + [m for c in range(8) for m in (8 + c, 16 + c)]
            for oi, mc in enumerate(order):
                ub = (oi % 2)
                uof = ub * 2052
                aof = ub * L
                @block.vector
                def _(v, mc=mc, ub=ub, uof=uof, b=b, oi=oi):
                    ctx.wait(v, "DVE", ctx.cnt["DVE"])
                    ctx.mark("DVE", v.memset(u_sb[:, uof:uof + 2], 0.0))
                for tt in range(4):
                    pq = pu[tt % 2]
                    @block.tensor
                    def _(t, mc=mc, tt=tt, pq=pq, xld=xld, oi=oi, b=b):
                        if oi == 0 and tt == 0:
                            ctx.wait(t, "ACT", ld_w[1])
                            ctx.wait(t, "cLD", ld_w[0])
                            ctx.wait(t, "ACT", xld)
                        ctx.wait(t, "ACT", ctx.cnt["ACT"])
                        for kc in range(8):
                            inst = t.matmul(
                                pq[:, :], winb[:, (kc * 24 + mc) * 128:(kc * 24 + mc + 1) * 128],
                                xb[:, kc * L + tt * 512:kc * L + (tt + 1) * 512],
                                start=(kc == 0), stop=(kc == 7))
                        ctx.mark("PE", inst)
                    @block.scalar
                    def _(s_, mc=mc, tt=tt, pq=pq, uof=uof):
                        ctx.wait(s_, "PE", ctx.cnt["PE"])
                        ctx.wait(s_, "DVE", ctx.cnt["DVE"])
                        ctx.wait(s_, "cLD", ld_w[0])
                        ctx.mark("ACT", s_.activation(
                            u_sb[:, uof + 2 + tt * 512:uof + 2 + (tt + 1) * 512],
                            pq[:, :], AF.Copy))
                @block.vector
                def _(v, mc=mc, uof=uof, aof=aof):
                    ctx.wait(v, "ACT", ctx.cnt["ACT"])
                    ctx.wait(v, "DVE", ctx.cnt["DVE"])
                    ctx.mark("DVE", v.tensor_scalar(
                        u_sb[:, uof + 2:uof + 2050], u_sb[:, uof + 2:uof + 2050],
                        binp[:, mc:mc + 1], None, ALU.add))
                    ctx.wait(v, "DVE")
                    ctx.mark("DVE", v.tensor_scalar(
                        acc[:, aof:aof + L], u_sb[:, uof + 2:uof + 2050],
                        swp[:, mc * 3 + 2:mc * 3 + 3], sbp[:, mc:mc + 1],
                        ALU.mult, ALU.add))
                    ctx.wait(v, "DVE")
                    ctx.mark("DVE", v.scalar_tensor_tensor(
                        acc[:, aof:aof + L], u_sb[:, uof + 1:uof + 2049],
                        swp[:, mc * 3 + 1:mc * 3 + 2], acc[:, aof:aof + L],
                        ALU.mult, ALU.add))
                    ctx.wait(v, "DVE")
                    ctx.mark("DVE", v.scalar_tensor_tensor(
                        acc[:, aof:aof + L], u_sb[:, uof:uof + 2048],
                        swp[:, mc * 3 + 0:mc * 3 + 1], acc[:, aof:aof + L],
                        ALU.mult, ALU.add))
                dve_acc = ctx.cnt["DVE"]
                sof = (oi % 2) * L
                if mc < 8:
                    @block.scalar
                    def _(s_, aof=aof, sof=sof):
                        ctx.wait(s_, "DVE", dve_acc)
                        ctx.wait(s_, "cST", ctx.cnt["cST"])
                        ctx.mark("ACT", s_.activation(stg[:, sof:sof + L],
                                                      acc[:, aof:aof + L], AF.Copy))
                    @block.sync
                    def _(sy, b=b, mc=mc, sof=sof):
                        ctx.wait(sy, "ACT", ctx.cnt["ACT"])
                        ctx.dma("cST", sy, out=dt["x0_sc"][b, mc * 128:(mc + 1) * 128, :],
                            in_=stg[:, sof:sof + L])
                elif mc < 16:
                    @block.vector
                    def _(v, aof=aof):
                        ctx.wait(v, "DVE")
                        ctx.mark("DVE", v.tensor_copy(x1h[:, :], acc[:, aof:aof + L]))
                else:
                    @block.vector
                    def _(v, aof=aof, sof=sof):
                        ctx.wait(v, "DVE")
                        ctx.wait(v, "cST", ctx.cnt["cST"])
                        ctx.mark("DVE", v.tensor_tensor(stg[:, sof:sof + L],
                                                        acc[:, aof:aof + L],
                                                        x1h[:, :], ALU.mult))
                    @block.sync
                    def _(sy, b=b, mc=mc, sof=sof):
                        ctx.wait(sy, "DVE", ctx.cnt["DVE"])
                        ctx.dma("cST", sy, out=dt["v_sc"][b, (mc - 16) * 128:(mc - 15) * 128, :],
                            in_=stg[:, sof:sof + L])


def phase2(nc, ctx, block, dt, P, sb, ps):
    import contextlib
    with contextlib.ExitStack() as st:
        sbl = lambda n, s_, d: st.enter_context(nc.sbuf_tensor(n, list(s_), d))
        psl = lambda n, s_, d=F32: st.enter_context(nc.psum_tensor(n, list(s_), d))

        for _n in ("d1tw_re", "d1tw_im", "d1tw_imn"):
            P["_" + _n] = sbl("p2" + _n, (64, 32 * 128), BF16)
        vre = sbl("p2_vre", (128, L), BF16)
        vim = sbl("p2_vim", (128, L), BF16)
        x0re = sbl("p2_x0re", (128, L), BF16)
        x0im = sbl("p2_x0im", (128, L), BF16)
        gre = sbl("p2_gre", (128, 32 * 128), BF16)
        gim = sbl("p2_gim", (128, 32 * 128), BF16)
        U_re = sbl("p2_ure", (64, 32 * 128), BF16)
        U_im = sbl("p2_uim", (64, 32 * 128), BF16)
        S2_re = sbl("p2_s2re", (128, 32 * 128), BF16)
        S2_im = sbl("p2_s2im", (128, 32 * 128), BF16)
        M2_re = sbl("p2_m2re", (128, 32 * 128), BF16)
        M2_im = sbl("p2_m2im", (128, 32 * 128), BF16)
        X_re = sbl("p2_xre", (128, 32 * 128), BF16)
        X_im = sbl("p2_xim", (128, 32 * 128), BF16)
        Y_re = sbl("p2_yre", (128, 32 * 128), BF16)
        Y_im = sbl("p2_yim", (128, 32 * 128), BF16)
        Bt_re = sbl("p2_btre", (128, 256), BF16)
        Bt_im = sbl("p2_btim", (128, 256), BF16)
        Bm_re = sbl("p2_bmre", (128, 256), F32)
        Bm_im = sbl("p2_bmim", (128, 256), F32)
        Bp_re = sbl("p2_bpre", (128, 32 * 128), BF16)
        Bp_im = sbl("p2_bpim", (128, 32 * 128), BF16)
        Yb_re = sbl("p2_ybre", (64, 32 * 128), BF16)
        Yb_im = sbl("p2_ybim", (64, 32 * 128), BF16)
        yg = sbl("p2_yg", (128, 512), F32)
        stg_re = sbl("p2_stgre", (128, L), BF16)
        stg_im = sbl("p2_stgim", (128, L), BF16)
        fbias = sbl("p2_fbias", (128, 8), F32)
        twi2_re = sbl("p2_twi2re", (128, 256), F32)
        twi2_im = sbl("p2_twi2im", (128, 256), F32)

        pT = psl("p2_T", (128, 512))
        pS = psl("p2_S", (128, 512))
        pS2 = psl("p2_S2", (128, 512))
        pX = psl("p2_X", (128, 512))
        pX2 = psl("p2_X2", (128, 512))

        @block.sync
        def _(sy):
            ctx.dma("cLD", sy, out=fbias[:, :], in_=dt["fbias_pp"])
            for _n in ("d1tw_re", "d1tw_im", "d1tw_imn"):
                ctx.dma("cLD", sy, out=P["_" + _n].ap().rearrange(
                    "p (t2 k) -> p t2 k", t2=32),
                    in_=dt[_n].rearrange("t2 p k -> p t2 k"))
        @block.vector
        def _(v):
            ctx.wait(v, "cLD", ctx.cnt["cLD"])
            for h in range(2):
                v.tensor_copy(twi2_re[:, h * 128:(h + 1) * 128], P["twi_re"][:, :])
                ctx.mark("DVE", v.tensor_copy(twi2_im[:, h * 128:(h + 1) * 128],
                                              P["twi_im"][:, :]))

        for pair in range(2):
            b0, b1 = 2 * pair, 2 * pair + 1
            for dc in range(8):
                csl = slice(dc * 128, (dc + 1) * 128)
                @block.sync
                def _(sy, b0=b0, b1=b1, dc=dc, csl=csl):
                    ctx.wait(sy, "PE", ctx.cnt["PE"])
                    ctx.wait(sy, "DVE", ctx.cnt["DVE"])
                    ctx.dma("cLD", sy, out=vre[:, :], in_=dt["v_sc"][b0, csl, :])
                    ctx.dma("cLD", sy, out=vim[:, :], in_=dt["v_sc"][b1, csl, :])
                    ctx.dma("cLD", sy, out=x0re[:, :], in_=dt["x0_sc"][b0, csl, :])
                    ctx.dma("cLD", sy, out=x0im[:, :], in_=dt["x0_sc"][b1, csl, :])
                    ctx.dma("cLD", sy, out=gre.ap().rearrange("p (j k) -> p j k", j=32),
                            in_=dt["gf_sc"][dc, :, 0, :, :].rearrange("j p k -> p j k"))
                    ctx.dma("cLD", sy, out=gim.ap().rearrange("p (j k) -> p j k", j=32),
                            in_=dt["gf_sc"][dc, :, 1, :, :].rearrange("j p k -> p j k"))
                ld_u = ctx.cnt["cLD"]

                @block.tensor
                def _(t, ld_u=ld_u):
                    ctx.wait(t, "cLD", ld_u)

                fwd_fft(nc, ctx, block, P, vre, vim, U_re, U_im, S2_re, S2_im,
                        M2_re, M2_im, X_re, X_im, pT, pS, pS2, pX, pX2)

                for grp in range(8):
                    sl = slice(grp * 512, (grp + 1) * 512)
                    @block.vector
                    def _(v, sl=sl, ld_u=ld_u):
                        ctx.wait(v, "cLD", ld_u)
                        ctx.wait(v, "ACT", ctx.cnt["ACT"])
                        ctx.wait(v, "DVE", ctx.cnt["DVE"])
                        ctx.mark("DVE", v.tensor_tensor(Y_re[:, sl], X_re[:, sl],
                                                        gre[:, sl], ALU.mult))
                        ctx.mark("DVE", v.tensor_tensor(Y_im[:, sl], X_im[:, sl],
                                                        gim[:, sl], ALU.mult))
                        ctx.wait(v, "DVE")
                        ctx.mark("DVE", v.tensor_tensor(Y_re[:, sl], Y_re[:, sl],
                                                        Y_im[:, sl], ALU.subtract))
                        ctx.wait(v, "DVE")
                        ctx.mark("DVE", v.tensor_tensor(Y_im[:, sl], X_re[:, sl],
                                                        gim[:, sl], ALU.mult))
                        ctx.wait(v, "DVE")
                        ctx.mark("DVE", v.tensor_tensor(yg[:, 0:512], X_im[:, sl],
                                                        gre[:, sl], ALU.mult))
                        ctx.wait(v, "DVE")
                        ctx.mark("DVE", v.tensor_tensor(Y_im[:, sl], Y_im[:, sl],
                                                        yg[:, 0:512], ALU.add))

                for pr in range(16):
                    @block.tensor
                    def _(t, pr=pr):
                        ctx.wait(t, "DVE", ctx.cnt["DVE"])
                        ctx.wait(t, "ACT", ctx.cnt["ACT"])
                        for jj in range(2):
                            j2 = pr * 2 + jj
                            ysl = slice(j2 * 128, (j2 + 1) * 128)
                            po = slice(jj * 128, (jj + 1) * 128)
                            t.matmul(pS[:, po], Y_re[:, ysl], P["d2i_re"][:, :],
                                     start=True, stop=False)
                            t.matmul(pS[:, po], Y_im[:, ysl], P["d2i_imn"][:, :],
                                     start=False, stop=True)
                            t.matmul(pS2[:, po], Y_im[:, ysl], P["d2i_re"][:, :],
                                     start=True, stop=False)
                            ctx.mark("PE", t.matmul(pS2[:, po], Y_re[:, ysl],
                                                    P["d2i_im"][:, :],
                                                    start=False, stop=True))
                    pe_c = ctx.cnt["PE"]
                    @block.scalar
                    def _(s_, pr=pr, pe_c=pe_c):
                        ctx.wait(s_, "PE", pe_c)
                        ctx.wait(s_, "DVE", ctx.cnt["DVE"])
                        ctx.mark("ACT", s_.activation(Bt_re[:, :], pS[:, 0:256], AF.Copy))
                        ctx.mark("ACT", s_.activation(Bt_im[:, :], pS2[:, 0:256], AF.Copy))
                    act_c = ctx.cnt["ACT"]
                    @block.vector
                    def _(v, pr=pr, act_c=act_c):
                        ctx.wait(v, "ACT", act_c)
                        psl_ = slice(pr * 256, (pr + 1) * 256)
                        ctx.mark("DVE", v.tensor_tensor(Bm_re[:, :], Bt_re[:, :],
                                                        twi2_re[:, :], ALU.mult))
                        ctx.mark("DVE", v.tensor_tensor(Bm_im[:, :], Bt_im[:, :],
                                                        twi2_im[:, :], ALU.mult))
                        ctx.wait(v, "DVE")
                        ctx.mark("DVE", v.tensor_tensor(Bp_re[:, psl_], Bm_re[:, :],
                                                        Bm_im[:, :], ALU.subtract))
                        ctx.wait(v, "DVE")
                        ctx.mark("DVE", v.tensor_tensor(Bm_re[:, :], Bt_re[:, :],
                                                        twi2_im[:, :], ALU.mult))
                        ctx.mark("DVE", v.tensor_tensor(Bm_im[:, :], Bt_im[:, :],
                                                        twi2_re[:, :], ALU.mult))
                        ctx.wait(v, "DVE")
                        ctx.mark("DVE", v.tensor_tensor(Bp_im[:, psl_], Bm_re[:, :],
                                                        Bm_im[:, :], ALU.add))

                for grp in range(8):
                    @block.tensor
                    def _(t, grp=grp):
                        ctx.wait(t, "DVE", ctx.cnt["DVE"])
                        ctx.wait(t, "ACT", ctx.cnt["ACT"])
                        sl = slice(grp * 512, (grp + 1) * 512)
                        bre = Bp_re.ap()[:, sl]
                        bim = Bp_im.ap()[:, sl]
                        t.matmul(pX[0:64, 0:512], P["d1i_re"][:, :], bre,
                                 start=True, stop=False)
                        t.matmul(pX[0:64, 0:512], P["d1i_imn"][:, :], bim,
                                 start=False, stop=True)
                        t.matmul(pX2[0:64, 0:512], P["d1i_im"][:, :], bre,
                                 start=True, stop=False)
                        ctx.mark("PE", t.matmul(pX2[0:64, 0:512], P["d1i_re"][:, :],
                                                bim, start=False, stop=True))
                    pe_c = ctx.cnt["PE"]
                    @block.scalar
                    def _(s_, grp=grp, pe_c=pe_c):
                        ctx.wait(s_, "PE", pe_c)
                        ctx.mark("ACT", s_.activation(Yb_re[0:64, grp * 512:(grp + 1) * 512],
                                                      pX[0:64, 0:512], AF.Copy))
                    @block.vector
                    def _(v, grp=grp, pe_c=pe_c):
                        ctx.wait(v, "PE", pe_c)
                        ctx.mark("DVE", v.tensor_copy(Yb_im[0:64, grp * 512:(grp + 1) * 512],
                                                      pX2[0:64, 0:512]))

                for mg in range(4):
                    @block.tensor
                    def _(t, mg=mg):
                        ctx.wait(t, "ACT", ctx.cnt["ACT"])
                        ctx.wait(t, "DVE", ctx.cnt["DVE"])
                        for ml in range(8):
                            m2 = mg * 8 + ml
                            sel_re = Yb_re.ap().rearrange(
                                "p (j c m) -> p m (j c)", c=4, m=32)[:, m2, :]
                            sel_im = Yb_im.ap().rearrange(
                                "p (j c m) -> p m (j c)", c=4, m=32)[:, m2, :]
                            t.matmul(pT[:, ml * 64:(ml + 1) * 64], sel_re,
                                     P["ident64"][:, :], start=True, stop=True)
                            ctx.mark("PE", t.matmul(pS[:, ml * 64:(ml + 1) * 64],
                                                    sel_im, P["ident64"][:, :],
                                                    start=True, stop=True))
                    pe_c = ctx.cnt["PE"]
                    for pi, (pp, x0b, stgb) in enumerate(
                            ((pT, x0re, stg_re), (pS, x0im, stg_im))):
                        @block.vector
                        def _(v, mg=mg, pp=pp, x0b=x0b, stgb=stgb, pe_c=pe_c, dc=dc):
                            ctx.wait(v, "PE", pe_c)
                            ctx.wait(v, "cST", ctx.cnt["cST"])
                            ctx.wait(v, "DVE")
                            ctx.mark("DVE", v.tensor_scalar(
                                yg[:, 0:512], pp[:, 0:512], fbias[:, dc:dc + 1],
                                None, ALU.add))
                            ctx.wait(v, "DVE")
                            xv = x0b.ap().rearrange(
                                "p (T mg ml) -> p mg ml T", T=64, mg=4)[:, mg]
                            sv = stgb.ap().rearrange(
                                "p (T mg ml) -> p mg ml T", T=64, mg=4)[:, mg]
                            ctx.mark("DVE", v.tensor_tensor(sv, yg[:, 0:512], xv,
                                                            ALU.mult))
                @block.sync
                def _(sy, b0=b0, b1=b1, csl=csl):
                    ctx.wait(sy, "DVE", ctx.cnt["DVE"])
                    ctx.dma("cST", sy, out=dt["yx0_sc"][b0, csl, :], in_=stg_re[:, :])
                    ctx.dma("cST", sy, out=dt["yx0_sc"][b1, csl, :], in_=stg_im[:, :])


def phase3(nc, ctx, block, dt, P, sb, ps):
    import contextlib
    with contextlib.ExitStack() as st:
        sbl = lambda n, s_, d: st.enter_context(nc.sbuf_tensor(n, list(s_), d))
        psl = lambda n, s_, d=F32: st.enter_context(nc.psum_tensor(n, list(s_), d))

        woutb = sbl("p3_wout", (128, 8 * D), BF16)
        boutp = sbl("p3_bout", (128, 8), F32)
        gwf = sbl("p3_gwf", (128, 8 * 16), F32)
        gwh = sbl("p3_gwh", (128, 8 * 16), BF16)
        gwhf = sbl("p3_gwhf", (128, 8 * 16), F32)
        gwl = sbl("p3_gwl", (128, 8 * 16), BF16)
        ebias = sbl("p3_ebias", (16, 1), F32)
        yx0b = sbl("p3_yx0", (128, 8 * L), BF16)
        r_sb = sbl("p3_r", (128, 8 * L), BF16)
        xpbb = sbl("p3_xpb", (128, 8 * L), BF16)
        xr = sbl("p3_xr", (128, 2 * 512), F32)
        r2t = sbl("p3_r2t", (128, 2 * 512), BF16)
        rm = sbl("p3_rm", (1, 512), F32)
        rv = sbl("p3_rv", (1, 512), F32)
        rw = sbl("p3_rw", (1, 512), F32)
        m_b = sbl("p3_mb", (128, L), BF16)
        rs_b = sbl("p3_rsb", (128, L), BF16)
        t1t = sbl("p3_t1", (128, 512), BF16)
        xp32 = sbl("p3_xp32", (128, 2 * 512), F32)
        xlot = sbl("p3_xlo", (128, 512), BF16)
        bia = sbl("p3_bia", (16, L), F32)
        sc_sb = sbl("p3_sc", (16, L), F32)
        biaT = sbl("p3_biaT", (128, 256), F32)
        scT = sbl("p3_scT", (128, 256), F32)
        m1c = sbl("p3_m1c", (128, 8), F32)
        eq1 = sbl("p3_eq1", (128, 16), F32)
        neg = sbl("p3_neg", (128, 16), F32)
        eq2 = sbl("p3_eq2", (128, 16), F32)
        msk = sbl("p3_msk", (128, 16), F32)
        num = sbl("p3_num", (128, 16), F32)
        gwt = sbl("p3_gwt", (128, 256), F32)
        gem = bia

        _pw = psl("p3_w0", (128, 512))
        pW = [_pw, _pw]
        pst = psl("p3_st", (64, 512))
        plog = psl("p3_log", (16, 2048))
        ptok = psl("p3_tok", (128, 256))
        pgw = psl("p3_gw", (16, 512))
        pbc = _pw

        @block.sync
        def _(sy):
            ctx.dma("cLD", sy, out=boutp[:, :], in_=dt["bout_pp"])
            ctx.dma("cLD", sy, out=ebias[:, :], in_=dt["expert_bias"][:, None])
            ctx.dma("cLD", sy, out=gwf.ap().rearrange("p (kc e) -> p kc e", kc=8),
                    in_=dt["gate_wT"].rearrange("(kc p) e -> p kc e", p=128))
        stg3 = st.enter_context(nc.sbuf_tensor("p3_cstg", [128, 2 * 1024], F32))
        wout_src = dt["W_out"].rearrange("(kc p) m -> p kc m", p=128)
        wm3 = []
        for i in range(8):
            par = i % 2
            @block.sync
            def _(sy, i=i, par=par):
                if i >= 2:
                    ctx.wait(sy, "ACT", wm3[i - 2])
                ctx.dma("cLD", sy, out=stg3[:, par * 1024:(par + 1) * 1024],
                        in_=wout_src[:, i, :])
            ldi = ctx.cnt["cLD"]
            @block.scalar
            def _(s_, i=i, par=par, ldi=ldi):
                ctx.wait(s_, "cLD", ldi)
                wm3.append(ctx.mark("ACT", s_.activation(
                    woutb[:, i * D:(i + 1) * D],
                    stg3[:, par * 1024:(par + 1) * 1024], AF.Copy)))
        ldw = (ctx.cnt["cLD"], ctx.cnt["ACT"])

        @block.vector
        def _(v):
            ctx.wait(v, "cLD", ldw[0])
            ctx.mark("DVE", v.tensor_copy(gwh[:, :], gwf[:, :]))
            ctx.wait(v, "DVE")
            ctx.mark("DVE", v.tensor_copy(gwhf[:, :], gwh[:, :]))
            ctx.wait(v, "DVE")
            ctx.mark("DVE", v.tensor_tensor(gwl[:, :], gwf[:, :], gwhf[:, :],
                                            ALU.subtract))

        for b in range(BPC):
            @block.sync
            def _(sy, b=b):
                ctx.wait(sy, "PE", ctx.cnt["PE"])
                ctx.dma("cLD", sy, out=yx0b.ap().rearrange("p (kc t) -> p kc t", kc=8),
                        in_=dt["yx0_sc"][b].rearrange("(kc p) t -> p kc t", p=128))
            ldy = ctx.cnt["cLD"]

            for dout in range(8):
                for tt in range(4):
                    par = (dout * 4 + tt) % 2
                    pq = pW[par]
                    @block.sync
                    def _(sy, b=b, dout=dout, tt=tt, par=par):
                        ctx.wait(sy, "DVE", ctx.cnt["DVE"])
                        ctx.dma("cLD", sy, out=xr[:, par * 512:(par + 1) * 512],
                                in_=dt["xfm"][b, dout * 128:(dout + 1) * 128,
                                              tt * 512:(tt + 1) * 512])
                    ldx = ctx.cnt["cLD"]
                    @block.tensor
                    def _(t, dout=dout, tt=tt, pq=pq, ldy=ldy):
                        ctx.wait(t, "cLD", ldy)
                        ctx.wait(t, "ACT", ldw[1])
                        ctx.wait(t, "DVE", ctx.cnt["DVE"])
                        for kc in range(8):
                            inst = t.matmul(
                                pq[:, :],
                                woutb[:, (kc * 8 + dout) * 128:(kc * 8 + dout + 1) * 128],
                                yx0b[:, kc * L + tt * 512:kc * L + (tt + 1) * 512],
                                start=(kc == 0), stop=(kc == 7))
                        ctx.mark("PE", inst)
                    @block.vector
                    def _(v, dout=dout, tt=tt, pq=pq, par=par, ldx=ldx):
                        ctx.wait(v, "PE", ctx.cnt["PE"])
                        ctx.wait(v, "cLD", ldx)
                        ctx.wait(v, "DVE")
                        ctx.mark("DVE", v.tensor_scalar(
                            xp32[:, 0:512], pq[:, :], boutp[:, dout:dout + 1],
                            None, ALU.add))
                        ctx.wait(v, "DVE")
                        ctx.mark("DVE", v.tensor_tensor(
                            r_sb[:, dout * L + tt * 512:dout * L + (tt + 1) * 512],
                            xp32[:, 0:512], xr[:, par * 512:(par + 1) * 512],
                            ALU.add))

            for tt in range(4):
                tsl = slice(tt * 512, (tt + 1) * 512)
                for kc in range(8):
                    par = kc % 2
                    @block.vector
                    def _(v, kc=kc, tt=tt, par=par):
                        ctx.wait(v, "DVE")
                        ctx.wait(v, "PE", ctx.cnt["PE"])
                        ctx.mark("DVE", v.tensor_tensor(
                            r2t[:, par * 512:(par + 1) * 512],
                            r_sb[:, kc * L + tt * 512:kc * L + (tt + 1) * 512],
                            r_sb[:, kc * L + tt * 512:kc * L + (tt + 1) * 512],
                            ALU.mult))
                    @block.tensor
                    def _(t, kc=kc, tt=tt, par=par):
                        ctx.wait(t, "DVE", ctx.cnt["DVE"])
                        t.matmul(pst[:, :], P["stat_a"][:, :],
                                 r_sb[:, kc * L + tt * 512:kc * L + (tt + 1) * 512],
                                 start=(kc == 0), stop=False)
                        ctx.mark("PE", t.matmul(
                            pst[:, :], P["stat_b"][:, :],
                            r2t[:, par * 512:(par + 1) * 512],
                            start=False, stop=(kc == 7)))
                @block.vector
                def _(v, tt=tt):
                    ctx.wait(v, "PE", ctx.cnt["PE"])
                    ctx.wait(v, "ACT", ctx.cnt["ACT"])
                    ctx.mark("DVE", v.tensor_scalar_mul(rm[:, :], pst[0:1, :],
                                                        float(1.0 / D)))
                    ctx.mark("DVE", v.tensor_scalar_mul(rv[:, :], pst[32:33, :],
                                                        float(1.0 / D)))
                    ctx.wait(v, "DVE")
                    ctx.mark("DVE", v.tensor_tensor(rw[:, :], rm[:, :], rm[:, :],
                                                    ALU.mult))
                    ctx.wait(v, "DVE")
                    ctx.mark("DVE", v.tensor_tensor(rv[:, :], rv[:, :], rw[:, :],
                                                    ALU.subtract))
                    ctx.wait(v, "DVE")
                    ctx.mark("DVE", v.tensor_scalar_add(rv[:, :], rv[:, :],
                                                        float(1e-5)))
                @block.scalar
                def _(s_, tt=tt):
                    ctx.wait(s_, "DVE", ctx.cnt["DVE"])
                    ctx.mark("ACT", s_.activation(rv[:, :], rv[:, :], AF.Sqrt))
                @block.vector
                def _(v, tt=tt):
                    ctx.wait(v, "ACT", ctx.cnt["ACT"])
                    ctx.mark("DVE", v.reciprocal(rv[:, :], rv[:, :]))
                @block.tensor
                def _(t, tt=tt, tsl=tsl):
                    ctx.wait(t, "DVE", ctx.cnt["DVE"])
                    ctx.wait(t, "ACT", ctx.cnt["ACT"])
                    ctx.mark("PE", t.matmul(pbc[:, 0:512], P["ones1x128f"][:, :],
                                            rm[:, :], start=True, stop=True))
                @block.scalar
                def _(s_, tsl=tsl):
                    ctx.wait(s_, "PE", ctx.cnt["PE"])
                    ctx.mark("ACT", s_.activation(m_b[:, tsl], pbc[:, 0:512], AF.Copy))
                @block.tensor
                def _(t, tsl=tsl):
                    ctx.wait(t, "ACT", ctx.cnt["ACT"])
                    ctx.mark("PE", t.matmul(pbc[:, 0:512], P["ones1x128f"][:, :],
                                            rv[:, :], start=True, stop=True))
                @block.scalar
                def _(s_, tsl=tsl):
                    ctx.wait(s_, "PE", ctx.cnt["PE"])
                    ctx.mark("ACT", s_.activation(rs_b[:, tsl], pbc[:, 0:512], AF.Copy))
            bc_done = (ctx.cnt["ACT"], ctx.cnt["DVE"])

            for kc in range(8):
                for tt in range(4):
                    par = tt % 2
                    rsl = slice(kc * L + tt * 512, kc * L + (tt + 1) * 512)
                    tsl = slice(tt * 512, (tt + 1) * 512)
                    @block.vector
                    def _(v, kc=kc, tt=tt, par=par, rsl=rsl, tsl=tsl):
                        ctx.wait(v, "ACT", ctx.cnt["ACT"])
                        ctx.wait(v, "DVE")
                        ctx.mark("DVE", v.tensor_tensor(t1t[:, :], r_sb[:, rsl],
                                                        m_b[:, tsl], ALU.subtract))
                        ctx.wait(v, "DVE")
                        ctx.wait(v, "cST", ctx.cnt["cST"])
                        ctx.mark("DVE", v.tensor_tensor(
                            xp32[:, par * 512:(par + 1) * 512], t1t[:, :],
                            rs_b[:, tsl], ALU.mult))
                    @block.scalar
                    def _(s_, kc=kc, tt=tt, par=par):
                        ctx.wait(s_, "DVE", ctx.cnt["DVE"])
                        ctx.mark("ACT", s_.activation(
                            xpbb[:, kc * L + tt * 512:kc * L + (tt + 1) * 512],
                            xp32[:, par * 512:(par + 1) * 512], AF.Copy))
                    @block.vector
                    def _(v, kc=kc, tt=tt, par=par):
                        ctx.wait(v, "ACT", ctx.cnt["ACT"])
                        ctx.wait(v, "PE", ctx.cnt["PE"])
                        ctx.wait(v, "DVE")
                        ctx.mark("DVE", v.tensor_tensor(
                            xlot[:, :], xp32[:, par * 512:(par + 1) * 512],
                            xpbb[:, kc * L + tt * 512:kc * L + (tt + 1) * 512],
                            ALU.subtract))
                    @block.sync
                    def _(sy, b=b, kc=kc, tt=tt, par=par):
                        ctx.wait(sy, "DVE", ctx.cnt["DVE"])
                        ctx.dma("cST", sy,
                                out=dt["xP_sc"][b, kc * 128:(kc + 1) * 128,
                                                tt * 512:(tt + 1) * 512],
                                in_=xp32[:, par * 512:(par + 1) * 512])
                    @block.tensor
                    def _(t, kc=kc, tt=tt):
                        ctx.wait(t, "DVE", ctx.cnt["DVE"])
                        gsl = slice(kc * 16, (kc + 1) * 16)
                        first = (kc == 0)
                        t.matmul(plog[:, tt * 512:(tt + 1) * 512], gwh[:, gsl],
                                 xpbb[:, kc * L + tt * 512:kc * L + (tt + 1) * 512],
                                 start=first, stop=False)
                        t.matmul(plog[:, tt * 512:(tt + 1) * 512], gwl[:, gsl],
                                 xpbb[:, kc * L + tt * 512:kc * L + (tt + 1) * 512],
                                 start=False, stop=False)
                        ctx.mark("PE", t.matmul(
                            plog[:, tt * 512:(tt + 1) * 512], gwh[:, gsl],
                            xlot[:, :], start=False, stop=(kc == 7)))
            @block.sync
            def _(sy, b=b):
                ctx.wait(sy, "ACT", ctx.cnt["ACT"])
                ctx.dma("cST", sy,
                        out=dt["xPb_sc"][b].rearrange("(kc p) t -> p kc t", p=128),
                        in_=xpbb.ap().rearrange("p (kc t) -> p kc t", kc=8))

            @block.vector
            def _(v):
                ctx.wait(v, "PE", ctx.cnt["PE"])
                ctx.wait(v, "DVE")
                ctx.mark("DVE", v.tensor_scalar(bia[:, :], plog[:, :],
                                                ebias[:, 0:1], None, ALU.add))
            @block.scalar
            def _(s_):
                ctx.wait(s_, "PE", ctx.cnt["PE"])
                ctx.mark("ACT", s_.activation(sc_sb[:, :], plog[:, :], AF.Sigmoid))
            for sg in range(4):
                @block.tensor
                def _(t, sg=sg):
                    ctx.wait(t, "DVE", ctx.cnt["DVE"])
                    ctx.wait(t, "ACT", ctx.cnt["ACT"])
                    for q in range(4):
                        sl4 = sg * 4 + q
                        ctx.mark("PE", t.matmul(
                            ptok[:, q * 16:(q + 1) * 16],
                            bia[:, sl4 * 128:(sl4 + 1) * 128],
                            P["identf128"][0:16, 0:16], start=True, stop=True))
                @block.vector
                def _(v, sg=sg):
                    ctx.wait(v, "PE", ctx.cnt["PE"])
                    ctx.mark("DVE", v.tensor_copy(biaT[:, sg * 64:(sg + 1) * 64],
                                                  ptok[:, 0:64]))
                @block.tensor
                def _(t, sg=sg):
                    ctx.wait(t, "DVE", ctx.cnt["DVE"])
                    for q in range(4):
                        sl4 = sg * 4 + q
                        ctx.mark("PE", t.matmul(
                            ptok[:, q * 16:(q + 1) * 16],
                            sc_sb[:, sl4 * 128:(sl4 + 1) * 128],
                            P["identf128"][0:16, 0:16], start=True, stop=True))
                @block.vector
                def _(v, sg=sg):
                    ctx.wait(v, "PE", ctx.cnt["PE"])
                    ctx.mark("DVE", v.tensor_copy(scT[:, sg * 64:(sg + 1) * 64],
                                                  ptok[:, 0:64]))
            for sl4 in range(16):
                esl = slice(sl4 * 16, (sl4 + 1) * 16)
                @block.vector
                def _(v, sl4=sl4, esl=esl):
                    ctx.wait(v, "DVE")
                    ctx.mark("DVE", v.reduce_max(m1c[:, 0:1], biaT[:, esl],
                                                 mybir.AxisListType.X))
                    ctx.wait(v, "DVE")
                    ctx.mark("DVE", v.tensor_scalar(eq1[:, :], biaT[:, esl],
                                                    m1c[:, 0:1], None, ALU.is_equal))
                    ctx.wait(v, "DVE")
                    ctx.mark("DVE", v.scalar_tensor_tensor(
                        neg[:, :], eq1[:, :], -1e30, biaT[:, esl],
                        ALU.mult, ALU.add))
                    ctx.wait(v, "DVE")
                    ctx.mark("DVE", v.reduce_max(m1c[:, 1:2], neg[:, :],
                                                 mybir.AxisListType.X))
                    ctx.wait(v, "DVE")
                    ctx.mark("DVE", v.tensor_scalar(eq2[:, :], neg[:, :],
                                                    m1c[:, 1:2], None, ALU.is_equal))
                    ctx.wait(v, "DVE")
                    ctx.mark("DVE", v.tensor_tensor(msk[:, :], eq1[:, :],
                                                    eq2[:, :], ALU.add))
                    ctx.wait(v, "DVE")
                    ctx.mark("DVE", v.tensor_tensor(num[:, :], scT[:, esl],
                                                    msk[:, :], ALU.mult))
                    ctx.wait(v, "DVE")
                    ctx.mark("DVE", v.reduce_sum(m1c[:, 2:3], num[:, :],
                                                 mybir.AxisListType.X))
                    ctx.wait(v, "DVE")
                    ctx.mark("DVE", v.tensor_scalar_add(m1c[:, 3:4], m1c[:, 2:3],
                                                        float(1e-10)))
                    ctx.wait(v, "DVE")
                    ctx.mark("DVE", v.reciprocal(m1c[:, 4:5], m1c[:, 3:4]))
                    ctx.wait(v, "DVE")
                    ctx.mark("DVE", v.tensor_scalar(gwt[:, esl], num[:, :],
                                                    m1c[:, 4:5], None, ALU.mult))
            for sg in range(4):
                @block.tensor
                def _(t, sg=sg):
                    ctx.wait(t, "DVE", ctx.cnt["DVE"])
                    ctx.wait(t, "ACT", ctx.cnt["ACT"])
                    for q in range(4):
                        sl4 = sg * 4 + q
                        ctx.mark("PE", t.matmul(
                            pgw[:, q * 128:(q + 1) * 128],
                            gwt[:, sl4 * 16:(sl4 + 1) * 16],
                            P["identf128"][:, :], start=True, stop=True))
                @block.scalar
                def _(s_, sg=sg):
                    ctx.wait(s_, "PE", ctx.cnt["PE"])
                    ctx.wait(s_, "DVE", ctx.cnt["DVE"])
                    ctx.mark("ACT", s_.activation(gem[:, sg * 512:(sg + 1) * 512],
                                                  pgw[:, :], AF.Copy))
            @block.sync
            def _(sy, b=b):
                ctx.wait(sy, "ACT", ctx.cnt["ACT"])
                ctx.dma("cST", sy, out=dt["gatew_sc"][:, b * L:(b + 1) * L],
                        in_=gem[:, :])


def phase4(nc, ctx, block, dt, P, sb, ps):
    import contextlib
    with contextlib.ExitStack() as st:
        sbl = lambda n, s_, d: st.enter_context(nc.sbuf_tensor(n, list(s_), d))
        psl = lambda n, s_, d=F32: st.enter_context(nc.psum_tensor(n, list(s_), d))

        xpa = sbl("p4_xpa", (128, 8 * TT), BF16)
        sb1p = sbl("p4_sb1", (128, 8), F32)
        sb3p = sbl("p4_sb3", (128, 8), F32)
        sb2p = sbl("p4_sb2", (128, 8), F32)
        satile = sbl("p4_sa", (128, 2 * 512), BF16)
        b1p = sbl("p4_b1p", (128, 16 * 4), F32)
        b3p = sbl("p4_b3p", (128, 16 * 4), F32)
        stg4 = st.enter_context(nc.sbuf_tensor("p4_cstg", [128, 2 * 1024], F32))
        stsh = contextlib.ExitStack()
        sbl2 = lambda n, s_, d: stsh.enter_context(nc.sbuf_tensor(n, list(s_), d))
        sw1b = sbl2("p4_sw1", (128, 8 * D), BF16)
        sw3b = sbl2("p4_sw3", (128, 8 * D), BF16)
        sw2b = sbl2("p4_sw2", (128, 8 * D), BF16)
        hsb = sbl2("p4_hs", (128, 8 * 256), BF16)
        ysst = sbl2("p4_ysst", (128, 2048), BF16)

        pa = psl("p4_a", (128, 1024))
        pb = psl("p4_b", (128, 1024))
        pys = psl("p4_ys", (128, 2048))

        @block.sync
        def _(sy):
            ctx.dma("cLD", sy, out=sb1p[:, :], in_=dt["sb1_pp"])
            ctx.dma("cLD", sy, out=sb3p[:, :], in_=dt["sb3_pp"])
            ctx.dma("cLD", sy, out=sb2p[:, :], in_=dt["sb2_pp"])
            ctx.dma("cLD", sy, out=b1p.ap().rearrange("p (e c) -> p e c", e=16),
                    in_=dt["b1e_pp"])
            ctx.dma("cLD", sy, out=b3p.ap().rearrange("p (e c) -> p e c", e=16),
                    in_=dt["b3e_pp"])
        @block.sync
        def _(sy):
            for b in range(BPC):
                ctx.dma("cLD", sy, out=xpa.ap().rearrange(
                    "p (kc bb t) -> p kc bb t", kc=8, bb=BPC)[:, :, b, :],
                    in_=dt["xPb_sc"][b].rearrange("(kc p) t -> p kc t", p=128))
        swm = []
        for wi, (nm, dstb) in enumerate((("sw1T", sw1b), ("sw3T", sw3b),
                                         ("sw2T", sw2b))):
            src = dt[nm].rearrange("(kc p) m -> p kc m", p=128)
            for i in range(8):
                j = wi * 8 + i
                par = j % 2
                @block.sync
                def _(sy, i=i, par=par, src=src, j=j):
                    if j >= 2:
                        ctx.wait(sy, "ACT", swm[j - 2])
                    ctx.dma("cLD", sy, out=stg4[:, par * 1024:(par + 1) * 1024],
                            in_=src[:, i, :])
                ldi = ctx.cnt["cLD"]
                @block.scalar
                def _(s_, i=i, par=par, dstb=dstb, ldi=ldi):
                    ctx.wait(s_, "cLD", ldi)
                    swm.append(ctx.mark("ACT", s_.activation(
                        dstb[:, i * D:(i + 1) * D],
                        stg4[:, par * 1024:(par + 1) * 1024], AF.Copy)))
        ld0 = (ctx.cnt["cLD"], ctx.cnt["ACT"])

        sh_act = {0: 0, 1: 0}
        sh_dve = {0: 0, 1: 0}
        sh_pe = {0: 0, 1: 0}
        for tau in range(32):
            toff = tau * 256
            for fc in range(8):
                par = fc % 2
                po = par * 512
                @block.tensor
                def _(t, tau=tau, fc=fc, toff=toff, par=par, po=po):
                    if tau == 0 and fc == 0:
                        ctx.wait(t, "cLD", ld0[0])
                        ctx.wait(t, "ACT", ld0[1])
                    ctx.wait(t, "ACT", sh_act[par])
                    ctx.wait(t, "DVE", sh_dve[par])
                    for kc in range(8):
                        mv = xpa[:, kc * TT + toff:kc * TT + toff + 256]
                        t.matmul(pa[:, po:po + 256], sw1b[:, (kc * 8 + fc) * 128:(kc * 8 + fc + 1) * 128],
                                 mv, start=(kc == 0), stop=(kc == 7))
                        inst = t.matmul(pb[:, po:po + 256],
                                        sw3b[:, (kc * 8 + fc) * 128:(kc * 8 + fc + 1) * 128],
                                        mv, start=(kc == 0), stop=(kc == 7))
                    sh_pe[par] = ctx.mark("PE", inst)
                @block.scalar
                def _(s_, fc=fc, par=par, po=po):
                    ctx.wait(s_, "PE", sh_pe[par])
                    ctx.wait(s_, "DVE", sh_dve[par])
                    sh_act[par] = ctx.mark("ACT", s_.activation(
                        satile[:, po:po + 256], pa[:, po:po + 256],
                        AF.Silu, bias=sb1p[:, fc:fc + 1]))
                @block.vector
                def _(v, fc=fc, par=par, po=po):
                    ctx.wait(v, "ACT", sh_act[par])
                    ctx.wait(v, "PE", ctx.cnt["PE"])
                    sh_dve[par] = ctx.mark("DVE", v.scalar_tensor_tensor(
                        hsb[:, fc * 256:(fc + 1) * 256], pb[:, po:po + 256],
                        sb3p[:, fc:fc + 1], satile[:, po:po + 256],
                        ALU.add, ALU.mult))
            @block.tensor
            def _(t, tau=tau):
                ctx.wait(t, "DVE", ctx.cnt["DVE"])
                for dout in range(8):
                    for fc in range(8):
                        inst = t.matmul(
                            pys[:, dout * 256:(dout + 1) * 256],
                            sw2b[:, (fc * 8 + dout) * 128:(fc * 8 + dout + 1) * 128],
                            hsb[:, fc * 256:(fc + 1) * 256],
                            start=(fc == 0 and dout % 2 == 0),
                            stop=(fc == 7 and dout % 2 == 1))
                ctx.mark("PE", inst)
            @block.vector
            def _(v, tau=tau):
                ctx.wait(v, "PE", ctx.cnt["PE"])
                ctx.wait(v, "cST", ctx.cnt["cST"])
                for dout in range(8):
                    ctx.mark("DVE", v.tensor_scalar(
                        ysst[:, dout * 256:(dout + 1) * 256],
                        pys[:, dout * 256:(dout + 1) * 256],
                        sb2p[:, dout:dout + 1], None, ALU.add))
            @block.sync
            def _(sy, tau=tau, toff=toff):
                ctx.wait(sy, "DVE", ctx.cnt["DVE"])
                b = toff // L
                off = toff % L
                ctx.dma("cST", sy,
                        out=dt["ys_sc"][b].rearrange("(kc p) t -> p kc t", p=128)[:, :, off:off + 256],
                        in_=ysst.ap().rearrange("p (c t) -> p c t", c=8))

        stsh.close()
        w1r = sbl("p4_w1r", (128, 2 * 8 * 512), BF16)
        w3r = sbl("p4_w3r", (128, 2 * 8 * 512), BF16)
        ger = sbl("p4_ger", (1, 2 * 512), F32)
        geb = sbl("p4_geb", (128, 512), BF16)
        hpt = sbl("p4_hpt", (128, 2 * 4 * 512), BF16)
        pe_after = {}
        for e in range(16):
            wpar = e % 2
            war_pe = pe_after.get(e - 2, 0)
            for wi, (wt, dstb) in enumerate((("w1T", w1r), ("w3T", w3r))):
                src = dt[wt][e].rearrange("(kc p) m -> p kc m", p=128)
                for i in range(4):
                    j = wi * 4 + i
                    spar = j % 2
                    @block.sync
                    def _(sy, i=i, spar=spar, src=src, j=j):
                        ctx.wait(sy, "GP", max(0, ctx.cnt["GP"] - 1))
                        ctx.dma("cLD", sy, out=stg4.ap().rearrange(
                            "p (r k m) -> p r k m", r=2, k=2)[:, spar],
                            in_=src[:, 2 * i:2 * i + 2, :])
                    ldi = ctx.cnt["cLD"]
                    @block.gpsimd
                    def _(g, i=i, spar=spar, dstb=dstb, ldi=ldi,
                          wpar=wpar, war_pe=war_pe):
                        ctx.wait(g, "cLD", ldi)
                        ctx.wait(g, "PE", war_pe)
                        ctx.mark("GP", g.tensor_copy(
                            dstb[:, (wpar * 8 + 2 * i) * 512:(wpar * 8 + 2 * (i + 1)) * 512],
                            stg4[:, spar * 1024:(spar + 1) * 1024]))
            ld_e = (ctx.cnt["GP"], ctx.cnt["cLD"])

            if e == 0:
                ex_act = {0: 0, 1: 0}
                ex_stt = {0: 0, 1: 0}
                ex_gm = {0: 0, 1: 0}
                ex_pe = {0: 0, 1: 0}
                ex_st = {0: 0, 1: 0}
                ex_geb = [0]
                ex_bc = [0]
            for ta in range(16):
                toff = ta * 512
                gpar = ta % 2
                @block.sync
                def _(sy, e=e, toff=toff, gpar=gpar):
                    ctx.wait(sy, "PE", ex_bc[0])
                    ctx.dma("cLD", sy, out=ger[:, gpar * 512:(gpar + 1) * 512],
                            in_=dt["gatew_sc"][e:e + 1, toff:toff + 512])
                ldg = ctx.cnt["cLD"]
                @block.tensor
                def _(t, ta=ta, gpar=gpar, ldg=ldg):
                    ctx.wait(t, "cLD", ldg)
                    ctx.wait(t, "DVE", ex_geb[0])
                    ex_bc[0] = ctx.mark("PE", t.matmul(
                        pys[:, 0:512], P["ones1x128f"][:, :],
                        ger[:, gpar * 512:(gpar + 1) * 512],
                        start=True, stop=True))
                @block.vector
                def _(v):
                    ctx.wait(v, "PE", ex_bc[0])
                    ctx.wait(v, "DVE", ex_gm[0])
                    ctx.wait(v, "DVE", ex_gm[1])
                    ex_geb[0] = ctx.mark("DVE", v.tensor_copy(geb[:, :], pys[:, 0:512]))
                hpar = ta % 2
                for fc in range(4):
                    par = fc % 2
                    po = par * 512
                    @block.tensor
                    def _(t, e=e, ta=ta, fc=fc, toff=toff, wpar=wpar,
                          ld_e=ld_e, par=par, po=po):
                        if ta == 0 and fc == 0:
                            ctx.wait(t, "GP", ld_e[0])
                        ctx.wait(t, "ACT", ex_act[par])
                        ctx.wait(t, "DVE", ex_stt[par])
                        for kc in range(8):
                            mv = xpa[:, kc * TT + toff:kc * TT + toff + 512]
                            t.matmul(pa[:, po:po + 512],
                                     w1r[:, (wpar * 8 + kc) * 512 + fc * 128:
                                         (wpar * 8 + kc) * 512 + (fc + 1) * 128],
                                     mv, start=(kc == 0), stop=(kc == 7))
                            inst = t.matmul(
                                pb[:, po:po + 512],
                                w3r[:, (wpar * 8 + kc) * 512 + fc * 128:
                                    (wpar * 8 + kc) * 512 + (fc + 1) * 128],
                                mv, start=(kc == 0), stop=(kc == 7))
                        ex_pe[par] = ctx.mark("PE", inst)
                    @block.scalar
                    def _(s_, e=e, fc=fc, par=par, po=po):
                        ctx.wait(s_, "PE", ex_pe[par])
                        ctx.wait(s_, "DVE", ex_stt[par])
                        ex_act[par] = ctx.mark("ACT", s_.activation(
                            satile[:, po:po + 512], pa[:, po:po + 512], AF.Silu,
                            bias=b1p[:, e * 4 + fc:e * 4 + fc + 1]))
                    @block.vector
                    def _(v, e=e, fc=fc, hpar=hpar, par=par, po=po):
                        ctx.wait(v, "ACT", ex_act[par])
                        ctx.wait(v, "cST", ex_st[hpar])
                        ex_stt[par] = ctx.mark("DVE", v.scalar_tensor_tensor(
                            hpt[:, hpar * 2048 + fc * 512:hpar * 2048 + (fc + 1) * 512],
                            pb[:, po:po + 512], b3p[:, e * 4 + fc:e * 4 + fc + 1],
                            satile[:, po:po + 512], ALU.add, ALU.mult))
                        ctx.wait(v, "DVE")
                        ex_gm[par] = ctx.mark("DVE", v.tensor_tensor(
                            hpt[:, hpar * 2048 + fc * 512:hpar * 2048 + (fc + 1) * 512],
                            hpt[:, hpar * 2048 + fc * 512:hpar * 2048 + (fc + 1) * 512],
                            geb[:, :], ALU.mult))
                @block.sync
                def _(sy, e=e, ta=ta, hpar=hpar):
                    ctx.wait(sy, "DVE", max(ex_gm[0], ex_gm[1]))
                    for tb in range(2):
                        inst_m = ctx.dma("cST", sy,
                                out=dt["hp_sc"][2 * ta + tb, e].rearrange(
                                    "(fc p) t -> p fc t", p=128),
                                in_=hpt.ap().rearrange(
                                    "p (r fc tb t) -> p r fc tb t",
                                    r=2, fc=4, tb=2)[:, hpar, :, tb, :])
                    ex_st[hpar] = inst_m
                pe_after[e] = ctx.cnt["PE"]


def phase5(nc, ctx, block, dt, P, sb, ps):
    import contextlib
    with contextlib.ExitStack() as st:
        sbl = lambda n, s_, d: st.enter_context(nc.sbuf_tensor(n, list(s_), d))
        psl = lambda n, s_, d=F32: st.enter_context(nc.psum_tensor(n, list(s_), d))

        w2a = sbl("p5_w2", (128, 16 * 4 * D), BF16)
        b2b = sbl("p5_b2", (16, D), BF16)
        st5 = contextlib.ExitStack()
        stg5 = st5.enter_context(nc.sbuf_tensor("p5_cstg", [128, 2048], F32))

        py = psl("p5_y", (128, 2048))
        pst = psl("p5_st", (64, 256))
        pbc = psl("p5_bc", (128, 256))

        w2m = []
        for j in range(32):
            e, h = j // 2, j % 2
            src = dt["w2T"][e].rearrange("(fc p) m -> p fc m", p=128)
            @block.sync
            def _(sy, j=j, src=src, h=h):
                if j >= 1:
                    ctx.wait(sy, "ACT", w2m[j - 1])
                ctx.dma("cLD", sy, out=stg5.ap().rearrange(
                    "p (k m) -> p k m", k=2), in_=src[:, 2 * h:2 * h + 2, :])
            ldi = ctx.cnt["cLD"]
            @block.scalar
            def _(s_, j=j, e=e, h=h, ldi=ldi):
                ctx.wait(s_, "cLD", ldi)
                w2m.append(ctx.mark("ACT", s_.activation(
                    w2a[:, (e * 4 + 2 * h) * D:(e * 4 + 2 * h + 2) * D],
                    stg5[:, :], AF.Copy)))
        @block.sync
        def _(sy):
            ctx.wait(sy, "ACT", w2m[-1])
            ctx.dma("cLD", sy, out=stg5[0:16, 0:D], in_=dt["b2e"])
        ldb2 = ctx.cnt["cLD"]
        @block.scalar
        def _(s_, ldb2=ldb2):
            ctx.wait(s_, "cLD", ldb2)
            ctx.mark("ACT", s_.activation(b2b[:, :], stg5[0:16, 0:D], AF.Copy))
        gp_b2 = ctx.cnt["ACT"]
        st5.close()
        hpr = sbl("p5_hp", (128, 3 * 4 * 4 * 256), BF16)
        gwb = sbl("p5_gwb", (16, 2 * 256), BF16)
        gwf = sbl("p5_gwf", (16, 2 * 256), F32)
        xr = sbl("p5_xr", (128, 2 * 8 * 256), BF16)
        ysr = sbl("p5_ysr", (128, 2 * 8 * 256), BF16)
        r2b = sbl("p5_r2", (128, 2 * 8 * 256), BF16)
        r2q = sbl("p5_r2q", (128, 2 * 256), BF16)
        rm = sbl("p5_rm", (1, 256), F32)
        rv = sbl("p5_rv", (1, 256), F32)
        rw = sbl("p5_rw", (1, 256), F32)
        m_b = sbl("p5_mb", (128, 256), BF16)
        rs_b = sbl("p5_rsb", (128, 256), BF16)
        t1t = sbl("p5_t1", (128, 256), BF16)
        outs = sbl("p5_out", (128, 2 * 8 * 256), F32)

        evac_marks = {0: 0, 1: 0}
        hp_marks = {}
        ld_marks = {}
        st_marks = {0: 0, 1: 0}
        ln_state = {}

        def emit_loads(tau):
            if tau >= 32:
                return
            toff = tau * 256
            b = toff // L
            off = toff % L
            rpar = tau % 2
            @block.sync
            def _(sy, tau=tau, b=b, off=off, rpar=rpar):
                ctx.wait(sy, "DVE", evac_marks[rpar])
                ctx.dma("cLD", sy, out=xr.ap().rearrange(
                    "p (r c t) -> p r c t", r=2, c=8)[:, rpar],
                    in_=dt["xPb_sc"][b].rearrange("(c p) t -> p c t", p=128)[:, :, off:off + 256])
                ctx.dma("cLD", sy, out=ysr.ap().rearrange(
                    "p (r c t) -> p r c t", r=2, c=8)[:, rpar],
                    in_=dt["ys_sc"][b].rearrange("(c p) t -> p c t", p=128)[:, :, off:off + 256])
                ctx.dma("cLD", sy, out=gwf[:, rpar * 256:(rpar + 1) * 256],
                        in_=dt["gatew_sc"][:, toff:toff + 256])
            ldx = ctx.cnt["cLD"]
            @block.vector
            def _(v, ldx=ldx, rpar=rpar):
                ctx.wait(v, "cLD", ldx)
                ctx.wait(v, "PE", ctx.cnt["PE"])
                ctx.mark("DVE", v.tensor_copy(gwb[:, rpar * 256:(rpar + 1) * 256],
                                              gwf[:, rpar * 256:(rpar + 1) * 256]))
            ld_marks[tau] = (ldx, ctx.cnt["DVE"])

        def emit_hp_load(tau, gq):
            if tau >= 32:
                return
            rq = (tau * 4 + gq) % 3
            @block.sync
            def _(sy, tau=tau, gq=gq, rq=rq):
                if tau == 0 and gq == 0:
                    ctx.wait(sy, "ACT", gp_b2)
                prev = hp_marks.get(("pe", tau * 4 + gq - 3), 0)
                ctx.wait(sy, "PE", prev)
                ctx.dma("cLD", sy, out=hpr.ap().rearrange(
                    "p (r e fc t) -> p r e fc t", r=3, e=4, fc=4)[:, rq],
                    in_=dt["hp_sc"][tau, 4 * gq:4 * gq + 4].rearrange(
                        "e (fc p) t -> p e fc t", p=128))
            hp_marks[(tau, gq)] = ctx.cnt["cLD"]

        def emit_mms(tau):
            rpar = tau % 2
            for gq in range(4):
                rq = (tau * 4 + gq) % 3
                @block.tensor
                def _(t, tau=tau, gq=gq, rq=rq, rpar=rpar):
                    ctx.wait(t, "cLD", hp_marks[(tau, gq)])
                    if tau == 0 and gq == 0:
                        ctx.wait(t, "ACT", gp_b2)
                    if gq == 0:
                        ctx.wait(t, "DVE", max(evac_marks[0], evac_marks[1],
                                               ld_marks[tau][1]))
                    elif tau == 0:
                        pass
                    if tau == 0:
                        ctx.wait(t, "ACT", w2m[(gq + 1) * 8 - 1])
                    for dout in range(8):
                        for el in range(4):
                            e = gq * 4 + el
                            for fc in range(4):
                                first = (gq == 0 and el == 0 and fc == 0
                                         and dout % 2 == 0)
                                inst = t.matmul(
                                    py[:, dout * 256:(dout + 1) * 256],
                                    w2a[:, ((e * 4 + fc) * 8 + dout) * 128:
                                        ((e * 4 + fc) * 8 + dout + 1) * 128],
                                    hpr[:, (rq * 16 + el * 4 + fc) * 256:
                                        (rq * 16 + el * 4 + fc + 1) * 256],
                                    start=first, stop=False)
                        if gq == 3:
                            inst = t.matmul(
                                py[:, dout * 256:(dout + 1) * 256],
                                b2b[:, dout * 128:(dout + 1) * 128],
                                gwb[:, rpar * 256:(rpar + 1) * 256],
                                start=False, stop=(dout % 2 == 1))
                    m = ctx.mark("PE", inst)
                    hp_marks[("pe", tau * 4 + gq)] = m
                g3 = tau * 4 + gq + 3
                emit_hp_load(g3 // 4, g3 % 4)

        def emit_evac(tau):
            rpar = tau % 2
            ro = rpar * 2048
            for dout in range(8):
                @block.vector
                def _(v, tau=tau, dout=dout, rpar=rpar, ro=ro):
                    if dout == 0:
                        ctx.wait(v, "PE", hp_marks[("pe", tau * 4 + 3)])
                        ctx.wait(v, "cST", st_marks[rpar])
                    ctx.wait(v, "DVE")
                    ctx.mark("DVE", v.tensor_tensor(
                        outs[:, ro + dout * 256:ro + (dout + 1) * 256],
                        py[:, dout * 256:(dout + 1) * 256],
                        ysr[:, ro + dout * 256:ro + (dout + 1) * 256], ALU.add))
                    ctx.wait(v, "DVE")
                    m1 = ctx.mark("DVE", v.tensor_tensor(
                        r2b[:, ro + dout * 256:ro + (dout + 1) * 256],
                        outs[:, ro + dout * 256:ro + (dout + 1) * 256],
                        xr[:, ro + dout * 256:ro + (dout + 1) * 256], ALU.add))
                    evac_marks[rpar] = m1

        def emit_ln(tau):
            rpar = tau % 2
            ro = rpar * 2048
            b = (tau * 256) // L
            off = (tau * 256) % L
            dve_r2 = evac_marks[rpar]
            for kc in range(8):
                qp = kc % 2
                @block.vector
                def _(v, kc=kc, qp=qp, ro=ro, dve_r2=dve_r2):
                    ctx.wait(v, "DVE")
                    ctx.wait(v, "PE", ctx.cnt["PE"])
                    ctx.mark("DVE", v.tensor_tensor(
                        r2q[:, qp * 256:(qp + 1) * 256],
                        r2b[:, ro + kc * 256:(kc + 1) * 256 + ro],
                        r2b[:, ro + kc * 256:(kc + 1) * 256 + ro], ALU.mult))
                @block.tensor
                def _(t, kc=kc, qp=qp, ro=ro):
                    ctx.wait(t, "DVE", ctx.cnt["DVE"])
                    t.matmul(pst[:, :], P["stat_a"][:, :],
                             r2b[:, ro + kc * 256:(kc + 1) * 256 + ro],
                             start=(kc == 0), stop=False)
                    ctx.mark("PE", t.matmul(pst[:, :], P["stat_b"][:, :],
                                            r2q[:, qp * 256:(qp + 1) * 256],
                                            start=False, stop=(kc == 7)))
            @block.vector
            def _(v):
                ctx.wait(v, "PE", ctx.cnt["PE"])
                ctx.wait(v, "ACT", ctx.cnt["ACT"])
                ctx.mark("DVE", v.tensor_scalar_mul(rm[:, :], pst[0:1, :],
                                                    float(1.0 / D)))
                ctx.mark("DVE", v.tensor_scalar_mul(rv[:, :], pst[32:33, :],
                                                    float(1.0 / D)))
                ctx.wait(v, "DVE")
                ctx.mark("DVE", v.tensor_tensor(rw[:, :], rm[:, :], rm[:, :], ALU.mult))
                ctx.wait(v, "DVE")
                ctx.mark("DVE", v.tensor_tensor(rv[:, :], rv[:, :], rw[:, :], ALU.subtract))
                ctx.wait(v, "DVE")
                ctx.mark("DVE", v.tensor_scalar_add(rv[:, :], rv[:, :], float(1e-5)))
            @block.scalar
            def _(s_):
                ctx.wait(s_, "DVE", ctx.cnt["DVE"])
                ctx.mark("ACT", s_.activation(rv[:, :], rv[:, :], AF.Sqrt))
            @block.vector
            def _(v):
                ctx.wait(v, "ACT", ctx.cnt["ACT"])
                ctx.mark("DVE", v.reciprocal(rv[:, :], rv[:, :]))
            @block.tensor
            def _(t):
                ctx.wait(t, "DVE", ctx.cnt["DVE"])
                ctx.wait(t, "ACT", ctx.cnt["ACT"])
                ctx.mark("PE", t.matmul(pbc[:, :], P["ones1x128f"][:, :], rm[:, :],
                                        start=True, stop=True))
            @block.scalar
            def _(s_):
                ctx.wait(s_, "PE", ctx.cnt["PE"])
                ctx.mark("ACT", s_.activation(m_b[:, :], pbc[:, :], AF.Copy))
            @block.tensor
            def _(t):
                ctx.wait(t, "ACT", ctx.cnt["ACT"])
                ctx.mark("PE", t.matmul(pbc[:, :], P["ones1x128f"][:, :], rv[:, :],
                                        start=True, stop=True))
            @block.scalar
            def _(s_):
                ctx.wait(s_, "PE", ctx.cnt["PE"])
                ctx.mark("ACT", s_.activation(rs_b[:, :], pbc[:, :], AF.Copy))
            for dout in range(8):
                @block.vector
                def _(v, dout=dout, ro=ro):
                    ctx.wait(v, "ACT", ctx.cnt["ACT"])
                    ctx.wait(v, "DVE")
                    ctx.mark("DVE", v.tensor_tensor(
                        t1t[:, :], r2b[:, ro + dout * 256:ro + (dout + 1) * 256],
                        m_b[:, :], ALU.subtract))
                    ctx.wait(v, "DVE")
                    ctx.mark("DVE", v.tensor_tensor(
                        outs[:, ro + dout * 256:ro + (dout + 1) * 256],
                        t1t[:, :], rs_b[:, :], ALU.mult))
            @block.sync
            def _(sy, b=b, off=off, ro=ro, rpar=rpar):
                ctx.wait(sy, "DVE", ctx.cnt["DVE"])
                st_marks[rpar] = ctx.dma("cST", sy,
                        out=dt["out_fm"][b].rearrange("(c p) t -> p c t", p=128)[:, :, off:off + 256],
                        in_=outs.ap().rearrange("p (r c t) -> p r c t", r=2, c=8)[:, rpar])

        emit_loads(0)
        for gq in range(3):
            emit_hp_load(0, gq)
        for tau in range(32):
            emit_mms(tau)
            emit_loads(tau + 1)
            emit_evac(tau)
            if tau >= 1:
                emit_ln(tau - 1)
        emit_ln(31)



def ppv(v, groups=None):
    v = np.asarray(v)
    if v.ndim == 1:
        return np.ascontiguousarray(v.reshape(-1, 128).T)
    return np.ascontiguousarray(v.reshape(v.shape[0], -1, 128).transpose(2, 0, 1))


def make_in_maps(inputs):
    c = make_consts()
    x = np.ascontiguousarray(
        np.asarray(inputs["L0_attn_x"]).transpose(0, 2, 1))
    base = {
        "W_in": np.ascontiguousarray(inputs["L0_attn_W_in"]),
        "bin_pp": ppv(inputs["L0_attn_b_in"]),
        "shortw_pp": np.ascontiguousarray(
            np.asarray(inputs["L0_attn_short_w"]).reshape(24, 128, 3).transpose(1, 0, 2)),
        "shortb_pp": ppv(inputs["L0_attn_short_b"]),
        "fW1": np.ascontiguousarray(inputs["L0_attn_W1"]),
        "fb1": np.ascontiguousarray(inputs["L0_attn_b1"]),
        "ff1": np.ascontiguousarray(inputs["L0_attn_f1"]),
        "fW2": np.ascontiguousarray(inputs["L0_attn_W2"]),
        "fb2": np.ascontiguousarray(inputs["L0_attn_b2"]),
        "ff2": np.ascontiguousarray(inputs["L0_attn_f2"]),
        "fW3": np.ascontiguousarray(inputs["L0_attn_W3"]),
        "fb3_pp": ppv(inputs["L0_attn_b3"]),
        "fbias_pp": ppv(inputs["L0_attn_filter_bias"]),
        "W_out": np.ascontiguousarray(inputs["L0_attn_W_out"]),
        "bout_pp": ppv(inputs["L0_attn_b_out"]),
        "gate_wT": np.ascontiguousarray(np.asarray(inputs["L0_ffn_gate_w"]).T),
        "expert_bias": np.ascontiguousarray(inputs["L0_ffn_expert_bias"]),
        "w1T": np.ascontiguousarray(np.asarray(inputs["L0_ffn_w1"]).transpose(0, 2, 1)),
        "b1e_pp": ppv(inputs["L0_ffn_b1"]),
        "w3T": np.ascontiguousarray(np.asarray(inputs["L0_ffn_w3"]).transpose(0, 2, 1)),
        "b3e_pp": ppv(inputs["L0_ffn_b3"]),
        "w2T": np.ascontiguousarray(np.asarray(inputs["L0_ffn_w2"]).transpose(0, 2, 1)),
        "b2e": np.ascontiguousarray(inputs["L0_ffn_b2"]),
        "sw1T": np.ascontiguousarray(np.asarray(inputs["L0_ffn_sw1"]).T),
        "sb1_pp": ppv(inputs["L0_ffn_sb1"]),
        "sw3T": np.ascontiguousarray(np.asarray(inputs["L0_ffn_sw3"]).T),
        "sb3_pp": ppv(inputs["L0_ffn_sb3"]),
        "sw2T": np.ascontiguousarray(np.asarray(inputs["L0_ffn_sw2"]).T),
        "sb2_pp": ppv(inputs["L0_ffn_sb2"]),
        "dneg_pp": ppv(c["deltas_neg"][:, 0]),
    }
    c.pop("deltas_neg", None)
    base.update(c)
    maps = []
    for core in range(8):
        m = dict(base)
        m["xfm"] = np.ascontiguousarray(x[core * BPC:(core + 1) * BPC])
        maps.append(m)
    return maps


def kernel(**inputs):
    nc = build_program()
    maps = make_in_maps(inputs)
    res = run_bass_kernel_spmd(nc, maps, core_ids=list(range(8)))
    outs = [np.ascontiguousarray(r["out_fm"].transpose(0, 2, 1))
            for r in res.results]
    return np.concatenate(outs, 0).astype(np.float32)


# version 04
# speedup vs baseline: 1.4631x; 1.1485x over previous
import math
import numpy as np
import ml_dtypes

import concourse.bass as bass
import concourse.mybir as mybir
from concourse.bass_utils import run_bass_kernel_spmd

F32 = mybir.dt.float32
BF16 = mybir.dt.bfloat16
I32 = mybir.dt.int32
AF = mybir.ActivationFunctionType
ALU = mybir.AluOpType

L = 2048
D = 1024
M3 = 3 * D
BPC = 4
TT = 8192
NB = ml_dtypes.bfloat16
TWO_PI = float(2.0 * np.pi)



def make_consts():
    c = {}
    t1 = np.arange(64)
    k1 = np.arange(128)
    t2 = np.arange(32)
    k2 = np.arange(32)
    Tq = np.arange(64)

    w128 = np.exp(-2j * np.pi * np.outer(t1, k1) / 128.0)
    tw = np.exp(-2j * np.pi * np.outer(t2, k1) / 4096.0)
    d1 = np.stack([w128[None, :, :] * tw[:, None, :]][0], 0)
    c["d1tw_re"] = d1.real.astype(NB)
    c["d1tw_im"] = d1.imag.astype(NB)
    c["d1tw_imn"] = (-d1.imag).astype(NB)

    W2c = np.exp(-2j * np.pi * np.outer(t2, k2) / 32.0)
    D2BD = np.zeros((128, 128), np.complex128)
    for cs in range(4):
        D2BD[cs * 32:(cs + 1) * 32, cs * 32:(cs + 1) * 32] = W2c
    c["d2_re"] = D2BD.real.astype(NB)
    c["d2_im"] = D2BD.imag.astype(NB)
    c["d2_imn"] = (-D2BD.imag).astype(NB)

    W2i = np.exp(+2j * np.pi * np.outer(k2, t2) / 32.0)
    D2i = np.zeros((128, 128), np.complex128)
    for cs in range(4):
        D2i[cs * 32:(cs + 1) * 32, cs * 32:(cs + 1) * 32] = W2i
    c["d2i_re"] = D2i.real.astype(NB)
    c["d2i_im"] = D2i.imag.astype(NB)
    c["d2i_imn"] = (-D2i.imag).astype(NB)

    TWI = np.exp(+2j * np.pi * np.outer(k1, t2) / 4096.0) / 4096.0
    TWI4 = np.repeat(TWI[:, None, :], 4, axis=1).reshape(128, 128)
    c["twi_re"] = TWI4.real.astype(np.float32)
    c["twi_im"] = TWI4.imag.astype(np.float32)

    D1i = np.exp(+2j * np.pi * np.outer(k1, Tq) / 128.0)
    c["d1i_re"] = D1i.real.astype(NB)
    c["d1i_im"] = D1i.imag.astype(NB)
    c["d1i_imn"] = (-D1i.imag).astype(NB)

    c["ident128"] = np.eye(128, dtype=NB)
    c["ident64"] = np.eye(64, dtype=NB)
    c["ones1x128"] = np.ones((1, 128), dtype=NB)
    c["ones1x128f"] = np.ones((1, 128), dtype=np.float32)
    sa = np.zeros((128, 64), np.float32); sa[:, 0] = 1.0
    sbv = np.zeros((128, 64), np.float32); sbv[:, 32] = 1.0
    c["stat_a"] = sa.astype(NB)
    c["stat_b"] = sbv.astype(NB)
    c["identf128"] = np.eye(128, dtype=np.float32)

    tau = np.arange(L)
    trev = ((L - 1 - tau) / (L - 1.0)).astype(np.float32)
    ang = 2.0 * np.pi * (L - 1 - tau) / L
    phase = 0.0001 * ang
    c["pe_rev"] = np.stack([trev, np.cos(phase).astype(np.float32),
                            (-np.sin(phase)).astype(np.float32)], 0).astype(np.float32)
    dmax, dmin = (math.log(0.01) / 0.3, math.log(0.01) / 1.5)
    c["deltas_neg"] = (-np.abs(np.linspace(dmax, dmin, D))).astype(np.float32)[:, None]
    c["trev_row"] = trev[None, :]
    return c



class Ctx:

    ENGS = ("PE", "ACT", "DVE", "GP", "SP", "BAR")
    DMAS = ("cLD", "cST", "cGL", "cGS")

    def __init__(self, nc, stack):
        self.nc = nc
        self.sem = {}
        self.cnt = {}
        for n in self.ENGS + self.DMAS:
            self.sem[n] = stack.enter_context(nc.semaphore("sem_" + n))
            self.cnt[n] = 0

    def mark(self, name, inst, n=1):
        step = 16 if name in self.DMAS else 1
        inst.then_inc(self.sem[name], n * step)
        self.cnt[name] += n
        return self.cnt[name]

    def dma(self, name, eng, out, in_):
        self.wait(eng, name)
        inst = eng.dma_start(out=out, in_=in_)
        return self.mark(name, inst)

    def wait(self, eng, name, count=None):
        step = 16 if name in self.DMAS else 1
        if count is None:
            count = self.cnt[name]
        if count > 0:
            eng.wait_ge(self.sem[name], count * step)

    def barrier(self, block):
        @block.sync
        def _(sy):
            for d in self.DMAS:
                self.wait(sy, d)
        @block.tensor
        def _(t):
            self.mark("BAR", t.nop())
        @block.scalar
        def _(s):
            self.mark("BAR", s.nop())
        @block.vector
        def _(v):
            self.mark("BAR", v.nop())
        @block.gpsimd
        def _(g):
            self.mark("BAR", g.nop())
        @block.sync
        def _(sy):
            self.mark("BAR", sy.nop())
        tgt = self.cnt["BAR"]
        @block.tensor
        def _(t):
            self.wait(t, "BAR", tgt)
        @block.scalar
        def _(s):
            self.wait(s, "BAR", tgt)
        @block.vector
        def _(v):
            self.wait(v, "BAR", tgt)
        @block.gpsimd
        def _(g):
            self.wait(g, "BAR", tgt)
        @block.sync
        def _(sy):
            self.wait(sy, "BAR", tgt)



def build_program(dbg=(), upto=99):
    import contextlib
    nc = bass.Bass("TRN2", target_bir_lowering=False, debug=False)

    dt = {}

    def din(name, shape, dtype=F32):
        dt[name] = nc.dram_tensor(name, list(shape), dtype, kind="ExternalInput").ap()
        return dt[name]

    def dsc(name, shape, dtype):
        kind = "ExternalOutput" if name in dbg else "Internal"
        dt[name] = nc.dram_tensor(name, list(shape), dtype, kind=kind).ap()
        return dt[name]

    din("xfm", (BPC, D, L))
    din("W_in", (D, M3)); din("bin_pp", (128, 24))
    din("shortw_pp", (128, 24, 3)); din("shortb_pp", (128, 24))
    din("fW1", (3, 64)); din("fb1", (64,)); din("ff1", (64,))
    din("fW2", (64, 64)); din("fb2", (64,)); din("ff2", (64,))
    din("fW3", (64, D)); din("fb3_pp", (128, 8)); din("fbias_pp", (128, 8))
    din("W_out", (D, D)); din("bout_pp", (128, 8))
    din("gate_wT", (D, 16)); din("expert_bias", (16,))
    din("w1T", (16, D, 512)); din("b1e_pp", (128, 16, 4))
    din("w3T", (16, D, 512)); din("b3e_pp", (128, 16, 4))
    din("w2T", (16, 512, D)); din("b2e", (16, D))
    din("sw1T", (D, D)); din("sb1_pp", (128, 8))
    din("sw3T", (D, D)); din("sb3_pp", (128, 8))
    din("sw2T", (D, D)); din("sb2_pp", (128, 8))
    din("dneg_pp", (128, 8))
    for n in ("d1tw_re", "d1tw_im", "d1tw_imn"):
        din(n, (32, 64, 128), BF16)
    for n in ("d2_re", "d2_im", "d2_imn", "d2i_re", "d2i_im", "d2i_imn",
              "ident128"):
        din(n, (128, 128), BF16)
    din("twi_re", (128, 128)); din("twi_im", (128, 128))
    for n in ("d1i_re", "d1i_im", "d1i_imn"):
        din(n, (128, 64), BF16)
    din("ident64", (64, 64), BF16)
    din("ones1x128", (1, 128), BF16)
    din("ones1x128f", (1, 128))
    din("stat_a", (128, 64), BF16); din("stat_b", (128, 64), BF16)
    din("identf128", (128, 128))
    din("pe_rev", (3, L)); din("trev_row", (1, L))

    dsc("gf_sc", (8, 32, 2, 128, 128), BF16)
    dsc("v_sc", (BPC, D, L), BF16)
    dsc("x0_sc", (BPC, D, L), BF16)
    dsc("yx0_sc", (BPC, D, L), BF16)
    dsc("xP_sc", (BPC, D, L), F32)
    dsc("xPb_sc", (BPC, D, L), BF16)
    dsc("gatew_sc", (16, TT), F32)
    dsc("ys_sc", (BPC, D, L), BF16)
    dsc("hp_sc", (32, 16, 512, 256), BF16)
    out = nc.dram_tensor("out_fm", [BPC, D, L], F32, kind="ExternalOutput").ap()
    dt["out_fm"] = out

    with contextlib.ExitStack() as stack:
        ctx = Ctx(nc, stack)
        sb = lambda name, shape, dtype: stack.enter_context(
            nc.sbuf_tensor(name, list(shape), dtype))
        ps = lambda name, shape, dtype=F32: stack.enter_context(
            nc.psum_tensor(name, list(shape), dtype))
        block = stack.enter_context(nc.Block())

        P = {}
        for n in ("d2_re", "d2_im", "d2_imn", "d2i_re", "d2i_im", "d2i_imn",
                  "ident128"):
            P[n] = sb("c_" + n, (128, 128), BF16)
        P["twi_re"] = sb("c_twire", (128, 128), F32)
        P["twi_im"] = sb("c_twiim", (128, 128), F32)
        for n in ("d1i_re", "d1i_im", "d1i_imn"):
            P[n] = sb("c_" + n, (128, 64), BF16)
        P["ident64"] = sb("c_i64", (64, 64), BF16)
        P["ones1x128"] = sb("c_ones", (1, 128), BF16)
        P["ones1x128f"] = sb("c_onesf", (1, 128), F32)
        P["stat_a"] = sb("c_stata", (128, 64), BF16)
        P["stat_b"] = sb("c_statb", (128, 64), BF16)
        P["identf128"] = sb("c_if128", (128, 128), F32)

        @block.sync
        def _(sy):
            for n in ("d2_re", "d2_im", "d2_imn", "d2i_re", "d2i_im",
                      "d2i_imn", "ident128", "twi_re", "twi_im", "d1i_re",
                      "d1i_im", "d1i_imn", "ident64", "ones1x128", "ones1x128f",
                      "stat_a", "stat_b", "identf128"):
                ctx.dma("cLD", sy, out=P[n][:, :], in_=dt[n])

        ld_consts = ctx.cnt["cLD"]

        phases = [phase0_w(ld_consts), phase1, phase2, phase3, phase4, phase5]
        for i, ph in enumerate(phases):
            if i > upto:
                break
            ph(nc, ctx, block, dt, P, sb, ps)
            if i < upto and i < len(phases) - 1:
                ctx.barrier(block)

        @block.sync
        def _(sy):
            ctx.wait(sy, "cST")
            ctx.wait(sy, "cGS")

    return nc



def phase0_w(ld_consts):
    def f(nc, ctx, block, dt, P, sb, ps):
        return phase0(nc, ctx, block, dt, P, sb, ps, ld_consts)
    return f


def phase0(nc, ctx, block, dt, P, sb, ps, ld_consts):
    import contextlib
    with contextlib.ExitStack() as st:
        sbl = lambda n, s, d: st.enter_context(nc.sbuf_tensor(n, list(s), d))
        psl = lambda n, s, d=F32: st.enter_context(nc.psum_tensor(n, list(s), d))

        pe_sb = sbl("pe_sb", (3, L), F32)
        fW1 = sbl("fW1s", (3, 64), F32)
        fW2 = sbl("fW2s", (64, 64), F32)
        fW3 = sbl("fW3s", (64, D), F32)
        vecs = sbl("fvecs", (64, 6), F32)
        b3pp = sbl("b3pp", (128, 8), F32)
        dneg = sbl("dneg", (128, 8), F32)
        h_sb = sbl("h_sb", (64, L), F32)
        zi = sbl("zi", (64, L), I32)
        zf = sbl("zf", (64, L), F32)
        trow = sbl("trow", (1, L), F32)
        E_sb = sbl("E_sb", (128, L), F32)
        g_sb = sbl("g_sb", (128, L), BF16)
        kpre = sbl("kpre", (128, L), F32)
        for _n in ("d1tw_re", "d1tw_im", "d1tw_imn"):
            P["_" + _n] = sbl("p0" + _n, (64, 32 * 128), BF16)
        U_re = sbl("p0_ure", (64, 32 * 128), BF16)
        S2_re = sbl("p0_s2re", (128, 32 * 128), BF16)
        S2_im = sbl("p0_s2im", (128, 32 * 128), BF16)
        M2_re = sbl("p0_m2re", (128, 32 * 128), BF16)
        M2_im = sbl("p0_m2im", (128, 32 * 128), BF16)
        G_re = sbl("p0_gre", (128, 32 * 128), BF16)
        G_im = sbl("p0_gim", (128, 32 * 128), BF16)

        pmm = psl("p0_mm", (128, 512))
        pT = psl("p0_T", (128, 512))
        pS = psl("p0_S", (128, 512))
        pS2 = psl("p0_S2", (128, 512))
        pX = psl("p0_X", (128, 512))
        pX2 = psl("p0_X2", (128, 512))

        @block.sync
        def _(sy):
            ctx.dma("cLD", sy, out=pe_sb[:, :], in_=dt["pe_rev"])
            ctx.dma("cLD", sy, out=fW1[:, :], in_=dt["fW1"])
            ctx.dma("cLD", sy, out=fW2[:, :], in_=dt["fW2"])
            ctx.dma("cLD", sy, out=fW3[:, :], in_=dt["fW3"])
            ctx.dma("cLD", sy, out=vecs[:, 0:1], in_=dt["fb1"][:, None])
            ctx.dma("cLD", sy, out=vecs[:, 1:2], in_=dt["ff1"][:, None])
            ctx.dma("cLD", sy, out=vecs[:, 2:3], in_=dt["fb2"][:, None])
            ctx.dma("cLD", sy, out=vecs[:, 3:4], in_=dt["ff2"][:, None])
            ctx.dma("cLD", sy, out=b3pp[:, :], in_=dt["fb3_pp"])
            ctx.dma("cLD", sy, out=dneg[:, :], in_=dt["dneg_pp"])
            ctx.dma("cLD", sy, out=trow[:, :], in_=dt["trev_row"])
            for _n in ("d1tw_re", "d1tw_im", "d1tw_imn"):
                ctx.dma("cLD", sy, out=P["_" + _n].ap().rearrange(
                    "p (t2 k) -> p t2 k", t2=32),
                    in_=dt[_n].rearrange("t2 p k -> p t2 k"))
        ld0 = ctx.cnt["cLD"]

        @block.vector
        def _(v):
            ctx.wait(v, "cLD", ld0)
            v.tensor_tensor(vecs[:, 4:5], vecs[:, 0:1], vecs[:, 1:2], ALU.mult)
            ctx.mark("DVE", v.tensor_tensor(vecs[:, 5:6], vecs[:, 2:3],
                                            vecs[:, 3:4], ALU.mult))
        prod_done = ctx.cnt["DVE"]

        def filt_layer(Wsb, K, scale_col, bias_col, src, dst):
            for tt in range(4):
                sl = slice(tt * 512, (tt + 1) * 512)
                @block.tensor
                def _(t, sl=sl, tt=tt):
                    ctx.wait(t, "DVE", ctx.cnt["DVE"])
                    ctx.wait(t, "ACT", ctx.cnt["ACT"])
                    ctx.mark("PE", t.matmul(pmm[0:64, 0:512], Wsb[0:K, :],
                                            src[0:K, sl], start=True, stop=True))
                @block.vector
                def _(v, sl=sl):
                    ctx.wait(v, "PE", ctx.cnt["PE"])
                    ctx.wait(v, "DVE", prod_done)
                    ctx.mark("DVE", v.tensor_scalar(
                        dst[:, sl], pmm[0:64, 0:512],
                        vecs[:, scale_col:scale_col + 1],
                        vecs[:, bias_col:bias_col + 1], ALU.mult, ALU.add))
            @block.vector
            def _(v):
                ctx.wait(v, "DVE")
                ctx.mark("DVE", v.tensor_scalar_mul(zf[:, :], dst[:, :],
                                                    float(1.0 / TWO_PI)))
                ctx.wait(v, "DVE")
                ctx.mark("DVE", v.tensor_copy(zi[:, :], zf[:, :]))
                ctx.wait(v, "DVE")
                ctx.mark("DVE", v.tensor_copy(zf[:, :], zi[:, :]))
                ctx.wait(v, "DVE")
                ctx.mark("DVE", v.scalar_tensor_tensor(
                    dst[:, :], zf[:, :], -TWO_PI, dst[:, :], ALU.mult, ALU.add))
                ctx.wait(v, "DVE")
                ctx.mark("DVE", v.tensor_single_scalar(zf[:, :], dst[:, :],
                                                       float(np.pi), ALU.is_gt))
                ctx.wait(v, "DVE")
                ctx.mark("DVE", v.scalar_tensor_tensor(
                    dst[:, :], zf[:, :], -TWO_PI, dst[:, :], ALU.mult, ALU.add))
                ctx.wait(v, "DVE")
                ctx.mark("DVE", v.tensor_single_scalar(zf[:, :], dst[:, :],
                                                       float(-np.pi), ALU.is_lt))
                ctx.wait(v, "DVE")
                ctx.mark("DVE", v.scalar_tensor_tensor(
                    dst[:, :], zf[:, :], TWO_PI, dst[:, :], ALU.mult, ALU.add))
            @block.scalar
            def _(s):
                ctx.wait(s, "DVE", ctx.cnt["DVE"])
                ctx.mark("ACT", s.activation(dst[:, :], dst[:, :], AF.Sin))

        h1 = sbl("h1_sb", (64, L), F32)
        filt_layer(fW1, 3, 1, 4, pe_sb, h1)
        filt_layer(fW2, 64, 3, 5, h1, h_sb)

        for dc in range(8):
            csl = slice(dc * 128, (dc + 1) * 128)
            for tt in range(4):
                sl = slice(tt * 512, (tt + 1) * 512)
                @block.tensor
                def _(t, sl=sl, tt=tt, dc=dc):
                    ctx.wait(t, "ACT", ctx.cnt["ACT"])
                    ctx.mark("PE", t.matmul(pT[:, 0:512], P["ones1x128f"][:, :],
                                            trow[:, sl], start=True, stop=True))
                @block.scalar
                def _(s, sl=sl, dc=dc):
                    ctx.wait(s, "PE", ctx.cnt["PE"])
                    ctx.mark("ACT", s.activation(E_sb[:, sl], pT[:, 0:512],
                                                 AF.Exp, scale=dneg[:, dc:dc + 1]))
            for tt in range(4):
                sl = slice(tt * 512, (tt + 1) * 512)
                @block.tensor
                def _(t, sl=sl, dc=dc, tt=tt):
                    ctx.wait(t, "DVE", ctx.cnt["DVE"])
                    ctx.mark("PE", t.matmul(pS[:, 0:512], fW3[:, dc * 128:(dc + 1) * 128],
                                            h_sb[:, sl], start=True, stop=True))
                @block.vector
                def _(v, sl=sl, dc=dc):
                    ctx.wait(v, "PE", ctx.cnt["PE"])
                    ctx.wait(v, "ACT", ctx.cnt["ACT"])
                    ctx.mark("DVE", v.tensor_scalar(kpre[:, sl], pS[:, 0:512],
                                                    b3pp[:, dc:dc + 1], None, ALU.add))
                    ctx.wait(v, "DVE")
                    ctx.mark("DVE", v.tensor_tensor(g_sb[:, sl], kpre[:, sl],
                                                    E_sb[:, sl], ALU.mult))
            fwd_fft(nc, ctx, block, P, g_sb, None, U_re, None, S2_re, S2_im,
                    M2_re, M2_im, G_re, G_im, pT, pS, pS2, pX, pX2)
            @block.sync
            def _(sy, dc=dc):
                ctx.wait(sy, "DVE", ctx.cnt["DVE"])
                ctx.wait(sy, "ACT", ctx.cnt["ACT"])
                ctx.dma("cST", sy, out=dt["gf_sc"][dc, :, 0, :, :].rearrange("j p k -> p j k"),
                    in_=G_re.ap().rearrange("p (j k) -> p j k", j=32))
                ctx.dma("cST", sy, out=dt["gf_sc"][dc, :, 1, :, :].rearrange("j p k -> p j k"),
                    in_=G_im.ap().rearrange("p (j k) -> p j k", j=32))
            @block.vector
            def _(v):
                ctx.wait(v, "cST", ctx.cnt["cST"])
            @block.scalar
            def _(s):
                ctx.wait(s, "cST", ctx.cnt["cST"])


def fwd_fft(nc, ctx, block, P, vre_sb, vim_sb, U_re, U_im, S2_re, S2_im,
            M2_re, M2_im, X_re, X_im, pT, pS, pS2, pX, pX2):
    cplx = vim_sb is not None
    planes = ((vre_sb, U_re), (vim_sb, U_im)) if cplx else ((vre_sb, U_re),)

    per_grp = 4 // len(planes)
    ngrp = 32 // per_grp
    for grp in range(ngrp):
        @block.tensor
        def _(t, grp=grp):
            ctx.wait(t, "DVE", ctx.cnt["DVE"])
            ctx.wait(t, "ACT", ctx.cnt["ACT"])
            for q in range(per_grp):
                t2 = grp * per_grp + q
                for pi, (vsb, _u) in enumerate(planes):
                    src = vsb.ap().rearrange("p (t1 t2) -> p t2 t1", t2=32)[:, t2, 0:64]
                    off = (q * len(planes) + pi) * 128
                    ctx.mark("PE", t.matmul(pT[0:64, off:off + 128], src,
                                            P["ident128"][:, :], start=True, stop=True))
        @block.vector
        def _(v, grp=grp):
            ctx.wait(v, "PE", ctx.cnt["PE"])
            for q in range(per_grp):
                t2 = grp * per_grp + q
                for pi, (_v, usb) in enumerate(planes):
                    off = (q * len(planes) + pi) * 128
                    ctx.mark("DVE", v.tensor_copy(
                        usb[0:64, t2 * 128:(t2 + 1) * 128], pT[0:64, off:off + 128]))

    for grp in range(8):
        @block.tensor
        def _(t, grp=grp):
            ctx.wait(t, "DVE", ctx.cnt["DVE"])
            ctx.wait(t, "ACT", ctx.cnt["ACT"])
            for q in range(4):
                t2 = grp * 4 + q
                d_re = P["_d1tw_re"][:, t2 * 128:(t2 + 1) * 128]
                d_im = P["_d1tw_im"][:, t2 * 128:(t2 + 1) * 128]
                d_imn = P["_d1tw_imn"][:, t2 * 128:(t2 + 1) * 128]
                ure = U_re[0:64, t2 * 128:(t2 + 1) * 128]
                re_ps = pS[:, q * 128:(q + 1) * 128]
                im_ps = pS2[:, q * 128:(q + 1) * 128]
                if cplx:
                    uim = U_im[0:64, t2 * 128:(t2 + 1) * 128]
                    t.matmul(re_ps, d_re, ure, start=True, stop=False)
                    t.matmul(re_ps, d_imn, uim, start=False, stop=True)
                    t.matmul(im_ps, d_im, ure, start=True, stop=False)
                    ctx.mark("PE", t.matmul(im_ps, d_re, uim, start=False, stop=True))
                else:
                    t.matmul(re_ps, d_re, ure, start=True, stop=True)
                    ctx.mark("PE", t.matmul(im_ps, d_im, ure, start=True, stop=True))
        pe_c = ctx.cnt["PE"]
        @block.scalar
        def _(s, grp=grp, pe_c=pe_c):
            ctx.wait(s, "PE", pe_c)
            for q in range(4):
                t2 = grp * 4 + q
                dst = S2_re.ap().rearrange("p (c t2) -> p c t2", t2=32)[:, :, t2]
                ctx.mark("ACT", s.activation(dst, pS[:, q * 128:(q + 1) * 128], AF.Copy))
        @block.vector
        def _(v, grp=grp, pe_c=pe_c):
            ctx.wait(v, "PE", pe_c)
            for q in range(4):
                t2 = grp * 4 + q
                dst = S2_im.ap().rearrange("p (c t2) -> p c t2", t2=32)[:, :, t2]
                ctx.mark("DVE", v.tensor_copy(dst, pS2[:, q * 128:(q + 1) * 128]))

    for grp in range(8):
        @block.tensor
        def _(t, grp=grp):
            ctx.wait(t, "ACT", ctx.cnt["ACT"])
            ctx.wait(t, "DVE", ctx.cnt["DVE"])
            for q in range(2):
                j2 = grp * 4 + q * 2
                for jj in range(2):
                    sl = slice((j2 + jj) * 128, (j2 + jj + 1) * 128)
                    t.matmul(pT[:, (q * 2 + jj) * 128:(q * 2 + jj + 1) * 128],
                             S2_re.ap()[:, sl], P["ident128"][:, :],
                             start=True, stop=True)
                    ctx.mark("PE", t.matmul(
                        pS[:, (q * 2 + jj) * 128:(q * 2 + jj + 1) * 128],
                        S2_im.ap()[:, sl], P["ident128"][:, :],
                        start=True, stop=True))
        pe_c = ctx.cnt["PE"]
        @block.scalar
        def _(s, grp=grp, pe_c=pe_c):
            ctx.wait(s, "PE", pe_c)
            ctx.mark("ACT", s.activation(M2_re[:, grp * 512:(grp + 1) * 512],
                                         pT[:, 0:512], AF.Copy))
        @block.vector
        def _(v, grp=grp, pe_c=pe_c):
            ctx.wait(v, "PE", pe_c)
            ctx.mark("DVE", v.tensor_copy(M2_im[:, grp * 512:(grp + 1) * 512],
                                          pS[:, 0:512]))

    for grp in range(8):
        @block.tensor
        def _(t, grp=grp):
            ctx.wait(t, "ACT", ctx.cnt["ACT"])
            ctx.wait(t, "DVE", ctx.cnt["DVE"])
            sl = slice(grp * 512, (grp + 1) * 512)
            mre = M2_re.ap()[:, sl]
            mim = M2_im.ap()[:, sl]
            t.matmul(pX[:, 0:512], P["d2_re"][:, :], mre, start=True, stop=False)
            t.matmul(pX[:, 0:512], P["d2_imn"][:, :], mim, start=False, stop=True)
            t.matmul(pX2[:, 0:512], P["d2_im"][:, :], mre, start=True, stop=False)
            ctx.mark("PE", t.matmul(pX2[:, 0:512], P["d2_re"][:, :], mim,
                                    start=False, stop=True))
        pe_c = ctx.cnt["PE"]
        @block.scalar
        def _(s, grp=grp, pe_c=pe_c):
            ctx.wait(s, "PE", pe_c)
            ctx.mark("ACT", s.activation(X_re[:, grp * 512:(grp + 1) * 512],
                                         pX[:, 0:512], AF.Copy))
        @block.vector
        def _(v, grp=grp, pe_c=pe_c):
            ctx.wait(v, "PE", pe_c)
            ctx.mark("DVE", v.tensor_copy(X_im[:, grp * 512:(grp + 1) * 512],
                                          pX2[:, 0:512]))


def phase1(nc, ctx, block, dt, P, sb, ps):
    import contextlib
    with contextlib.ExitStack() as st:
        sbl = lambda n, s_, d: st.enter_context(nc.sbuf_tensor(n, list(s_), d))
        psl = lambda n, s_, d=F32: st.enter_context(nc.psum_tensor(n, list(s_), d))

        winb = sbl("winb", (128, 8 * M3), BF16)
        binp = sbl("binp", (128, 24), F32)
        swp = sbl("swp", (128, 24 * 3), F32)
        sbp = sbl("sbp", (128, 24), F32)
        xb = sbl("xb", (128, 8 * L), BF16)
        u_sb = sbl("u_sb", (128, 2 * 2052), BF16)
        acc = sbl("acc_sb", (128, 2 * L), F32)
        x1h = sbl("x1h", (128, L), F32)
        stg = sbl("stg", (128, 2 * L), BF16)
        pu = [psl("p1_u0", (128, 512)), psl("p1_u1", (128, 512))]

        stg1 = st.enter_context(nc.sbuf_tensor("p1_cstg", [128, 2 * 2048], F32))
        @block.sync
        def _(sy):
            ctx.dma("cLD", sy, out=binp[:, :], in_=dt["bin_pp"])
            ctx.dma("cLD", sy, out=swp.ap().rearrange("p (c s) -> p c s", s=3), in_=dt["shortw_pp"])
            ctx.dma("cLD", sy, out=sbp[:, :], in_=dt["shortb_pp"])
        win_src = dt["W_in"].rearrange("(kc p) m -> p kc m", p=128)
        cl_marks = []
        for i in range(16):
            kc, h = i // 2, i % 2
            par = i % 2
            @block.sync
            def _(sy, i=i, kc=kc, h=h, par=par):
                if i >= 2:
                    ctx.wait(sy, "ACT", cl_marks[i - 2])
                ctx.dma("cLD", sy, out=stg1[:, par * 2048:par * 2048 + 1536],
                        in_=win_src[:, kc, h * 1536:(h + 1) * 1536])
            ldi = ctx.cnt["cLD"]
            @block.scalar
            def _(s_, i=i, kc=kc, h=h, par=par, ldi=ldi):
                ctx.wait(s_, "cLD", ldi)
                cl_marks.append(ctx.mark("ACT", s_.activation(
                    winb[:, kc * M3 + h * 1536:kc * M3 + (h + 1) * 1536],
                    stg1[:, par * 2048:par * 2048 + 1536], AF.Copy)))
        ld_w = (ctx.cnt["cLD"], ctx.cnt["ACT"])

        for b in range(BPC):
            x_src = dt["xfm"][b].rearrange("(kc p) t -> p kc t", p=128)
            pe_before = ctx.cnt["PE"]
            xm = []
            for i in range(8):
                par = i % 2
                @block.sync
                def _(sy, i=i, par=par, b=b, pe_before=pe_before):
                    if i >= 2:
                        ctx.wait(sy, "ACT", xm[i - 2])
                    else:
                        ctx.wait(sy, "ACT", ctx.cnt["ACT"])
                    if i == 0:
                        ctx.wait(sy, "PE", pe_before)
                    ctx.dma("cLD", sy, out=stg1[:, par * 2048:(par + 1) * 2048],
                            in_=x_src[:, i, :])
                ldi = ctx.cnt["cLD"]
                @block.scalar
                def _(s_, i=i, par=par, ldi=ldi, pe_before=pe_before):
                    ctx.wait(s_, "cLD", ldi)
                    if i == 0:
                        ctx.wait(s_, "PE", pe_before)
                    xm.append(ctx.mark("ACT", s_.activation(
                        xb[:, i * L:(i + 1) * L],
                        stg1[:, par * 2048:(par + 1) * 2048], AF.Copy)))
            xld = ctx.cnt["ACT"]

            order = list(range(8)) + [m for c in range(8) for m in (8 + c, 16 + c)]
            for oi, mc in enumerate(order):
                ub = (oi % 2)
                uof = ub * 2052
                aof = ub * L
                @block.vector
                def _(v, mc=mc, ub=ub, uof=uof, b=b, oi=oi):
                    ctx.wait(v, "DVE", ctx.cnt["DVE"])
                    ctx.mark("DVE", v.memset(u_sb[:, uof:uof + 2], 0.0))
                for tt in range(4):
                    pq = pu[tt % 2]
                    @block.tensor
                    def _(t, mc=mc, tt=tt, pq=pq, xld=xld, oi=oi, b=b):
                        if oi == 0 and tt == 0:
                            ctx.wait(t, "ACT", ld_w[1])
                            ctx.wait(t, "cLD", ld_w[0])
                            ctx.wait(t, "ACT", xld)
                        ctx.wait(t, "ACT", ctx.cnt["ACT"])
                        for kc in range(8):
                            inst = t.matmul(
                                pq[:, :], winb[:, (kc * 24 + mc) * 128:(kc * 24 + mc + 1) * 128],
                                xb[:, kc * L + tt * 512:kc * L + (tt + 1) * 512],
                                start=(kc == 0), stop=(kc == 7))
                        ctx.mark("PE", inst)
                    @block.scalar
                    def _(s_, mc=mc, tt=tt, pq=pq, uof=uof):
                        ctx.wait(s_, "PE", ctx.cnt["PE"])
                        ctx.wait(s_, "DVE", ctx.cnt["DVE"])
                        ctx.wait(s_, "cLD", ld_w[0])
                        ctx.mark("ACT", s_.activation(
                            u_sb[:, uof + 2 + tt * 512:uof + 2 + (tt + 1) * 512],
                            pq[:, :], AF.Copy))
                @block.vector
                def _(v, mc=mc, uof=uof, aof=aof):
                    ctx.wait(v, "ACT", ctx.cnt["ACT"])
                    ctx.wait(v, "DVE", ctx.cnt["DVE"])
                    ctx.mark("DVE", v.tensor_scalar(
                        u_sb[:, uof + 2:uof + 2050], u_sb[:, uof + 2:uof + 2050],
                        binp[:, mc:mc + 1], None, ALU.add))
                    ctx.wait(v, "DVE")
                    ctx.mark("DVE", v.tensor_scalar(
                        acc[:, aof:aof + L], u_sb[:, uof + 2:uof + 2050],
                        swp[:, mc * 3 + 2:mc * 3 + 3], sbp[:, mc:mc + 1],
                        ALU.mult, ALU.add))
                    ctx.wait(v, "DVE")
                    ctx.mark("DVE", v.scalar_tensor_tensor(
                        acc[:, aof:aof + L], u_sb[:, uof + 1:uof + 2049],
                        swp[:, mc * 3 + 1:mc * 3 + 2], acc[:, aof:aof + L],
                        ALU.mult, ALU.add))
                    ctx.wait(v, "DVE")
                    ctx.mark("DVE", v.scalar_tensor_tensor(
                        acc[:, aof:aof + L], u_sb[:, uof:uof + 2048],
                        swp[:, mc * 3 + 0:mc * 3 + 1], acc[:, aof:aof + L],
                        ALU.mult, ALU.add))
                dve_acc = ctx.cnt["DVE"]
                sof = (oi % 2) * L
                if mc < 8:
                    @block.scalar
                    def _(s_, aof=aof, sof=sof):
                        ctx.wait(s_, "DVE", dve_acc)
                        ctx.wait(s_, "cST", ctx.cnt["cST"])
                        ctx.mark("ACT", s_.activation(stg[:, sof:sof + L],
                                                      acc[:, aof:aof + L], AF.Copy))
                    @block.sync
                    def _(sy, b=b, mc=mc, sof=sof):
                        ctx.wait(sy, "ACT", ctx.cnt["ACT"])
                        ctx.dma("cST", sy, out=dt["x0_sc"][b, mc * 128:(mc + 1) * 128, :],
                            in_=stg[:, sof:sof + L])
                elif mc < 16:
                    @block.vector
                    def _(v, aof=aof):
                        ctx.wait(v, "DVE")
                        ctx.mark("DVE", v.tensor_copy(x1h[:, :], acc[:, aof:aof + L]))
                else:
                    @block.vector
                    def _(v, aof=aof, sof=sof):
                        ctx.wait(v, "DVE")
                        ctx.wait(v, "cST", ctx.cnt["cST"])
                        ctx.mark("DVE", v.tensor_tensor(stg[:, sof:sof + L],
                                                        acc[:, aof:aof + L],
                                                        x1h[:, :], ALU.mult))
                    @block.sync
                    def _(sy, b=b, mc=mc, sof=sof):
                        ctx.wait(sy, "DVE", ctx.cnt["DVE"])
                        ctx.dma("cST", sy, out=dt["v_sc"][b, (mc - 16) * 128:(mc - 15) * 128, :],
                            in_=stg[:, sof:sof + L])


def phase2(nc, ctx, block, dt, P, sb, ps):
    import contextlib
    with contextlib.ExitStack() as st:
        sbl = lambda n, s_, d: st.enter_context(nc.sbuf_tensor(n, list(s_), d))
        psl = lambda n, s_, d=F32: st.enter_context(nc.psum_tensor(n, list(s_), d))

        for _n in ("d1tw_re", "d1tw_im", "d1tw_imn"):
            P["_" + _n] = sbl("p2" + _n, (64, 32 * 128), BF16)
        vre = sbl("p2_vre", (128, L), BF16)
        vim = sbl("p2_vim", (128, L), BF16)
        x0re = sbl("p2_x0re", (128, L), BF16)
        x0im = sbl("p2_x0im", (128, L), BF16)
        gre = sbl("p2_gre", (128, 32 * 128), BF16)
        gim = sbl("p2_gim", (128, 32 * 128), BF16)
        U_re = sbl("p2_ure", (64, 32 * 128), BF16)
        U_im = sbl("p2_uim", (64, 32 * 128), BF16)
        S2_re = sbl("p2_s2re", (128, 32 * 128), BF16)
        S2_im = sbl("p2_s2im", (128, 32 * 128), BF16)
        M2_re = sbl("p2_m2re", (128, 32 * 128), BF16)
        M2_im = sbl("p2_m2im", (128, 32 * 128), BF16)
        X_re = sbl("p2_xre", (128, 32 * 128), BF16)
        X_im = sbl("p2_xim", (128, 32 * 128), BF16)
        Y_re = sbl("p2_yre", (128, 32 * 128), BF16)
        Y_im = sbl("p2_yim", (128, 32 * 128), BF16)
        Bt_re = sbl("p2_btre", (128, 256), BF16)
        Bt_im = sbl("p2_btim", (128, 256), BF16)
        Bm_re = sbl("p2_bmre", (128, 256), F32)
        Bm_im = sbl("p2_bmim", (128, 256), F32)
        Bp_re = sbl("p2_bpre", (128, 32 * 128), BF16)
        Bp_im = sbl("p2_bpim", (128, 32 * 128), BF16)
        Yb_re = sbl("p2_ybre", (64, 32 * 128), BF16)
        Yb_im = sbl("p2_ybim", (64, 32 * 128), BF16)
        yg = sbl("p2_yg", (128, 512), F32)
        stg_re = sbl("p2_stgre", (128, L), BF16)
        stg_im = sbl("p2_stgim", (128, L), BF16)
        fbias = sbl("p2_fbias", (128, 8), F32)
        twi2_re = sbl("p2_twi2re", (128, 256), F32)
        twi2_im = sbl("p2_twi2im", (128, 256), F32)

        pT = psl("p2_T", (128, 512))
        pS = psl("p2_S", (128, 512))
        pS2 = psl("p2_S2", (128, 512))
        pX = psl("p2_X", (128, 512))
        pX2 = psl("p2_X2", (128, 512))

        @block.sync
        def _(sy):
            ctx.dma("cLD", sy, out=fbias[:, :], in_=dt["fbias_pp"])
            for _n in ("d1tw_re", "d1tw_im", "d1tw_imn"):
                ctx.dma("cLD", sy, out=P["_" + _n].ap().rearrange(
                    "p (t2 k) -> p t2 k", t2=32),
                    in_=dt[_n].rearrange("t2 p k -> p t2 k"))
        @block.vector
        def _(v):
            ctx.wait(v, "cLD", ctx.cnt["cLD"])
            for h in range(2):
                v.tensor_copy(twi2_re[:, h * 128:(h + 1) * 128], P["twi_re"][:, :])
                ctx.mark("DVE", v.tensor_copy(twi2_im[:, h * 128:(h + 1) * 128],
                                              P["twi_im"][:, :]))

        for pair in range(2):
            b0, b1 = 2 * pair, 2 * pair + 1
            for dc in range(8):
                csl = slice(dc * 128, (dc + 1) * 128)
                @block.sync
                def _(sy, b0=b0, b1=b1, dc=dc, csl=csl):
                    ctx.wait(sy, "PE", ctx.cnt["PE"])
                    ctx.wait(sy, "DVE", ctx.cnt["DVE"])
                    ctx.dma("cLD", sy, out=vre[:, :], in_=dt["v_sc"][b0, csl, :])
                    ctx.dma("cLD", sy, out=vim[:, :], in_=dt["v_sc"][b1, csl, :])
                    ctx.dma("cLD", sy, out=x0re[:, :], in_=dt["x0_sc"][b0, csl, :])
                    ctx.dma("cLD", sy, out=x0im[:, :], in_=dt["x0_sc"][b1, csl, :])
                    ctx.dma("cLD", sy, out=gre.ap().rearrange("p (j k) -> p j k", j=32),
                            in_=dt["gf_sc"][dc, :, 0, :, :].rearrange("j p k -> p j k"))
                    ctx.dma("cLD", sy, out=gim.ap().rearrange("p (j k) -> p j k", j=32),
                            in_=dt["gf_sc"][dc, :, 1, :, :].rearrange("j p k -> p j k"))
                ld_u = ctx.cnt["cLD"]

                @block.tensor
                def _(t, ld_u=ld_u):
                    ctx.wait(t, "cLD", ld_u)

                fwd_fft(nc, ctx, block, P, vre, vim, U_re, U_im, S2_re, S2_im,
                        M2_re, M2_im, X_re, X_im, pT, pS, pS2, pX, pX2)

                for grp in range(8):
                    sl = slice(grp * 512, (grp + 1) * 512)
                    @block.vector
                    def _(v, sl=sl, ld_u=ld_u):
                        ctx.wait(v, "cLD", ld_u)
                        ctx.wait(v, "ACT", ctx.cnt["ACT"])
                        ctx.wait(v, "DVE", ctx.cnt["DVE"])
                        ctx.mark("DVE", v.tensor_tensor(Y_re[:, sl], X_re[:, sl],
                                                        gre[:, sl], ALU.mult))
                        ctx.mark("DVE", v.tensor_tensor(Y_im[:, sl], X_im[:, sl],
                                                        gim[:, sl], ALU.mult))
                        ctx.wait(v, "DVE")
                        ctx.mark("DVE", v.tensor_tensor(Y_re[:, sl], Y_re[:, sl],
                                                        Y_im[:, sl], ALU.subtract))
                        ctx.wait(v, "DVE")
                        ctx.mark("DVE", v.tensor_tensor(Y_im[:, sl], X_re[:, sl],
                                                        gim[:, sl], ALU.mult))
                        ctx.wait(v, "DVE")
                        ctx.mark("DVE", v.tensor_tensor(yg[:, 0:512], X_im[:, sl],
                                                        gre[:, sl], ALU.mult))
                        ctx.wait(v, "DVE")
                        ctx.mark("DVE", v.tensor_tensor(Y_im[:, sl], Y_im[:, sl],
                                                        yg[:, 0:512], ALU.add))

                for pr in range(16):
                    @block.tensor
                    def _(t, pr=pr):
                        ctx.wait(t, "DVE", ctx.cnt["DVE"])
                        ctx.wait(t, "ACT", ctx.cnt["ACT"])
                        for jj in range(2):
                            j2 = pr * 2 + jj
                            ysl = slice(j2 * 128, (j2 + 1) * 128)
                            po = slice(jj * 128, (jj + 1) * 128)
                            t.matmul(pS[:, po], Y_re[:, ysl], P["d2i_re"][:, :],
                                     start=True, stop=False)
                            t.matmul(pS[:, po], Y_im[:, ysl], P["d2i_imn"][:, :],
                                     start=False, stop=True)
                            t.matmul(pS2[:, po], Y_im[:, ysl], P["d2i_re"][:, :],
                                     start=True, stop=False)
                            ctx.mark("PE", t.matmul(pS2[:, po], Y_re[:, ysl],
                                                    P["d2i_im"][:, :],
                                                    start=False, stop=True))
                    pe_c = ctx.cnt["PE"]
                    @block.scalar
                    def _(s_, pr=pr, pe_c=pe_c):
                        ctx.wait(s_, "PE", pe_c)
                        ctx.wait(s_, "DVE", ctx.cnt["DVE"])
                        ctx.mark("ACT", s_.activation(Bt_re[:, :], pS[:, 0:256], AF.Copy))
                        ctx.mark("ACT", s_.activation(Bt_im[:, :], pS2[:, 0:256], AF.Copy))
                    act_c = ctx.cnt["ACT"]
                    @block.vector
                    def _(v, pr=pr, act_c=act_c):
                        ctx.wait(v, "ACT", act_c)
                        psl_ = slice(pr * 256, (pr + 1) * 256)
                        ctx.mark("DVE", v.tensor_tensor(Bm_re[:, :], Bt_re[:, :],
                                                        twi2_re[:, :], ALU.mult))
                        ctx.mark("DVE", v.tensor_tensor(Bm_im[:, :], Bt_im[:, :],
                                                        twi2_im[:, :], ALU.mult))
                        ctx.wait(v, "DVE")
                        ctx.mark("DVE", v.tensor_tensor(Bp_re[:, psl_], Bm_re[:, :],
                                                        Bm_im[:, :], ALU.subtract))
                        ctx.wait(v, "DVE")
                        ctx.mark("DVE", v.tensor_tensor(Bm_re[:, :], Bt_re[:, :],
                                                        twi2_im[:, :], ALU.mult))
                        ctx.mark("DVE", v.tensor_tensor(Bm_im[:, :], Bt_im[:, :],
                                                        twi2_re[:, :], ALU.mult))
                        ctx.wait(v, "DVE")
                        ctx.mark("DVE", v.tensor_tensor(Bp_im[:, psl_], Bm_re[:, :],
                                                        Bm_im[:, :], ALU.add))

                for grp in range(8):
                    @block.tensor
                    def _(t, grp=grp):
                        ctx.wait(t, "DVE", ctx.cnt["DVE"])
                        ctx.wait(t, "ACT", ctx.cnt["ACT"])
                        sl = slice(grp * 512, (grp + 1) * 512)
                        bre = Bp_re.ap()[:, sl]
                        bim = Bp_im.ap()[:, sl]
                        t.matmul(pX[0:64, 0:512], P["d1i_re"][:, :], bre,
                                 start=True, stop=False)
                        t.matmul(pX[0:64, 0:512], P["d1i_imn"][:, :], bim,
                                 start=False, stop=True)
                        t.matmul(pX2[0:64, 0:512], P["d1i_im"][:, :], bre,
                                 start=True, stop=False)
                        ctx.mark("PE", t.matmul(pX2[0:64, 0:512], P["d1i_re"][:, :],
                                                bim, start=False, stop=True))
                    pe_c = ctx.cnt["PE"]
                    @block.scalar
                    def _(s_, grp=grp, pe_c=pe_c):
                        ctx.wait(s_, "PE", pe_c)
                        ctx.mark("ACT", s_.activation(Yb_re[0:64, grp * 512:(grp + 1) * 512],
                                                      pX[0:64, 0:512], AF.Copy))
                    @block.vector
                    def _(v, grp=grp, pe_c=pe_c):
                        ctx.wait(v, "PE", pe_c)
                        ctx.mark("DVE", v.tensor_copy(Yb_im[0:64, grp * 512:(grp + 1) * 512],
                                                      pX2[0:64, 0:512]))

                for mg in range(4):
                    @block.tensor
                    def _(t, mg=mg):
                        ctx.wait(t, "ACT", ctx.cnt["ACT"])
                        ctx.wait(t, "DVE", ctx.cnt["DVE"])
                        for ml in range(8):
                            m2 = mg * 8 + ml
                            sel_re = Yb_re.ap().rearrange(
                                "p (j c m) -> p m (j c)", c=4, m=32)[:, m2, :]
                            sel_im = Yb_im.ap().rearrange(
                                "p (j c m) -> p m (j c)", c=4, m=32)[:, m2, :]
                            t.matmul(pT[:, ml * 64:(ml + 1) * 64], sel_re,
                                     P["ident64"][:, :], start=True, stop=True)
                            ctx.mark("PE", t.matmul(pS[:, ml * 64:(ml + 1) * 64],
                                                    sel_im, P["ident64"][:, :],
                                                    start=True, stop=True))
                    pe_c = ctx.cnt["PE"]
                    for pi, (pp, x0b, stgb) in enumerate(
                            ((pT, x0re, stg_re), (pS, x0im, stg_im))):
                        @block.vector
                        def _(v, mg=mg, pp=pp, x0b=x0b, stgb=stgb, pe_c=pe_c, dc=dc):
                            ctx.wait(v, "PE", pe_c)
                            ctx.wait(v, "cST", ctx.cnt["cST"])
                            ctx.wait(v, "DVE")
                            ctx.mark("DVE", v.tensor_scalar(
                                yg[:, 0:512], pp[:, 0:512], fbias[:, dc:dc + 1],
                                None, ALU.add))
                            ctx.wait(v, "DVE")
                            xv = x0b.ap().rearrange(
                                "p (T mg ml) -> p mg ml T", T=64, mg=4)[:, mg]
                            sv = stgb.ap().rearrange(
                                "p (T mg ml) -> p mg ml T", T=64, mg=4)[:, mg]
                            ctx.mark("DVE", v.tensor_tensor(sv, yg[:, 0:512], xv,
                                                            ALU.mult))
                @block.sync
                def _(sy, b0=b0, b1=b1, csl=csl):
                    ctx.wait(sy, "DVE", ctx.cnt["DVE"])
                    ctx.dma("cST", sy, out=dt["yx0_sc"][b0, csl, :], in_=stg_re[:, :])
                    ctx.dma("cST", sy, out=dt["yx0_sc"][b1, csl, :], in_=stg_im[:, :])


def phase3(nc, ctx, block, dt, P, sb, ps):
    import contextlib
    with contextlib.ExitStack() as st:
        sbl = lambda n, s_, d: st.enter_context(nc.sbuf_tensor(n, list(s_), d))
        psl = lambda n, s_, d=F32: st.enter_context(nc.psum_tensor(n, list(s_), d))

        woutb = sbl("p3_wout", (128, 8 * D), BF16)
        boutp = sbl("p3_bout", (128, 8), F32)
        gwf = sbl("p3_gwf", (128, 8 * 16), F32)
        gwh = sbl("p3_gwh", (128, 8 * 16), BF16)
        gwhf = sbl("p3_gwhf", (128, 8 * 16), F32)
        gwl = sbl("p3_gwl", (128, 8 * 16), BF16)
        ebias = sbl("p3_ebias", (16, 1), F32)
        yx0b = sbl("p3_yx0", (128, 8 * L), BF16)
        r_sb = sbl("p3_r", (128, 8 * L), BF16)
        xpbb = sbl("p3_xpb", (128, 8 * L), BF16)
        xr = sbl("p3_xr", (128, 2 * 512), F32)
        r2t = sbl("p3_r2t", (128, 2 * 512), BF16)
        rm = sbl("p3_rm", (1, 512), F32)
        rv = sbl("p3_rv", (1, 512), F32)
        rw = sbl("p3_rw", (1, 512), F32)
        m_b = sbl("p3_mb", (128, L), BF16)
        rs_b = sbl("p3_rsb", (128, L), BF16)
        t1t = sbl("p3_t1", (128, 512), BF16)
        xp32 = sbl("p3_xp32", (128, 2 * 512), F32)
        xlot = sbl("p3_xlo", (128, 512), BF16)
        bia = sbl("p3_bia", (16, L), F32)
        sc_sb = sbl("p3_sc", (16, L), F32)
        biaT = sbl("p3_biaT", (128, 256), F32)
        scT = sbl("p3_scT", (128, 256), F32)
        m1c = sbl("p3_m1c", (128, 8), F32)
        eq1 = sbl("p3_eq1", (128, 16), F32)
        neg = sbl("p3_neg", (128, 16), F32)
        eq2 = sbl("p3_eq2", (128, 16), F32)
        msk = sbl("p3_msk", (128, 16), F32)
        num = sbl("p3_num", (128, 16), F32)
        gwt = sbl("p3_gwt", (128, 256), F32)
        gem = bia

        _pw = psl("p3_w0", (128, 512))
        pW = [_pw, _pw]
        pst = psl("p3_st", (64, 512))
        plog = psl("p3_log", (16, 2048))
        ptok = psl("p3_tok", (128, 256))
        pgw = psl("p3_gw", (16, 512))
        pbc = _pw

        @block.sync
        def _(sy):
            ctx.dma("cLD", sy, out=boutp[:, :], in_=dt["bout_pp"])
            ctx.dma("cLD", sy, out=ebias[:, :], in_=dt["expert_bias"][:, None])
            ctx.dma("cLD", sy, out=gwf.ap().rearrange("p (kc e) -> p kc e", kc=8),
                    in_=dt["gate_wT"].rearrange("(kc p) e -> p kc e", p=128))
        stg3 = st.enter_context(nc.sbuf_tensor("p3_cstg", [128, 2 * 1024], F32))
        wout_src = dt["W_out"].rearrange("(kc p) m -> p kc m", p=128)
        wm3 = []
        for i in range(8):
            par = i % 2
            @block.sync
            def _(sy, i=i, par=par):
                if i >= 2:
                    ctx.wait(sy, "ACT", wm3[i - 2])
                ctx.dma("cLD", sy, out=stg3[:, par * 1024:(par + 1) * 1024],
                        in_=wout_src[:, i, :])
            ldi = ctx.cnt["cLD"]
            @block.scalar
            def _(s_, i=i, par=par, ldi=ldi):
                ctx.wait(s_, "cLD", ldi)
                wm3.append(ctx.mark("ACT", s_.activation(
                    woutb[:, i * D:(i + 1) * D],
                    stg3[:, par * 1024:(par + 1) * 1024], AF.Copy)))
        ldw = (ctx.cnt["cLD"], ctx.cnt["ACT"])

        @block.vector
        def _(v):
            ctx.wait(v, "cLD", ldw[0])
            ctx.mark("DVE", v.tensor_copy(gwh[:, :], gwf[:, :]))
            ctx.wait(v, "DVE")
            ctx.mark("DVE", v.tensor_copy(gwhf[:, :], gwh[:, :]))
            ctx.wait(v, "DVE")
            ctx.mark("DVE", v.tensor_tensor(gwl[:, :], gwf[:, :], gwhf[:, :],
                                            ALU.subtract))

        for b in range(BPC):
            @block.sync
            def _(sy, b=b):
                ctx.wait(sy, "PE", ctx.cnt["PE"])
                ctx.dma("cLD", sy, out=yx0b.ap().rearrange("p (kc t) -> p kc t", kc=8),
                        in_=dt["yx0_sc"][b].rearrange("(kc p) t -> p kc t", p=128))
            ldy = ctx.cnt["cLD"]

            for dout in range(8):
                for tt in range(4):
                    par = (dout * 4 + tt) % 2
                    pq = pW[par]
                    @block.sync
                    def _(sy, b=b, dout=dout, tt=tt, par=par):
                        ctx.wait(sy, "DVE", ctx.cnt["DVE"])
                        ctx.dma("cLD", sy, out=xr[:, par * 512:(par + 1) * 512],
                                in_=dt["xfm"][b, dout * 128:(dout + 1) * 128,
                                              tt * 512:(tt + 1) * 512])
                    ldx = ctx.cnt["cLD"]
                    @block.tensor
                    def _(t, dout=dout, tt=tt, pq=pq, ldy=ldy):
                        ctx.wait(t, "cLD", ldy)
                        ctx.wait(t, "ACT", ldw[1])
                        ctx.wait(t, "DVE", ctx.cnt["DVE"])
                        for kc in range(8):
                            inst = t.matmul(
                                pq[:, :],
                                woutb[:, (kc * 8 + dout) * 128:(kc * 8 + dout + 1) * 128],
                                yx0b[:, kc * L + tt * 512:kc * L + (tt + 1) * 512],
                                start=(kc == 0), stop=(kc == 7))
                        ctx.mark("PE", inst)
                    @block.vector
                    def _(v, dout=dout, tt=tt, pq=pq, par=par, ldx=ldx):
                        ctx.wait(v, "PE", ctx.cnt["PE"])
                        ctx.wait(v, "cLD", ldx)
                        ctx.wait(v, "DVE")
                        ctx.mark("DVE", v.tensor_scalar(
                            xp32[:, 0:512], pq[:, :], boutp[:, dout:dout + 1],
                            None, ALU.add))
                        ctx.wait(v, "DVE")
                        ctx.mark("DVE", v.tensor_tensor(
                            r_sb[:, dout * L + tt * 512:dout * L + (tt + 1) * 512],
                            xp32[:, 0:512], xr[:, par * 512:(par + 1) * 512],
                            ALU.add))

            for tt in range(4):
                tsl = slice(tt * 512, (tt + 1) * 512)
                for kc in range(8):
                    par = kc % 2
                    @block.vector
                    def _(v, kc=kc, tt=tt, par=par):
                        ctx.wait(v, "DVE")
                        ctx.wait(v, "PE", max(0, ctx.cnt["PE"] - 1))
                        ctx.mark("DVE", v.tensor_tensor(
                            r2t[:, par * 512:(par + 1) * 512],
                            r_sb[:, kc * L + tt * 512:kc * L + (tt + 1) * 512],
                            r_sb[:, kc * L + tt * 512:kc * L + (tt + 1) * 512],
                            ALU.mult))
                    @block.tensor
                    def _(t, kc=kc, tt=tt, par=par):
                        ctx.wait(t, "DVE", ctx.cnt["DVE"])
                        t.matmul(pst[:, :], P["stat_a"][:, :],
                                 r_sb[:, kc * L + tt * 512:kc * L + (tt + 1) * 512],
                                 start=(kc == 0), stop=False)
                        ctx.mark("PE", t.matmul(
                            pst[:, :], P["stat_b"][:, :],
                            r2t[:, par * 512:(par + 1) * 512],
                            start=False, stop=(kc == 7)))
                @block.vector
                def _(v, tt=tt):
                    ctx.wait(v, "PE", ctx.cnt["PE"])
                    ctx.wait(v, "ACT", ctx.cnt["ACT"])
                    ctx.mark("DVE", v.tensor_scalar_mul(rm[:, :], pst[0:1, :],
                                                        float(1.0 / D)))
                    ctx.mark("DVE", v.tensor_scalar_mul(rv[:, :], pst[32:33, :],
                                                        float(1.0 / D)))
                    ctx.wait(v, "DVE")
                    ctx.mark("DVE", v.tensor_tensor(rw[:, :], rm[:, :], rm[:, :],
                                                    ALU.mult))
                    ctx.wait(v, "DVE")
                    ctx.mark("DVE", v.tensor_tensor(rv[:, :], rv[:, :], rw[:, :],
                                                    ALU.subtract))
                    ctx.wait(v, "DVE")
                    ctx.mark("DVE", v.tensor_scalar_add(rv[:, :], rv[:, :],
                                                        float(1e-5)))
                @block.scalar
                def _(s_, tt=tt):
                    ctx.wait(s_, "DVE", ctx.cnt["DVE"])
                    ctx.mark("ACT", s_.activation(rv[:, :], rv[:, :], AF.Sqrt))
                @block.vector
                def _(v, tt=tt):
                    ctx.wait(v, "ACT", ctx.cnt["ACT"])
                    ctx.mark("DVE", v.reciprocal(rv[:, :], rv[:, :]))
                @block.tensor
                def _(t, tt=tt, tsl=tsl):
                    ctx.wait(t, "DVE", ctx.cnt["DVE"])
                    ctx.wait(t, "ACT", ctx.cnt["ACT"])
                    ctx.mark("PE", t.matmul(pbc[:, 0:512], P["ones1x128f"][:, :],
                                            rm[:, :], start=True, stop=True))
                @block.scalar
                def _(s_, tsl=tsl):
                    ctx.wait(s_, "PE", ctx.cnt["PE"])
                    ctx.mark("ACT", s_.activation(m_b[:, tsl], pbc[:, 0:512], AF.Copy))
                @block.tensor
                def _(t, tsl=tsl):
                    ctx.wait(t, "ACT", ctx.cnt["ACT"])
                    ctx.mark("PE", t.matmul(pbc[:, 0:512], P["ones1x128f"][:, :],
                                            rv[:, :], start=True, stop=True))
                @block.scalar
                def _(s_, tsl=tsl):
                    ctx.wait(s_, "PE", ctx.cnt["PE"])
                    ctx.mark("ACT", s_.activation(rs_b[:, tsl], pbc[:, 0:512], AF.Copy))
            bc_done = (ctx.cnt["ACT"], ctx.cnt["DVE"])

            for kc in range(8):
                for tt in range(4):
                    par = tt % 2
                    rsl = slice(kc * L + tt * 512, kc * L + (tt + 1) * 512)
                    tsl = slice(tt * 512, (tt + 1) * 512)
                    @block.vector
                    def _(v, kc=kc, tt=tt, par=par, rsl=rsl, tsl=tsl):
                        ctx.wait(v, "ACT", ctx.cnt["ACT"])
                        ctx.wait(v, "DVE")
                        ctx.mark("DVE", v.tensor_tensor(t1t[:, :], r_sb[:, rsl],
                                                        m_b[:, tsl], ALU.subtract))
                        ctx.wait(v, "DVE")
                        ctx.wait(v, "cST", ctx.cnt["cST"])
                        ctx.mark("DVE", v.tensor_tensor(
                            xp32[:, par * 512:(par + 1) * 512], t1t[:, :],
                            rs_b[:, tsl], ALU.mult))
                    @block.scalar
                    def _(s_, kc=kc, tt=tt, par=par):
                        ctx.wait(s_, "DVE", ctx.cnt["DVE"])
                        ctx.mark("ACT", s_.activation(
                            xpbb[:, kc * L + tt * 512:kc * L + (tt + 1) * 512],
                            xp32[:, par * 512:(par + 1) * 512], AF.Copy))
                    @block.vector
                    def _(v, kc=kc, tt=tt, par=par):
                        ctx.wait(v, "ACT", ctx.cnt["ACT"])
                        ctx.wait(v, "PE", ctx.cnt["PE"])
                        ctx.wait(v, "DVE")
                        ctx.mark("DVE", v.tensor_tensor(
                            xlot[:, :], xp32[:, par * 512:(par + 1) * 512],
                            xpbb[:, kc * L + tt * 512:kc * L + (tt + 1) * 512],
                            ALU.subtract))
                    @block.sync
                    def _(sy, b=b, kc=kc, tt=tt, par=par):
                        ctx.wait(sy, "DVE", ctx.cnt["DVE"])
                        ctx.dma("cST", sy,
                                out=dt["xP_sc"][b, kc * 128:(kc + 1) * 128,
                                                tt * 512:(tt + 1) * 512],
                                in_=xp32[:, par * 512:(par + 1) * 512])
                    @block.tensor
                    def _(t, kc=kc, tt=tt):
                        ctx.wait(t, "DVE", ctx.cnt["DVE"])
                        gsl = slice(kc * 16, (kc + 1) * 16)
                        first = (kc == 0)
                        t.matmul(plog[:, tt * 512:(tt + 1) * 512], gwh[:, gsl],
                                 xpbb[:, kc * L + tt * 512:kc * L + (tt + 1) * 512],
                                 start=first, stop=False)
                        t.matmul(plog[:, tt * 512:(tt + 1) * 512], gwl[:, gsl],
                                 xpbb[:, kc * L + tt * 512:kc * L + (tt + 1) * 512],
                                 start=False, stop=False)
                        ctx.mark("PE", t.matmul(
                            plog[:, tt * 512:(tt + 1) * 512], gwh[:, gsl],
                            xlot[:, :], start=False, stop=(kc == 7)))
            @block.sync
            def _(sy, b=b):
                ctx.wait(sy, "ACT", ctx.cnt["ACT"])
                ctx.dma("cST", sy,
                        out=dt["xPb_sc"][b].rearrange("(kc p) t -> p kc t", p=128),
                        in_=xpbb.ap().rearrange("p (kc t) -> p kc t", kc=8))

            @block.vector
            def _(v):
                ctx.wait(v, "PE", ctx.cnt["PE"])
                ctx.wait(v, "DVE")
                ctx.mark("DVE", v.tensor_scalar(bia[:, :], plog[:, :],
                                                ebias[:, 0:1], None, ALU.add))
            @block.scalar
            def _(s_):
                ctx.wait(s_, "PE", ctx.cnt["PE"])
                ctx.mark("ACT", s_.activation(sc_sb[:, :], plog[:, :], AF.Sigmoid))
            for sg in range(4):
                @block.tensor
                def _(t, sg=sg):
                    ctx.wait(t, "DVE", ctx.cnt["DVE"])
                    ctx.wait(t, "ACT", ctx.cnt["ACT"])
                    for q in range(4):
                        sl4 = sg * 4 + q
                        ctx.mark("PE", t.matmul(
                            ptok[:, q * 16:(q + 1) * 16],
                            bia[:, sl4 * 128:(sl4 + 1) * 128],
                            P["identf128"][0:16, 0:16], start=True, stop=True))
                @block.vector
                def _(v, sg=sg):
                    ctx.wait(v, "PE", ctx.cnt["PE"])
                    ctx.mark("DVE", v.tensor_copy(biaT[:, sg * 64:(sg + 1) * 64],
                                                  ptok[:, 0:64]))
                @block.tensor
                def _(t, sg=sg):
                    ctx.wait(t, "DVE", ctx.cnt["DVE"])
                    for q in range(4):
                        sl4 = sg * 4 + q
                        ctx.mark("PE", t.matmul(
                            ptok[:, q * 16:(q + 1) * 16],
                            sc_sb[:, sl4 * 128:(sl4 + 1) * 128],
                            P["identf128"][0:16, 0:16], start=True, stop=True))
                @block.vector
                def _(v, sg=sg):
                    ctx.wait(v, "PE", ctx.cnt["PE"])
                    ctx.mark("DVE", v.tensor_copy(scT[:, sg * 64:(sg + 1) * 64],
                                                  ptok[:, 0:64]))
            for sl4 in range(16):
                esl = slice(sl4 * 16, (sl4 + 1) * 16)
                @block.vector
                def _(v, sl4=sl4, esl=esl):
                    ctx.wait(v, "DVE")
                    ctx.mark("DVE", v.reduce_max(m1c[:, 0:1], biaT[:, esl],
                                                 mybir.AxisListType.X))
                    ctx.wait(v, "DVE")
                    ctx.mark("DVE", v.tensor_scalar(eq1[:, :], biaT[:, esl],
                                                    m1c[:, 0:1], None, ALU.is_equal))
                    ctx.wait(v, "DVE")
                    ctx.mark("DVE", v.scalar_tensor_tensor(
                        neg[:, :], eq1[:, :], -1e30, biaT[:, esl],
                        ALU.mult, ALU.add))
                    ctx.wait(v, "DVE")
                    ctx.mark("DVE", v.reduce_max(m1c[:, 1:2], neg[:, :],
                                                 mybir.AxisListType.X))
                    ctx.wait(v, "DVE")
                    ctx.mark("DVE", v.tensor_scalar(eq2[:, :], neg[:, :],
                                                    m1c[:, 1:2], None, ALU.is_equal))
                    ctx.wait(v, "DVE")
                    ctx.mark("DVE", v.tensor_tensor(msk[:, :], eq1[:, :],
                                                    eq2[:, :], ALU.add))
                    ctx.wait(v, "DVE")
                    ctx.mark("DVE", v.tensor_tensor(num[:, :], scT[:, esl],
                                                    msk[:, :], ALU.mult))
                    ctx.wait(v, "DVE")
                    ctx.mark("DVE", v.reduce_sum(m1c[:, 2:3], num[:, :],
                                                 mybir.AxisListType.X))
                    ctx.wait(v, "DVE")
                    ctx.mark("DVE", v.tensor_scalar_add(m1c[:, 3:4], m1c[:, 2:3],
                                                        float(1e-10)))
                    ctx.wait(v, "DVE")
                    ctx.mark("DVE", v.reciprocal(m1c[:, 4:5], m1c[:, 3:4]))
                    ctx.wait(v, "DVE")
                    ctx.mark("DVE", v.tensor_scalar(gwt[:, esl], num[:, :],
                                                    m1c[:, 4:5], None, ALU.mult))
            for sg in range(4):
                @block.tensor
                def _(t, sg=sg):
                    ctx.wait(t, "DVE", ctx.cnt["DVE"])
                    ctx.wait(t, "ACT", ctx.cnt["ACT"])
                    for q in range(4):
                        sl4 = sg * 4 + q
                        ctx.mark("PE", t.matmul(
                            pgw[:, q * 128:(q + 1) * 128],
                            gwt[:, sl4 * 16:(sl4 + 1) * 16],
                            P["identf128"][:, :], start=True, stop=True))
                @block.scalar
                def _(s_, sg=sg):
                    ctx.wait(s_, "PE", ctx.cnt["PE"])
                    ctx.wait(s_, "DVE", ctx.cnt["DVE"])
                    ctx.mark("ACT", s_.activation(gem[:, sg * 512:(sg + 1) * 512],
                                                  pgw[:, :], AF.Copy))
            @block.sync
            def _(sy, b=b):
                ctx.wait(sy, "ACT", ctx.cnt["ACT"])
                ctx.dma("cST", sy, out=dt["gatew_sc"][:, b * L:(b + 1) * L],
                        in_=gem[:, :])


def phase4(nc, ctx, block, dt, P, sb, ps):
    import contextlib
    with contextlib.ExitStack() as st:
        sbl = lambda n, s_, d: st.enter_context(nc.sbuf_tensor(n, list(s_), d))
        psl = lambda n, s_, d=F32: st.enter_context(nc.psum_tensor(n, list(s_), d))

        xpa = sbl("p4_xpa", (128, 8 * TT), BF16)
        sb1p = sbl("p4_sb1", (128, 8), F32)
        sb3p = sbl("p4_sb3", (128, 8), F32)
        sb2p = sbl("p4_sb2", (128, 8), F32)
        satile = sbl("p4_sa", (128, 2 * 512), BF16)
        b1p = sbl("p4_b1p", (128, 16 * 4), F32)
        b3p = sbl("p4_b3p", (128, 16 * 4), F32)
        stg4 = st.enter_context(nc.sbuf_tensor("p4_cstg", [128, 2 * 1024], F32))
        stsh = contextlib.ExitStack()
        sbl2 = lambda n, s_, d: stsh.enter_context(nc.sbuf_tensor(n, list(s_), d))
        sw1b = sbl2("p4_sw1", (128, 8 * D), BF16)
        sw3b = sbl2("p4_sw3", (128, 8 * D), BF16)
        sw2b = sbl2("p4_sw2", (128, 8 * D), BF16)
        hsb = sbl2("p4_hs", (128, 8 * 256), BF16)
        ysst = sbl2("p4_ysst", (128, 2048), BF16)

        pa = psl("p4_a", (128, 1024))
        pb = psl("p4_b", (128, 1024))
        pys = psl("p4_ys", (128, 2048))

        @block.sync
        def _(sy):
            ctx.dma("cLD", sy, out=sb1p[:, :], in_=dt["sb1_pp"])
            ctx.dma("cLD", sy, out=sb3p[:, :], in_=dt["sb3_pp"])
            ctx.dma("cLD", sy, out=sb2p[:, :], in_=dt["sb2_pp"])
            ctx.dma("cLD", sy, out=b1p.ap().rearrange("p (e c) -> p e c", e=16),
                    in_=dt["b1e_pp"])
            ctx.dma("cLD", sy, out=b3p.ap().rearrange("p (e c) -> p e c", e=16),
                    in_=dt["b3e_pp"])
        @block.sync
        def _(sy):
            for b in range(BPC):
                ctx.dma("cLD", sy, out=xpa.ap().rearrange(
                    "p (kc bb t) -> p kc bb t", kc=8, bb=BPC)[:, :, b, :],
                    in_=dt["xPb_sc"][b].rearrange("(kc p) t -> p kc t", p=128))
        swm = []
        for wi, (nm, dstb) in enumerate((("sw1T", sw1b), ("sw3T", sw3b),
                                         ("sw2T", sw2b))):
            src = dt[nm].rearrange("(kc p) m -> p kc m", p=128)
            for i in range(8):
                j = wi * 8 + i
                par = j % 2
                @block.sync
                def _(sy, i=i, par=par, src=src, j=j):
                    if j >= 2:
                        ctx.wait(sy, "ACT", swm[j - 2])
                    ctx.dma("cLD", sy, out=stg4[:, par * 1024:(par + 1) * 1024],
                            in_=src[:, i, :])
                ldi = ctx.cnt["cLD"]
                @block.scalar
                def _(s_, i=i, par=par, dstb=dstb, ldi=ldi):
                    ctx.wait(s_, "cLD", ldi)
                    swm.append(ctx.mark("ACT", s_.activation(
                        dstb[:, i * D:(i + 1) * D],
                        stg4[:, par * 1024:(par + 1) * 1024], AF.Copy)))
        ld0 = (ctx.cnt["cLD"], ctx.cnt["ACT"])

        sh_act = {0: 0, 1: 0}
        sh_dve = {0: 0, 1: 0}
        sh_pe = {0: 0, 1: 0}
        for tau in range(32):
            toff = tau * 256
            for fc in range(8):
                par = fc % 2
                po = par * 512
                @block.tensor
                def _(t, tau=tau, fc=fc, toff=toff, par=par, po=po):
                    if tau == 0 and fc == 0:
                        ctx.wait(t, "cLD", ld0[0])
                        ctx.wait(t, "ACT", ld0[1])
                    ctx.wait(t, "ACT", sh_act[par])
                    ctx.wait(t, "DVE", sh_dve[par])
                    for kc in range(8):
                        mv = xpa[:, kc * TT + toff:kc * TT + toff + 256]
                        t.matmul(pa[:, po:po + 256], sw1b[:, (kc * 8 + fc) * 128:(kc * 8 + fc + 1) * 128],
                                 mv, start=(kc == 0), stop=(kc == 7))
                        inst = t.matmul(pb[:, po:po + 256],
                                        sw3b[:, (kc * 8 + fc) * 128:(kc * 8 + fc + 1) * 128],
                                        mv, start=(kc == 0), stop=(kc == 7))
                    sh_pe[par] = ctx.mark("PE", inst)
                @block.scalar
                def _(s_, fc=fc, par=par, po=po):
                    ctx.wait(s_, "PE", sh_pe[par])
                    ctx.wait(s_, "DVE", sh_dve[par])
                    sh_act[par] = ctx.mark("ACT", s_.activation(
                        satile[:, po:po + 256], pa[:, po:po + 256],
                        AF.Silu, bias=sb1p[:, fc:fc + 1]))
                @block.vector
                def _(v, fc=fc, par=par, po=po):
                    ctx.wait(v, "ACT", sh_act[par])
                    ctx.wait(v, "PE", ctx.cnt["PE"])
                    sh_dve[par] = ctx.mark("DVE", v.scalar_tensor_tensor(
                        hsb[:, fc * 256:(fc + 1) * 256], pb[:, po:po + 256],
                        sb3p[:, fc:fc + 1], satile[:, po:po + 256],
                        ALU.add, ALU.mult))
            @block.tensor
            def _(t, tau=tau):
                ctx.wait(t, "DVE", ctx.cnt["DVE"])
                for dout in range(8):
                    for fc in range(8):
                        inst = t.matmul(
                            pys[:, dout * 256:(dout + 1) * 256],
                            sw2b[:, (fc * 8 + dout) * 128:(fc * 8 + dout + 1) * 128],
                            hsb[:, fc * 256:(fc + 1) * 256],
                            start=(fc == 0 and dout % 2 == 0),
                            stop=(fc == 7 and dout % 2 == 1))
                ctx.mark("PE", inst)
            @block.vector
            def _(v, tau=tau):
                ctx.wait(v, "PE", ctx.cnt["PE"])
                ctx.wait(v, "cST", ctx.cnt["cST"])
                for dout in range(8):
                    ctx.mark("DVE", v.tensor_scalar(
                        ysst[:, dout * 256:(dout + 1) * 256],
                        pys[:, dout * 256:(dout + 1) * 256],
                        sb2p[:, dout:dout + 1], None, ALU.add))
            @block.sync
            def _(sy, tau=tau, toff=toff):
                ctx.wait(sy, "DVE", ctx.cnt["DVE"])
                b = toff // L
                off = toff % L
                ctx.dma("cST", sy,
                        out=dt["ys_sc"][b].rearrange("(kc p) t -> p kc t", p=128)[:, :, off:off + 256],
                        in_=ysst.ap().rearrange("p (c t) -> p c t", c=8))

        stsh.close()
        w1r = sbl("p4_w1r", (128, 2 * 8 * 512), BF16)
        w3r = sbl("p4_w3r", (128, 2 * 8 * 512), BF16)
        ger = sbl("p4_ger", (1, 2 * 512), F32)
        geb = sbl("p4_geb", (128, 2 * 512), BF16)
        hpt = sbl("p4_hpt", (128, 2 * 4 * 512), BF16)
        pe_after = {}
        for e in range(16):
            wpar = e % 2
            war_pe = pe_after.get(e - 2, 0)
            for wi, (wt, dstb) in enumerate((("w1T", w1r), ("w3T", w3r))):
                src = dt[wt][e].rearrange("(kc p) m -> p kc m", p=128)
                for i in range(4):
                    j = wi * 4 + i
                    spar = j % 2
                    @block.sync
                    def _(sy, i=i, spar=spar, src=src, j=j):
                        ctx.wait(sy, "GP", max(0, ctx.cnt["GP"] - 1))
                        ctx.dma("cLD", sy, out=stg4.ap().rearrange(
                            "p (r k m) -> p r k m", r=2, k=2)[:, spar],
                            in_=src[:, 2 * i:2 * i + 2, :])
                    ldi = ctx.cnt["cLD"]
                    @block.gpsimd
                    def _(g, i=i, spar=spar, dstb=dstb, ldi=ldi,
                          wpar=wpar, war_pe=war_pe):
                        ctx.wait(g, "cLD", ldi)
                        ctx.wait(g, "PE", war_pe)
                        ctx.mark("GP", g.tensor_copy(
                            dstb[:, (wpar * 8 + 2 * i) * 512:(wpar * 8 + 2 * (i + 1)) * 512],
                            stg4[:, spar * 1024:(spar + 1) * 1024]))
            ld_e = (ctx.cnt["GP"], ctx.cnt["cLD"])

            if e == 0:
                ex_act = {0: 0, 1: 0}
                ex_stt = {0: 0, 1: 0}
                ex_gm = {0: 0, 1: 0}
                ex_pe = {0: 0, 1: 0}
                ex_st = {0: 0, 1: 0}
                ex_geb = [0]
                ex_bc = [0]
            def ld_ger(e2, ta2):
                if ta2 >= 16:
                    return 0
                gpar = ta2 % 2
                @block.sync
                def _(sy, e2=e2, ta2=ta2, gpar=gpar):
                    ctx.wait(sy, "PE", ex_bc[0])
                    ctx.dma("cLD", sy, out=ger[:, gpar * 512:(gpar + 1) * 512],
                            in_=dt["gatew_sc"][e2:e2 + 1, ta2 * 512:(ta2 + 1) * 512])
                return ctx.cnt["cLD"]

            def bcast_geb(ta2, ldg):
                if ta2 >= 16:
                    return
                gpar = ta2 % 2
                @block.tensor
                def _(t, ta2=ta2, gpar=gpar, ldg=ldg):
                    ctx.wait(t, "cLD", ldg)
                    ctx.wait(t, "DVE", ex_geb[0])
                    ex_bc[0] = ctx.mark("PE", t.matmul(
                        pys[:, gpar * 512:(gpar + 1) * 512], P["ones1x128f"][:, :],
                        ger[:, gpar * 512:(gpar + 1) * 512], start=True, stop=True))
                @block.vector
                def _(v, gpar=gpar):
                    ctx.wait(v, "PE", ex_bc[0])
                    ctx.wait(v, "DVE", ex_gm[0])
                    ctx.wait(v, "DVE", ex_gm[1])
                    ex_geb[0] = ctx.mark("DVE", v.tensor_copy(
                        geb[:, gpar * 512:(gpar + 1) * 512],
                        pys[:, gpar * 512:(gpar + 1) * 512]))

            ldg0 = ld_ger(e, 0)
            bcast_geb(0, ldg0)
            for ta in range(16):
                toff = ta * 512
                gpar = ta % 2
                ldg1 = ld_ger(e, ta + 1)
                hpar = ta % 2
                for fc in range(4):
                    par = fc % 2
                    po = par * 512
                    @block.tensor
                    def _(t, e=e, ta=ta, fc=fc, toff=toff, wpar=wpar,
                          ld_e=ld_e, par=par, po=po):
                        if ta == 0 and fc == 0:
                            ctx.wait(t, "GP", ld_e[0])
                        ctx.wait(t, "ACT", ex_act[par])
                        ctx.wait(t, "DVE", ex_stt[par])
                        for kc in range(8):
                            mv = xpa[:, kc * TT + toff:kc * TT + toff + 512]
                            t.matmul(pa[:, po:po + 512],
                                     w1r[:, (wpar * 8 + kc) * 512 + fc * 128:
                                         (wpar * 8 + kc) * 512 + (fc + 1) * 128],
                                     mv, start=(kc == 0), stop=(kc == 7))
                            inst = t.matmul(
                                pb[:, po:po + 512],
                                w3r[:, (wpar * 8 + kc) * 512 + fc * 128:
                                    (wpar * 8 + kc) * 512 + (fc + 1) * 128],
                                mv, start=(kc == 0), stop=(kc == 7))
                        ex_pe[par] = ctx.mark("PE", inst)
                    @block.scalar
                    def _(s_, e=e, fc=fc, par=par, po=po):
                        ctx.wait(s_, "PE", ex_pe[par])
                        ctx.wait(s_, "DVE", ex_stt[par])
                        ex_act[par] = ctx.mark("ACT", s_.activation(
                            satile[:, po:po + 512], pa[:, po:po + 512], AF.Silu,
                            bias=b1p[:, e * 4 + fc:e * 4 + fc + 1]))
                    @block.vector
                    def _(v, e=e, fc=fc, hpar=hpar, par=par, po=po):
                        ctx.wait(v, "ACT", ex_act[par])
                        ctx.wait(v, "cST", ex_st[hpar])
                        ex_stt[par] = ctx.mark("DVE", v.scalar_tensor_tensor(
                            hpt[:, hpar * 2048 + fc * 512:hpar * 2048 + (fc + 1) * 512],
                            pb[:, po:po + 512], b3p[:, e * 4 + fc:e * 4 + fc + 1],
                            satile[:, po:po + 512], ALU.add, ALU.mult))
                        ctx.wait(v, "DVE")
                        ex_gm[par] = ctx.mark("DVE", v.tensor_tensor(
                            hpt[:, hpar * 2048 + fc * 512:hpar * 2048 + (fc + 1) * 512],
                            hpt[:, hpar * 2048 + fc * 512:hpar * 2048 + (fc + 1) * 512],
                            geb[:, gpar * 512:(gpar + 1) * 512], ALU.mult))
                bcast_geb(ta + 1, ldg1)
                @block.sync
                def _(sy, e=e, ta=ta, hpar=hpar):
                    ctx.wait(sy, "DVE", max(ex_gm[0], ex_gm[1]))
                    for tb in range(2):
                        inst_m = ctx.dma("cST", sy,
                                out=dt["hp_sc"][2 * ta + tb, e].rearrange(
                                    "(fc p) t -> p fc t", p=128),
                                in_=hpt.ap().rearrange(
                                    "p (r fc tb t) -> p r fc tb t",
                                    r=2, fc=4, tb=2)[:, hpar, :, tb, :])
                    ex_st[hpar] = inst_m
                pe_after[e] = ctx.cnt["PE"]


def phase5(nc, ctx, block, dt, P, sb, ps):
    import contextlib
    with contextlib.ExitStack() as st:
        sbl = lambda n, s_, d: st.enter_context(nc.sbuf_tensor(n, list(s_), d))
        psl = lambda n, s_, d=F32: st.enter_context(nc.psum_tensor(n, list(s_), d))

        w2a = sbl("p5_w2", (128, 16 * 4 * D), BF16)
        b2b = sbl("p5_b2", (16, D), BF16)
        st5 = contextlib.ExitStack()
        stg5 = st5.enter_context(nc.sbuf_tensor("p5_cstg", [128, 2048], F32))

        py = psl("p5_y", (128, 2048))
        pst = psl("p5_st", (64, 256))
        pbc = psl("p5_bc", (128, 256))

        w2m = []
        for j in range(32):
            e, h = j // 2, j % 2
            src = dt["w2T"][e].rearrange("(fc p) m -> p fc m", p=128)
            @block.sync
            def _(sy, j=j, src=src, h=h):
                if j >= 1:
                    ctx.wait(sy, "ACT", w2m[j - 1])
                ctx.dma("cLD", sy, out=stg5.ap().rearrange(
                    "p (k m) -> p k m", k=2), in_=src[:, 2 * h:2 * h + 2, :])
            ldi = ctx.cnt["cLD"]
            @block.scalar
            def _(s_, j=j, e=e, h=h, ldi=ldi):
                ctx.wait(s_, "cLD", ldi)
                w2m.append(ctx.mark("ACT", s_.activation(
                    w2a[:, (e * 4 + 2 * h) * D:(e * 4 + 2 * h + 2) * D],
                    stg5[:, :], AF.Copy)))
        @block.sync
        def _(sy):
            ctx.wait(sy, "ACT", w2m[-1])
            ctx.dma("cLD", sy, out=stg5[0:16, 0:D], in_=dt["b2e"])
        ldb2 = ctx.cnt["cLD"]
        @block.scalar
        def _(s_, ldb2=ldb2):
            ctx.wait(s_, "cLD", ldb2)
            ctx.mark("ACT", s_.activation(b2b[:, :], stg5[0:16, 0:D], AF.Copy))
        gp_b2 = ctx.cnt["ACT"]
        st5.close()
        hpr = sbl("p5_hp", (128, 3 * 4 * 4 * 256), BF16)
        gwb = sbl("p5_gwb", (16, 2 * 256), BF16)
        gwf = sbl("p5_gwf", (16, 2 * 256), F32)
        xr = sbl("p5_xr", (128, 2 * 8 * 256), BF16)
        ysr = sbl("p5_ysr", (128, 2 * 8 * 256), BF16)
        r2b = sbl("p5_r2", (128, 2 * 8 * 256), BF16)
        r2q = sbl("p5_r2q", (128, 2 * 256), BF16)
        rm = sbl("p5_rm", (1, 256), F32)
        rv = sbl("p5_rv", (1, 256), F32)
        rw = sbl("p5_rw", (1, 256), F32)
        m_b = sbl("p5_mb", (128, 256), BF16)
        rs_b = sbl("p5_rsb", (128, 256), BF16)
        t1t = sbl("p5_t1", (128, 256), BF16)
        outs = sbl("p5_out", (128, 2 * 8 * 256), F32)

        evac_marks = {0: 0, 1: 0}
        r2_marks = {0: 0, 1: 0}
        hp_marks = {}
        ld_marks = {}
        st_marks = {0: 0, 1: 0}
        ln_state = {}

        def emit_loads(tau):
            if tau >= 32:
                return
            toff = tau * 256
            b = toff // L
            off = toff % L
            rpar = tau % 2
            @block.sync
            def _(sy, tau=tau, b=b, off=off, rpar=rpar):
                ctx.wait(sy, "DVE", r2_marks[rpar])
                ctx.dma("cLD", sy, out=xr.ap().rearrange(
                    "p (r c t) -> p r c t", r=2, c=8)[:, rpar],
                    in_=dt["xPb_sc"][b].rearrange("(c p) t -> p c t", p=128)[:, :, off:off + 256])
                ctx.dma("cLD", sy, out=ysr.ap().rearrange(
                    "p (r c t) -> p r c t", r=2, c=8)[:, rpar],
                    in_=dt["ys_sc"][b].rearrange("(c p) t -> p c t", p=128)[:, :, off:off + 256])
                ctx.dma("cLD", sy, out=gwf[:, rpar * 256:(rpar + 1) * 256],
                        in_=dt["gatew_sc"][:, toff:toff + 256])
            ldx = ctx.cnt["cLD"]
            @block.vector
            def _(v, ldx=ldx, rpar=rpar):
                ctx.wait(v, "cLD", ldx)
                ctx.wait(v, "PE", ctx.cnt["PE"])
                ctx.mark("DVE", v.tensor_copy(gwb[:, rpar * 256:(rpar + 1) * 256],
                                              gwf[:, rpar * 256:(rpar + 1) * 256]))
            ld_marks[tau] = (ldx, ctx.cnt["DVE"])

        def emit_hp_load(tau, gq):
            if tau >= 32:
                return
            rq = (tau * 4 + gq) % 3
            @block.sync
            def _(sy, tau=tau, gq=gq, rq=rq):
                if tau == 0 and gq == 0:
                    ctx.wait(sy, "ACT", gp_b2)
                prev = hp_marks.get(("pe", tau * 4 + gq - 3), 0)
                ctx.wait(sy, "PE", prev)
                ctx.dma("cLD", sy, out=hpr.ap().rearrange(
                    "p (r e fc t) -> p r e fc t", r=3, e=4, fc=4)[:, rq],
                    in_=dt["hp_sc"][tau, 4 * gq:4 * gq + 4].rearrange(
                        "e (fc p) t -> p e fc t", p=128))
            hp_marks[(tau, gq)] = ctx.cnt["cLD"]

        def emit_mms(tau):
            rpar = tau % 2
            for gq in range(4):
                rq = (tau * 4 + gq) % 3
                @block.tensor
                def _(t, tau=tau, gq=gq, rq=rq, rpar=rpar):
                    ctx.wait(t, "cLD", hp_marks[(tau, gq)])
                    if tau == 0 and gq == 0:
                        ctx.wait(t, "ACT", gp_b2)
                    if gq == 0:
                        ctx.wait(t, "DVE", max(evac_marks[0], evac_marks[1],
                                               ld_marks[tau][1]))
                    elif tau == 0:
                        pass
                    if tau == 0:
                        ctx.wait(t, "ACT", w2m[(gq + 1) * 8 - 1])
                    for dout in range(8):
                        for el in range(4):
                            e = gq * 4 + el
                            for fc in range(4):
                                first = (gq == 0 and el == 0 and fc == 0
                                         and dout % 2 == 0)
                                inst = t.matmul(
                                    py[:, dout * 256:(dout + 1) * 256],
                                    w2a[:, ((e * 4 + fc) * 8 + dout) * 128:
                                        ((e * 4 + fc) * 8 + dout + 1) * 128],
                                    hpr[:, (rq * 16 + el * 4 + fc) * 256:
                                        (rq * 16 + el * 4 + fc + 1) * 256],
                                    start=first, stop=False)
                        if gq == 3:
                            inst = t.matmul(
                                py[:, dout * 256:(dout + 1) * 256],
                                b2b[:, dout * 128:(dout + 1) * 128],
                                gwb[:, rpar * 256:(rpar + 1) * 256],
                                start=False, stop=(dout % 2 == 1))
                    m = ctx.mark("PE", inst)
                    hp_marks[("pe", tau * 4 + gq)] = m
                g3 = tau * 4 + gq + 3
                emit_hp_load(g3 // 4, g3 % 4)

        def emit_evac(tau):
            rpar = tau % 2
            ro = rpar * 2048
            @block.vector
            def _(v, tau=tau, rpar=rpar, ro=ro):
                ctx.wait(v, "PE", hp_marks[("pe", tau * 4 + 3)])
                ctx.wait(v, "cST", st_marks[rpar])
                ctx.wait(v, "DVE")
                for dout in range(8):
                    m0 = ctx.mark("DVE", v.tensor_tensor(
                        outs[:, ro + dout * 256:ro + (dout + 1) * 256],
                        py[:, dout * 256:(dout + 1) * 256],
                        ysr[:, ro + dout * 256:ro + (dout + 1) * 256], ALU.add))
                evac_marks[rpar] = m0
                ctx.wait(v, "DVE")
                for dout in range(8):
                    m2 = ctx.mark("DVE", v.tensor_tensor(
                        r2b[:, ro + dout * 256:ro + (dout + 1) * 256],
                        outs[:, ro + dout * 256:ro + (dout + 1) * 256],
                        xr[:, ro + dout * 256:ro + (dout + 1) * 256], ALU.add))
                r2_marks[rpar] = m2

        def emit_ln(tau):
            rpar = tau % 2
            ro = rpar * 2048
            b = (tau * 256) // L
            off = (tau * 256) % L
            dve_r2 = evac_marks[rpar]
            for kc in range(8):
                qp = kc % 2
                @block.vector
                def _(v, kc=kc, qp=qp, ro=ro, dve_r2=dve_r2):
                    ctx.wait(v, "DVE")
                    ctx.wait(v, "PE", ctx.cnt["PE"])
                    ctx.mark("DVE", v.tensor_tensor(
                        r2q[:, qp * 256:(qp + 1) * 256],
                        r2b[:, ro + kc * 256:(kc + 1) * 256 + ro],
                        r2b[:, ro + kc * 256:(kc + 1) * 256 + ro], ALU.mult))
                @block.tensor
                def _(t, kc=kc, qp=qp, ro=ro):
                    ctx.wait(t, "DVE", ctx.cnt["DVE"])
                    t.matmul(pst[:, :], P["stat_a"][:, :],
                             r2b[:, ro + kc * 256:(kc + 1) * 256 + ro],
                             start=(kc == 0), stop=False)
                    ctx.mark("PE", t.matmul(pst[:, :], P["stat_b"][:, :],
                                            r2q[:, qp * 256:(qp + 1) * 256],
                                            start=False, stop=(kc == 7)))
            @block.vector
            def _(v):
                ctx.wait(v, "PE", ctx.cnt["PE"])
                ctx.wait(v, "ACT", ctx.cnt["ACT"])
                ctx.mark("DVE", v.tensor_scalar_mul(rm[:, :], pst[0:1, :],
                                                    float(1.0 / D)))
                ctx.mark("DVE", v.tensor_scalar_mul(rv[:, :], pst[32:33, :],
                                                    float(1.0 / D)))
                ctx.wait(v, "DVE")
                ctx.mark("DVE", v.tensor_tensor(rw[:, :], rm[:, :], rm[:, :], ALU.mult))
                ctx.wait(v, "DVE")
                ctx.mark("DVE", v.tensor_tensor(rv[:, :], rv[:, :], rw[:, :], ALU.subtract))
                ctx.wait(v, "DVE")
                ctx.mark("DVE", v.tensor_scalar_add(rv[:, :], rv[:, :], float(1e-5)))
            @block.scalar
            def _(s_):
                ctx.wait(s_, "DVE", ctx.cnt["DVE"])
                ctx.mark("ACT", s_.activation(rv[:, :], rv[:, :], AF.Sqrt))
            @block.vector
            def _(v):
                ctx.wait(v, "ACT", ctx.cnt["ACT"])
                ctx.mark("DVE", v.reciprocal(rv[:, :], rv[:, :]))
            @block.tensor
            def _(t):
                ctx.wait(t, "DVE", ctx.cnt["DVE"])
                ctx.wait(t, "ACT", ctx.cnt["ACT"])
                ctx.mark("PE", t.matmul(pbc[:, :], P["ones1x128f"][:, :], rm[:, :],
                                        start=True, stop=True))
            @block.scalar
            def _(s_):
                ctx.wait(s_, "PE", ctx.cnt["PE"])
                ctx.mark("ACT", s_.activation(m_b[:, :], pbc[:, :], AF.Copy))
            @block.tensor
            def _(t):
                ctx.wait(t, "ACT", ctx.cnt["ACT"])
                ctx.mark("PE", t.matmul(pbc[:, :], P["ones1x128f"][:, :], rv[:, :],
                                        start=True, stop=True))
            @block.scalar
            def _(s_):
                ctx.wait(s_, "PE", ctx.cnt["PE"])
                ctx.mark("ACT", s_.activation(rs_b[:, :], pbc[:, :], AF.Copy))
            for dout in range(8):
                @block.vector
                def _(v, dout=dout, ro=ro):
                    ctx.wait(v, "ACT", ctx.cnt["ACT"])
                    ctx.wait(v, "DVE")
                    ctx.mark("DVE", v.tensor_tensor(
                        t1t[:, :], r2b[:, ro + dout * 256:ro + (dout + 1) * 256],
                        m_b[:, :], ALU.subtract))
                    ctx.wait(v, "DVE")
                    ctx.mark("DVE", v.tensor_tensor(
                        outs[:, ro + dout * 256:ro + (dout + 1) * 256],
                        t1t[:, :], rs_b[:, :], ALU.mult))
            @block.sync
            def _(sy, b=b, off=off, ro=ro, rpar=rpar):
                ctx.wait(sy, "DVE", ctx.cnt["DVE"])
                st_marks[rpar] = ctx.dma("cST", sy,
                        out=dt["out_fm"][b].rearrange("(c p) t -> p c t", p=128)[:, :, off:off + 256],
                        in_=outs.ap().rearrange("p (r c t) -> p r c t", r=2, c=8)[:, rpar])

        emit_loads(0)
        for gq in range(3):
            emit_hp_load(0, gq)
        for tau in range(32):
            emit_mms(tau)
            emit_loads(tau + 1)
            emit_evac(tau)
            if tau >= 1:
                emit_ln(tau - 1)
        emit_ln(31)



def ppv(v, groups=None):
    v = np.asarray(v)
    if v.ndim == 1:
        return np.ascontiguousarray(v.reshape(-1, 128).T)
    return np.ascontiguousarray(v.reshape(v.shape[0], -1, 128).transpose(2, 0, 1))


def make_in_maps(inputs):
    c = make_consts()
    x = np.ascontiguousarray(
        np.asarray(inputs["L0_attn_x"]).transpose(0, 2, 1))
    base = {
        "W_in": np.ascontiguousarray(inputs["L0_attn_W_in"]),
        "bin_pp": ppv(inputs["L0_attn_b_in"]),
        "shortw_pp": np.ascontiguousarray(
            np.asarray(inputs["L0_attn_short_w"]).reshape(24, 128, 3).transpose(1, 0, 2)),
        "shortb_pp": ppv(inputs["L0_attn_short_b"]),
        "fW1": np.ascontiguousarray(inputs["L0_attn_W1"]),
        "fb1": np.ascontiguousarray(inputs["L0_attn_b1"]),
        "ff1": np.ascontiguousarray(inputs["L0_attn_f1"]),
        "fW2": np.ascontiguousarray(inputs["L0_attn_W2"]),
        "fb2": np.ascontiguousarray(inputs["L0_attn_b2"]),
        "ff2": np.ascontiguousarray(inputs["L0_attn_f2"]),
        "fW3": np.ascontiguousarray(inputs["L0_attn_W3"]),
        "fb3_pp": ppv(inputs["L0_attn_b3"]),
        "fbias_pp": ppv(inputs["L0_attn_filter_bias"]),
        "W_out": np.ascontiguousarray(inputs["L0_attn_W_out"]),
        "bout_pp": ppv(inputs["L0_attn_b_out"]),
        "gate_wT": np.ascontiguousarray(np.asarray(inputs["L0_ffn_gate_w"]).T),
        "expert_bias": np.ascontiguousarray(inputs["L0_ffn_expert_bias"]),
        "w1T": np.ascontiguousarray(np.asarray(inputs["L0_ffn_w1"]).transpose(0, 2, 1)),
        "b1e_pp": ppv(inputs["L0_ffn_b1"]),
        "w3T": np.ascontiguousarray(np.asarray(inputs["L0_ffn_w3"]).transpose(0, 2, 1)),
        "b3e_pp": ppv(inputs["L0_ffn_b3"]),
        "w2T": np.ascontiguousarray(np.asarray(inputs["L0_ffn_w2"]).transpose(0, 2, 1)),
        "b2e": np.ascontiguousarray(inputs["L0_ffn_b2"]),
        "sw1T": np.ascontiguousarray(np.asarray(inputs["L0_ffn_sw1"]).T),
        "sb1_pp": ppv(inputs["L0_ffn_sb1"]),
        "sw3T": np.ascontiguousarray(np.asarray(inputs["L0_ffn_sw3"]).T),
        "sb3_pp": ppv(inputs["L0_ffn_sb3"]),
        "sw2T": np.ascontiguousarray(np.asarray(inputs["L0_ffn_sw2"]).T),
        "sb2_pp": ppv(inputs["L0_ffn_sb2"]),
        "dneg_pp": ppv(c["deltas_neg"][:, 0]),
    }
    c.pop("deltas_neg", None)
    base.update(c)
    maps = []
    for core in range(8):
        m = dict(base)
        m["xfm"] = np.ascontiguousarray(x[core * BPC:(core + 1) * BPC])
        maps.append(m)
    return maps


def kernel(**inputs):
    nc = build_program()
    maps = make_in_maps(inputs)
    res = run_bass_kernel_spmd(nc, maps, core_ids=list(range(8)))
    outs = [np.ascontiguousarray(r["out_fm"].transpose(0, 2, 1))
            for r in res.results]
    return np.concatenate(outs, 0).astype(np.float32)
